# version 02
# baseline (speedup 1.0000x reference)
import numpy as np
import ml_dtypes
from contextlib import ExitStack

import concourse.bass as bass
import concourse.mybir as mybir
from concourse.bass_utils import run_bass_kernel_spmd
from concourse.bass import IndirectOffsetOnAxis

F32 = mybir.dt.float32
BF16 = mybir.dt.bfloat16
U32 = mybir.dt.uint32
I32 = mybir.dt.int32
U8 = mybir.dt.uint8
AF = mybir.ActivationFunctionType
ALU = mybir.AluOpType
AX = mybir.AxisListType

T = 2048
NT = 16
E = 1024
HA, DA = 16, 64
H, KD, NK, TK = 8, 128, 256, 8
NEXP = NK * NK
FQ = 2048
NBUF = 8
ENGS = ("sync", "scalar", "vector", "tensor", "gpsimd")


class Plan:
    def __init__(self, nc, stack):
        self.nc = nc
        self.fns = {e: [] for e in ENGS}
        self.sems = {}
        self.cnt = {}
        for s in ("ld", "sd2", "gd", "vv", "ss", "tt", "cc", "aa", "cw"):
            self.sems[s] = stack.enter_context(nc.semaphore(s))
            self.cnt[s] = 0

    def emit(self, eng, fn):
        self.fns[eng].append(fn)

    def wait(self, eng, sem, val):
        if val is None or val <= 0:
            return
        h = self.sems[sem]
        self.emit(eng, lambda e, h=h, v=val: e.wait_ge(h, v))

    def step(self, eng, sem, fn, n=1):
        self.cnt[sem] += n
        h = self.sems[sem]
        self.emit(eng, lambda e, h=h, n=n, fn=fn: fn(e).then_inc(h, n))
        return self.cnt[sem]

    def run(self, eng, fn):
        self.emit(eng, fn)

    def drain(self, eng):
        self.emit(eng, lambda e: e.drain())


def build(stage=3, dbg=False, att_stop=None, skip_ffn1=False, att_tiles=NT):
    nc = bass.Bass("TRN2", target_bir_lowering=False)
    stack = ExitStack()

    dtn = nc.dram_tensor
    x_in = dtn("x", [T, E], F32, kind="ExternalInput").ap()
    y_out = dtn("y", [T, E], F32, kind="ExternalOutput").ap()
    wq_a = dtn("wq_a", [E, E], F32, kind="ExternalInput").ap()
    wdn_d = dtn("wdn", [E, 512], F32, kind="ExternalInput").ap()
    wku_d = dtn("wku", [512, E], F32, kind="ExternalInput").ap()
    wvu_d = dtn("wvu", [512, E], F32, kind="ExternalInput").ap()
    wo_d = dtn("wo_a", [E, E], F32, kind="ExternalInput").ap()
    wqf_d = dtn("wqf", [E, FQ], F32, kind="ExternalInput").ap()
    bng_d = dtn("bng", [128, 16], F32, kind="ExternalInput").ap()
    bnb_d = dtn("bnb", [128, 16], F32, kind="ExternalInput").ap()
    keysT_d = dtn("keysT", [16, KD, NK], F32, kind="ExternalInput").ap()
    fused_d = dtn("fused", [NEXP, 2048], F32, kind="ExternalInput").ap()
    id_d = dtn("id128", [128, 128], BF16, kind="ExternalInput").ap()
    iota_d = dtn("iota256", [128, 256], U32, kind="ExternalInput").ap()
    ctab_d = dtn("ctab", [128, T], F32, kind="ExternalInput").ap()
    stab_d = dtn("stab", [128, T], F32, kind="ExternalInput").ap()
    swap_d = dtn("swap128", [128, 128], BF16, kind="ExternalInput").ap()

    fusedb = dtn("fusedb", [NEXP, 2048], BF16, kind="Internal").ap()
    ccin = dtn("ccin", [128, 32], F32, kind="Internal").ap()
    ccout = dtn("ccout", [128, 32], F32, kind="Internal").ap()

    def sbuf(name, shape, dt):
        return stack.enter_context(nc.sbuf_tensor(name, shape, dt))

    sb_x = sbuf("sb_x", [128, NT, E], F32)
    sb_b0 = sbuf("sb_b0", [128, 16384], BF16)
    sb_b1 = sbuf("sb_b1", [128, 16640], BF16)
    sb_b2 = sbuf("sb_b2", [128, 16384], BF16)
    sb_aux = sbuf("sb_aux", [128, 36864], U8)
    sb_id = sbuf("sb_id", [128, 128], BF16)
    sb_swap = sbuf("sb_swap", [128, 128], BF16)
    sb_iota = sbuf("sb_iota", [128, 256], U32)
    sb_small = sbuf("sb_small", [128, 96], F32)
    sb_conv = sbuf("sb_conv", [128, 2, 2048], BF16)

    psum = [stack.enter_context(nc.psum_tensor(f"pb{i}", [128, 512], F32))
            for i in range(8)]

    smallu = sb_small[:].bitcast(U32)
    C_M32 = sb_small[:, 0:1]
    C_MHI8 = smallu[:, 1:2]
    C_MHI16 = smallu[:, 2:3]
    C_MLO16 = smallu[:, 3:4]
    LN_SUM = sb_small[:, 4:20]
    LN_SQ = sb_small[:, 20:36]
    LN_SC = sb_small[:, 36:52]
    LN_B2 = sb_small[:, 52:68]
    LN_T4 = sb_small[:, 68:72]
    LN_T4B = sb_small[:, 72:76]
    ST_SUM = sb_small[:, 4:20]
    ST_SQ = sb_small[:, 20:36]
    ST_ALL = sb_small[:, 4:36]

    def aview(off, shape, dt):
        n = 1
        for s in shape[1:]:
            n *= s
        nbytes = n * mybir.dt.size(dt)
        assert off + nbytes <= 36864, (off, shape, dt)
        ap = sb_aux[:, off:off + nbytes].bitcast(dt)
        if len(shape) == 2:
            return ap
        kw = {f"d{i}": shape[i] for i in range(1, len(shape) - 1)}
        spec = " ".join(f"d{i}" for i in range(1, len(shape)))
        return ap.rearrange(f"p ({spec}) -> p {spec}", **kw)

    keysT_b = aview(0, [128, 16, NK], BF16)
    zwork = aview(0, [128, 8, 8, 8], F32)
    zidx = aview(2048, [128, 8, 8, 8], U32)
    sxw = aview(4096, [128, 8, 8], F32)
    syw = aview(4352, [128, 8, 8], F32)
    ixw = aview(4608, [128, 8, 8], U32)
    iyw = aview(4864, [128, 8, 8], U32)
    f8w = aview(5120, [128, 64], F32)
    s8w = aview(5376, [128, 8, 8], F32)
    e8w = aview(5632, [128, 8, 8], F32)
    r8w = aview(5888, [128, 8], F32)
    hdot = aview(6144, [128, 64], F32)
    gelu_c = aview(6400, [128, 64], F32)
    c8_all = aview(8192, [128, NT, 16, 8], F32)
    idx_all = aview(16384, [128, NT, 64], U32)
    smax_all = aview(20480, [128, NT, 64], F32)
    qn2 = aview(24576, [128, 2, FQ], BF16)
    simpay = aview(32768, [128, 2, NK], F32)
    sq_scr = aview(32768, [128, 1024], BF16)

    lat_fm = aview(0, [128, 4, T], BF16)
    ctab = aview(16384, [128, T], F32)
    stab = aview(24576, [128, T], F32)
    ln2t = aview(32768, [128, 1024], BF16)
    qraw = aview(34816, [128, 2, 128], BF16)
    lnfm = aview(24576, [128, 2, 8, 128], BF16)
    wvu_b = aview(16384, [128, 4, E], BF16)
    wo_b = aview(0, [128, 8, E], BF16)
    p_sb = aview(16384, [128, 2048], BF16)
    pt_sb = aview(20480, [128, 2048], BF16)
    atok = aview(24576, [128, 1024], BF16)
    afm = aview(26624, [128, 8, 128], BF16)
    dwork = aview(28672, [128, 16], F32)
    rope_tmp = aview(35328, [128, 128], F32)

    lnfm = aview(16384, [128, 2, 8, 128], BF16)
    lnfm = None

    lnfm_b0 = sb_b0[:, 0:2048].rearrange("p (r c q) -> p r c q", r=2, c=8)

    def lnfm_col(t, k):
        return lnfm_b0[:, t % 2, k, :]

    def lnfm_dst(t):
        return lnfm_b0[:, t % 2, :, :]

    xn_fm = sb_b0[:].rearrange("p (c t) -> p c t", c=8)
    k_rot = sb_b0[:].rearrange("p (c t) -> p c t", c=8)
    xn_tok = sb_b1[:, 0:16384].rearrange("p (a d) -> p a d", a=NT)
    v65 = sb_b1[:].rearrange("p (a d) -> p a d", a=NT)
    wq_asb = sb_b1[:, 0:8192].rearrange("p (k m) -> p k m", k=8)
    wdn_b = sb_b1[:, 8192:12288].rearrange("p (k m) -> p k m", k=8)
    wku_b = sb_b1[:, 12288:16384].rearrange("p (k m) -> p k m", k=4)
    wq_fsb = sb_b2[:].rearrange("p (k m) -> p k m", k=8)
    gbuf = sb_b2[:].rearrange("p (b d) -> p b d", b=NBUF)
    q_rot = sb_b2[:].rearrange("p (c t) -> p c t", c=8)

    P = Plan(nc, stack)
    state = {}

    v_x = P.step("sync", "ld", lambda e: e.dma_start(
        out=sb_x[:], in_=x_in.rearrange("(a p) d -> p a d", p=128)), 16)
    v_id = P.step("sync", "ld", lambda e: e.dma_start(out=sb_id[:], in_=id_d), 16)
    v_swap = P.step("sync", "ld", lambda e: e.dma_start(out=sb_swap[:], in_=swap_d), 16)
    v_iota = P.step("sync", "ld", lambda e: e.dma_start(out=sb_iota[:], in_=iota_d), 16)

    BN_G = aview(36352, [128, 16], F32)
    BN_B = aview(36416, [128, 16], F32)
    BN_SC = aview(36480, [128, 16], F32)
    BN_BI = aview(36544, [128, 16], F32)
    BN_T = aview(36608, [128, 16], F32)
    BN_T2 = aview(36672, [128, 16], F32)
    P.step("sync", "ld", lambda e: e.dma_start(out=BN_G[:], in_=bng_d), 16)
    v_bn = P.step("sync", "ld", lambda e: e.dma_start(out=BN_B[:], in_=bnb_d), 16)

    P.run("vector", lambda e: e.memset(C_M32, -32.0))
    P.run("vector", lambda e: e.memset(C_MHI8, 0xFFFFFF00))
    P.run("vector", lambda e: e.memset(C_MHI16, 0xFFFF0000))
    v_msk = P.step("vector", "vv", lambda e: e.memset(C_MLO16, 0x0000FFFF))

    CONV = 512
    RP = NEXP // CONV
    conv_w = [0] * CONV

    def emit_conv():
        base_w = P.cnt["cw"]
        for i in range(CONV):
            conv_w[i] = base_w + 16 * (i + 1)
        conv_r = [0] * CONV
        for i in range(CONV):
            if i >= 2:
                P.wait("gpsimd", "cw", conv_w[i - 2])
            conv_r[i] = P.step("gpsimd", "gd", lambda e, i=i: e.dma_start(
                out=sb_conv[:, i % 2, :],
                in_=fused_d[i * RP:(i + 1) * RP, :]), 16)
        for i in range(CONV):
            P.wait("sync", "gd", conv_r[i])
            vw = P.step("sync", "cw", lambda e, i=i: e.dma_start(
                out=fusedb[i * RP:(i + 1) * RP, :],
                in_=sb_conv[:, i % 2, :]), 16)
            assert vw == conv_w[i], (vw, conv_w[i])

    def emit_ln_stats(x_ready_vv):
        for t in range(NT):
            P.wait("vector", "ld", v_x)
            P.wait("vector", "vv", x_ready_vv)
            P.step("vector", "vv", lambda e, t=t: e.reduce_sum(
                out=LN_SUM[:, t:t + 1], in_=sb_x[:, t, :], axis=AX.X))
            P.wait("scalar", "ld", v_x)
            P.wait("scalar", "vv", x_ready_vv)
            P.step("scalar", "ss", lambda e, t=t: e.activation(
                out=sq_scr[:], in_=sb_x[:, t, :], func=AF.Square,
                accum_out=LN_SQ[:, t:t + 1]))
        v_sq = P.cnt["ss"]
        P.wait("vector", "ss", v_sq)
        P.drain("vector")
        P.step("vector", "vv", lambda e: e.tensor_scalar(
            out=LN_SUM[:], in0=LN_SUM[:], scalar1=1.0 / E, scalar2=None,
            op0=ALU.mult))
        P.step("vector", "vv", lambda e: e.tensor_scalar(
            out=LN_SQ[:], in0=LN_SQ[:], scalar1=1.0 / E, scalar2=None,
            op0=ALU.mult))
        P.drain("vector")
        P.step("vector", "vv", lambda e: e.tensor_tensor(
            out=LN_SC[:], in0=LN_SUM[:], in1=LN_SUM[:], op=ALU.mult))
        P.drain("vector")
        P.step("vector", "vv", lambda e: e.tensor_tensor(
            out=LN_SC[:], in0=LN_SQ[:], in1=LN_SC[:], op=ALU.subtract))
        P.drain("vector")
        v_var = P.step("vector", "vv", lambda e: e.tensor_scalar(
            out=LN_SC[:], in0=LN_SC[:], scalar1=1e-5, scalar2=None, op0=ALU.add))
        P.wait("scalar", "vv", v_var)
        v_sqr = P.step("scalar", "ss", lambda e: e.sqrt(LN_B2[:], LN_SC[:]))
        P.wait("vector", "ss", v_sqr)
        P.step("vector", "vv", lambda e: e.reciprocal(LN_SC[:], LN_B2[:]))
        P.drain("vector")
        v_ret = P.step("vector", "vv", lambda e: e.tensor_tensor(
            out=LN_B2[:], in0=LN_SUM[:], in1=LN_SC[:], op=ALU.mult))
        P.drain("vector")
        return v_ret

    def emit_ln_apply(t, out_ap, v_stats, extra_vv=0, strong=False):
        P.wait("vector", "vv", max(v_stats, extra_vv))
        v = P.step("vector", "vv", lambda e, t=t, out_ap=out_ap: e.tensor_scalar(
            out=out_ap, in0=sb_x[:, t, :],
            scalar1=LN_SC[:, t:t + 1], scalar2=LN_B2[:, t:t + 1],
            op0=ALU.mult, op1=ALU.subtract))
        if strong:
            P.drain("vector")
            P.step("aa" if False else "vector", "vv", lambda e: e.engine_nop())
            v = P.step("vector", "aa", lambda e: e.engine_nop())
        return v

    def emit_transpose_tile(src_ap, dst_ap, bk, waits_vv=()):
        pt8 = psum[bk][:].bitcast(BF16).rearrange("p (c q) -> p c q", c=8)
        for v in waits_vv:
            P.wait("tensor", "vv", v)
        P.wait("tensor", "ld", v_id)
        for c in range(8):
            P.step("tensor", "tt", lambda e, c=c, src_ap=src_ap, pt8=pt8:
                   e.transpose(pt8[:, c, :], src_ap[:, c * 128:(c + 1) * 128],
                               sb_id[:]))
        v_tr = P.cnt["tt"]
        P.wait("vector", "tt", v_tr)
        return P.step("vector", "vv", lambda e, pt8=pt8, dst_ap=dst_ap:
                      e.tensor_copy(out=dst_ap, in_=pt8))

    def emit_ffn(app, x_ready_vv):
        v_lnst = emit_ln_stats(x_ready_vv)
        tr_ev = [0, 0]
        for t in range(NT):
            v_a = emit_ln_apply(t, xn_tok[:, t, :], v_lnst)
            tr_ev[t % 2] = emit_transpose_tile(
                xn_tok[:, t, :], xn_fm[:, :, t * 128:(t + 1) * 128],
                4 + (t % 2), waits_vv=(v_a, tr_ev[t % 2]))
        v_xn = P.cnt["vv"]

        P.wait("tensor", "gd", state["v_wqf"])
        bank_free = [0, 0]
        v_comb = 0
        for m in range(16):
            for tb in range(4):
                bk = tb % 2
                P.wait("tensor", "vv", max(v_xn, bank_free[bk]))
                for k in range(8):
                    P.step("tensor", "tt", lambda e, m=m, tb=tb, k=k, bk=bk:
                           e.matmul(psum[bk][:],
                                    wq_fsb[:, k, m * 128:(m + 1) * 128],
                                    xn_fm[:, k, tb * 512:(tb + 1) * 512],
                                    start=(k == 0), stop=(k == 7)))
                v_mm = P.cnt["tt"]
                P.wait("scalar", "tt", v_mm)
                P.wait("scalar", "vv", v_comb)
                v_s = P.step("scalar", "ss", lambda e, tb=tb, bk=bk: e.activation(
                    out=sq_scr[:, 0:512], in_=psum[bk][:], func=AF.Square,
                    accum_out=LN_T4B[:, tb:tb + 1]))
                P.wait("vector", "tt", v_mm)
                P.wait("vector", "ss", v_s)
                bank_free[bk] = P.step("vector", "vv", lambda e, tb=tb, bk=bk:
                                       e.reduce_sum(out=LN_T4[:, tb:tb + 1],
                                                    in_=psum[bk][:], axis=AX.X))
            P.drain("vector")
            P.step("vector", "vv", lambda e, m=m: e.reduce_sum(
                out=ST_SUM[:, m:m + 1], in_=LN_T4[:], axis=AX.X))
            v_comb = P.step("vector", "vv", lambda e, m=m: e.reduce_sum(
                out=ST_SQ[:, m:m + 1], in_=LN_T4B[:], axis=AX.X))
            bank_free[1] = v_comb
        v_stats = P.cnt["vv"]

        P.wait("scalar", "vv", v_stats)
        v_ccw = P.step("scalar", "sd2", lambda e: e.dma_start(out=ccin, in_=ST_ALL), 16)
        if cc_pre[0]:
            v_cc = cc_pre[0]
            assert v_ccw == cc_pre[1], (v_ccw, cc_pre)
            cc_pre[0] = 0
        else:
            P.wait("gpsimd", "sd2", v_ccw)
            v_cc = P.step("gpsimd", "cc", lambda e: e.collective_compute(
                "AllReduce", ALU.add, replica_groups=[list(range(8))],
                ins=[ccin], outs=[ccout]))
        P.wait("scalar", "cc", v_cc)
        v_ccr = P.step("scalar", "sd2", lambda e: e.dma_start(
            out=ST_ALL, in_=ccout), 16)

        P.wait("vector", "ld", v_bn)
        P.wait("vector", "sd2", v_ccr)
        P.drain("vector")
        P.step("vector", "vv", lambda e: e.tensor_scalar(
            out=ST_SUM[:], in0=ST_SUM[:], scalar1=1.0 / (8 * T), scalar2=None,
            op0=ALU.mult))
        P.step("vector", "vv", lambda e: e.tensor_scalar(
            out=ST_SQ[:], in0=ST_SQ[:], scalar1=1.0 / (8 * T), scalar2=None,
            op0=ALU.mult))
        P.drain("vector")
        P.step("vector", "vv", lambda e: e.tensor_tensor(
            out=BN_T[:], in0=ST_SUM[:], in1=ST_SUM[:], op=ALU.mult))
        P.drain("vector")
        P.step("vector", "vv", lambda e: e.tensor_tensor(
            out=BN_T[:], in0=ST_SQ[:], in1=BN_T[:], op=ALU.subtract))
        P.drain("vector")
        v_v2 = P.step("vector", "vv", lambda e: e.tensor_scalar(
            out=BN_T[:], in0=BN_T[:], scalar1=1e-5, scalar2=None, op0=ALU.add))
        P.wait("scalar", "vv", v_v2)
        v_s2 = P.step("scalar", "ss", lambda e: e.sqrt(BN_T2[:], BN_T[:]))
        P.wait("vector", "ss", v_s2)
        P.step("vector", "vv", lambda e: e.reciprocal(BN_T[:], BN_T2[:]))
        P.drain("vector")
        P.step("vector", "vv", lambda e: e.tensor_tensor(
            out=BN_SC[:], in0=BN_G[:], in1=BN_T[:], op=ALU.mult))
        P.drain("vector")
        P.step("vector", "vv", lambda e: e.tensor_tensor(
            out=BN_T[:], in0=ST_SUM[:], in1=BN_SC[:], op=ALU.mult))
        P.drain("vector")
        v_bnc = P.step("vector", "vv", lambda e: e.tensor_tensor(
            out=BN_BI[:], in0=BN_B[:], in1=BN_T[:], op=ALU.subtract))
        P.drain("vector")

        P.wait("tensor", "gd", state["v_keysT"])
        sim_mm_last = [0, 0]
        sim_ev = [0, 0]
        for c in range(16):
            cb = c % 2
            qn_write_v = []
            for tb in range(4):
                bk = tb % 2
                P.wait("tensor", "vv", max(v_bnc, bank_free[bk]))
                for k in range(8):
                    P.step("tensor", "tt", lambda e, c=c, tb=tb, k=k, bk=bk:
                           e.matmul(psum[bk][:],
                                    wq_fsb[:, k, c * 128:(c + 1) * 128],
                                    xn_fm[:, k, tb * 512:(tb + 1) * 512],
                                    start=(k == 0), stop=(k == 7)))
                v_mm = P.cnt["tt"]
                P.wait("vector", "tt", max(v_mm, sim_mm_last[cb]))
                bank_free[bk] = P.step("vector", "vv",
                                       lambda e, c=c, tb=tb, cb=cb, bk=bk:
                                       e.tensor_scalar(
                    out=qn2[:, cb, tb * 512:(tb + 1) * 512], in0=psum[bk][:],
                    scalar1=BN_SC[:, c:c + 1], scalar2=BN_BI[:, c:c + 1],
                    op0=ALU.mult, op1=ALU.add))
            v_qn = P.cnt["vv"]
            for t in range(NT):
                sb2 = 2 + (t % 2)
                P.wait("tensor", "vv", max(v_qn, sim_ev[t % 2]))
                v_mm = P.step("tensor", "tt", lambda e, c=c, t=t, cb=cb, sb2=sb2:
                              e.matmul(psum[sb2][:, 0:256],
                                       qn2[:, cb, t * 128:(t + 1) * 128],
                                       keysT_b[:, c, :], start=True, stop=True))
                P.wait("vector", "tt", v_mm)
                P.wait("vector", "ld", v_iota)
                P.step("vector", "vv", lambda e, t=t, sb2=sb2:
                       e.scalar_tensor_tensor(
                    out=simpay[:, t % 2, :].bitcast(U32),
                    in0=psum[sb2][:, 0:256].bitcast(U32),
                    scalar=C_MHI8, in1=sb_iota[:],
                    op0=ALU.bitwise_and, op1=ALU.bitwise_or))
                P.drain("vector")
                sim_ev[t % 2] = P.step("vector", "vv", lambda e, c=c, t=t: e.max(
                    out=c8_all[:, t, c, :], in_=simpay[:, t % 2, :]))
            sim_mm_last[cb] = P.cnt["tt"]

        for t in range(NT):
            c8u = c8_all[:, t, :, :].bitcast(U32)
            P.step("vector", "vv", lambda e, c8u=c8u: e.tensor_scalar(
                out=sxw[:].rearrange("p h j -> p (h j)").bitcast(U32),
                in0=c8u[:, 0:8, :].rearrange("p h j -> p (h j)"),
                scalar1=C_MHI8, scalar2=None, op0=ALU.bitwise_and))
            P.step("vector", "vv", lambda e, c8u=c8u: e.tensor_scalar(
                out=syw[:].rearrange("p h j -> p (h j)").bitcast(U32),
                in0=c8u[:, 8:16, :].rearrange("p h j -> p (h j)"),
                scalar1=C_MHI8, scalar2=None, op0=ALU.bitwise_and))
            P.step("vector", "vv", lambda e, c8u=c8u: e.tensor_scalar(
                out=ixw[:].rearrange("p h j -> p (h j)"),
                in0=c8u[:, 0:8, :].rearrange("p h j -> p (h j)"),
                scalar1=24, scalar2=16,
                op0=ALU.logical_shift_left, op1=ALU.logical_shift_right))
            P.step("vector", "vv", lambda e, c8u=c8u: e.tensor_scalar(
                out=iyw[:].rearrange("p h j -> p (h j)"),
                in0=c8u[:, 8:16, :].rearrange("p h j -> p (h j)"),
                scalar1=24, scalar2=24,
                op0=ALU.logical_shift_left, op1=ALU.logical_shift_right))
            P.drain("vector")
            P.step("vector", "vv", lambda e: e.tensor_tensor(
                out=zwork[:],
                in0=sxw[:].unsqueeze(3).to_broadcast([128, 8, 8, 8]),
                in1=syw[:].unsqueeze(2).to_broadcast([128, 8, 8, 8]),
                op=ALU.add))
            P.drain("vector")
            P.step("vector", "vv", lambda e: e.scalar_tensor_tensor(
                out=zidx[:],
                in0=zwork[:].rearrange("p h a b -> p (h a b)").bitcast(U32)
                    .rearrange("p (h a b) -> p h a b", h=8, a=8),
                scalar=C_MHI16,
                in1=ixw[:].unsqueeze(3).to_broadcast([128, 8, 8, 8]),
                op0=ALU.bitwise_and, op1=ALU.bitwise_or))
            P.drain("vector")
            P.step("vector", "vv", lambda e: e.tensor_tensor(
                out=zidx[:], in0=zidx[:],
                in1=iyw[:].unsqueeze(2).to_broadcast([128, 8, 8, 8]),
                op=ALU.bitwise_or))
            P.drain("vector")
            for h in range(8):
                P.step("vector", "vv", lambda e, h=h: e.max(
                    out=f8w[:, h * 8:(h + 1) * 8],
                    in_=zidx[:, h, :, :].rearrange("p a b -> p (a b)")
                    .bitcast(F32)))
                P.step("vector", "vv", lambda e, h=h: e.max(
                    out=s8w[:, h, :],
                    in_=zwork[:, h, :, :].rearrange("p a b -> p (a b)")))
            P.wait("vector", "vv", v_msk)
            P.drain("vector")
            P.step("vector", "vv", lambda e, t=t: e.tensor_scalar(
                out=idx_all[:, t, :], in0=f8w[:].bitcast(U32),
                scalar1=C_MLO16, scalar2=None, op0=ALU.bitwise_and))
            P.step("vector", "vv", lambda e: e.tensor_tensor(
                out=e8w[:], in0=s8w[:],
                in1=s8w[:, :, 0:1].to_broadcast([128, 8, 8]), op=ALU.subtract))
            v_e = P.cnt["vv"]
            P.wait("scalar", "vv", v_e)
            v_x8 = P.step("scalar", "ss", lambda e: e.activation(
                out=e8w[:], in_=e8w[:], func=AF.Exp))
            P.wait("vector", "ss", v_x8)
            P.step("vector", "vv", lambda e: e.tensor_reduce(
                out=r8w[:], in_=e8w[:], axis=AX.X, op=ALU.add))
            P.drain("vector")
            P.step("vector", "vv", lambda e: e.reciprocal(r8w[:], r8w[:]))
            P.drain("vector")
            P.step("vector", "vv", lambda e, t=t: e.tensor_tensor(
                out=smax_all[:, t, :].rearrange("p (h j) -> p h j", h=8),
                in0=e8w[:], in1=r8w[:].unsqueeze(2).to_broadcast([128, 8, 8]),
                op=ALU.mult))
        v_topk = P.cnt["vv"]

        P.wait("gpsimd", "cw", conv_w[CONV - 1])
        P.wait("gpsimd", "vv", v_topk)
        up_mm = {}
        res_add = [0, 0]
        i = 0
        for t in range(NT):
            u0, u1 = (t % 2) * 2, (t % 2) * 2 + 1
            for hj in range(64):
                b = i % NBUF
                if i >= NBUF:
                    P.wait("gpsimd", "tt", up_mm[i - NBUF])
                v_g = P.step("gpsimd", "gd", lambda e, t=t, hj=hj, b=b:
                             e.indirect_dma_start(
                                 gbuf[:, b, :], None, fusedb,
                                 IndirectOffsetOnAxis(
                                     idx_all[:, t, hj:hj + 1].bitcast(I32), 0)), 16)
                P.wait("vector", "gd", v_g)
                v_d = P.step("vector", "vv", lambda e, t=t, hj=hj, b=b:
                             e.scalar_tensor_tensor(
                                 out=sq_scr[:, 0:1024], in0=gbuf[:, b, 0:1024],
                                 scalar=1.0, in1=xn_tok[:, t, :],
                                 op0=ALU.mult, op1=ALU.mult,
                                 accum_out=hdot[:, hj:hj + 1]))
                P.wait("scalar", "vv", v_d)
                v_gl = P.step("scalar", "ss", lambda e, hj=hj: e.activation(
                    out=gelu_c[:, hj:hj + 1], in_=hdot[:, hj:hj + 1],
                    func=AF.Gelu_apprx_tanh))
                P.wait("vector", "ss", v_gl)
                v_sc = P.step("vector", "vv", lambda e, t=t, hj=hj, b=b:
                              e.tensor_scalar(
                                  out=gbuf[:, b, 1024:2048],
                                  in0=gbuf[:, b, 1024:2048],
                                  scalar1=gelu_c[:, hj:hj + 1],
                                  scalar2=smax_all[:, t, hj:hj + 1],
                                  op0=ALU.mult, op1=ALU.mult))
                P.wait("tensor", "vv", v_sc)
                if hj == 0:
                    P.wait("tensor", "vv", res_add[t % 2])
                    P.wait("tensor", "ld", v_id)
                P.step("tensor", "tt", lambda e, b=b, hj=hj, u0=u0: e.matmul(
                    psum[u0][:], sb_id[:], gbuf[:, b, 1024:1536],
                    start=(hj == 0), stop=(hj == 63)))
                up_mm[i] = P.step("tensor", "tt", lambda e, b=b, hj=hj, u1=u1:
                                  e.matmul(psum[u1][:], sb_id[:],
                                           gbuf[:, b, 1536:2048],
                                           start=(hj == 0), stop=(hj == 63)))
                i += 1
            P.wait("vector", "tt", up_mm[i - 1])
            P.step("vector", "vv", lambda e, t=t, u0=u0: e.scalar_tensor_tensor(
                out=sb_x[:, t, 0:512], in0=psum[u0][:], scalar=0.5,
                in1=sb_x[:, t, 0:512], op0=ALU.mult, op1=ALU.add))
            res_add[t % 2] = P.step("vector", "vv", lambda e, t=t, u1=u1:
                                    e.scalar_tensor_tensor(
                out=sb_x[:, t, 512:1024], in0=psum[u1][:], scalar=0.5,
                in1=sb_x[:, t, 512:1024], op0=ALU.mult, op1=ALU.add))
        return P.cnt["vv"], P.cnt["tt"]

    def emit_rope_group(t, grp, pb, dst, v_qm, swb, last_rope):
        for mi in range(4):
            c = grp * 4 + mi
            P.wait("vector", "tt", v_qm)
            if last_rope["sw"]:
                P.wait("vector", "tt", last_rope["sw"])
            v_ev = P.step("vector", "vv", lambda e, mi=mi, pb=pb, c=c:
                          e.tensor_copy(out=qraw[:, c % 2, :],
                                        in_=psum[pb][:, mi * 128:(mi + 1) * 128]))
            P.wait("tensor", "vv", max(v_ev, last_rope["tt2"]))
            P.wait("tensor", "ld", v_swap)
            v_sw = P.step("tensor", "tt", lambda e, swb=swb, c=c: e.matmul(
                psum[swb][:, 0:128], sb_swap[:], qraw[:, c % 2, :],
                start=True, stop=True))
            last_rope["sw"] = v_sw
            P.wait("vector", "tt", v_sw)
            P.wait("vector", "ld", state.get("v_rope_ld", 0))
            P.drain("vector")
            P.step("vector", "vv", lambda e, t=t, swb=swb: e.tensor_tensor(
                out=rope_tmp[:], in0=psum[swb][:, 0:128],
                in1=stab[:, t * 128:(t + 1) * 128], op=ALU.mult))
            P.step("vector", "vv", lambda e, t=t, c=c, dst=dst: e.tensor_tensor(
                out=dst[:, c, t * 128:(t + 1) * 128], in0=qraw[:, c % 2, :],
                in1=ctab[:, t * 128:(t + 1) * 128], op=ALU.mult))
            P.drain("vector")
            last_rope["tt2"] = P.step("vector", "vv",
                                      lambda e, t=t, c=c, dst=dst: e.tensor_tensor(
                out=dst[:, c, t * 128:(t + 1) * 128],
                in0=dst[:, c, t * 128:(t + 1) * 128],
                in1=rope_tmp[:], op=ALU.add))
        return last_rope["tt2"]

    def emit_attention(x_ready_vv, att_stop=None, att_tiles=NT):
        v_lnst = emit_ln_stats(x_ready_vv)
        P.wait("tensor", "gd", state["v_attw1"])
        lat_ev = [0, 0]
        fm_ev = [0, 0]
        rq = {"sw": 0, "tt2": 0}
        for t in range(att_tiles):
            v_a = emit_ln_apply(t, ln2t[:], v_lnst,
                                extra_vv=max(fm_ev[0], fm_ev[1]), strong=True)
            P.wait("tensor", "aa", v_a)
            v_fm = emit_transpose_tile(ln2t[:], lnfm_dst(t), 4 + (t % 2),
                                       waits_vv=(fm_ev[t % 2],))
            fm_ev[t % 2] = v_fm
            lb = t % 2
            P.wait("tensor", "vv", max(v_fm, lat_ev[lb]))
            for ml in range(4):
                for k in range(8):
                    P.step("tensor", "tt", lambda e, t=t, ml=ml, k=k, lb=lb:
                           e.matmul(psum[lb][:, ml * 128:(ml + 1) * 128],
                                    wdn_b[:, k, ml * 128:(ml + 1) * 128],
                                    lnfm_col(t, k),
                                    start=(k == 0), stop=(k == 7)))
            v_lm = P.cnt["tt"]
            P.wait("vector", "tt", v_lm)
            lat_ev[lb] = P.step("vector", "vv", lambda e, t=t, lb=lb:
                                e.tensor_copy(
                out=lat_fm[:, :, t * 128:(t + 1) * 128],
                in_=psum[lb][:].rearrange("p (c q) -> p c q", c=4)))
            qb = 2 + (t % 2)
            for grp in range(2):
                P.wait("tensor", "vv", max(v_fm, rq["tt2"]))
                for m in range(grp * 4, grp * 4 + 4):
                    for k in range(8):
                        P.step("tensor", "tt",
                               lambda e, t=t, m=m, k=k, qb=qb: e.matmul(
                            psum[qb][:, (m % 4) * 128:(m % 4 + 1) * 128],
                            wq_asb[:, k, m * 128:(m + 1) * 128],
                            lnfm_col(t, k), start=(k == 0), stop=(k == 7)))
                emit_rope_group(t, grp, qb, q_rot, P.cnt["tt"], 6 + (t % 2), rq)
        v_loop1 = P.cnt["vv"]
        if att_stop == 1:
            return v_loop1, P.cnt["tt"]

        P.wait("tensor", "gd", state["v_attw2"])
        rk = {"sw": 0, "tt2": 0}
        for t in range(NT):
            kb = t % 2
            for grp in range(2):
                P.wait("tensor", "vv",
                       max(v_loop1 if t == 0 else 0, lat_ev[t % 2], rk["tt2"]))
                for m in range(grp * 4, grp * 4 + 4):
                    for k4 in range(4):
                        P.step("tensor", "tt",
                               lambda e, t=t, m=m, k4=k4, kb=kb: e.matmul(
                            psum[kb][:, (m % 4) * 128:(m % 4 + 1) * 128],
                            wku_b[:, k4, m * 128:(m + 1) * 128],
                            lat_fm[:, k4, t * 128:(t + 1) * 128],
                            start=(k4 == 0), stop=(k4 == 3)))
                emit_rope_group(t, grp, kb, k_rot, P.cnt["tt"], 2 + (t % 2), rk)
        v_loop2 = P.cnt["vv"]
        P.wait("gpsimd", "vv", v_loop2)
        state["v_attw3"] = P.step("gpsimd", "gd", lambda e: e.dma_start(
            out=wvu_b[:], in_=wvu_d.rearrange("(k p) m -> p k m", p=128)), 16)
        if att_stop == 2:
            return v_loop2, P.cnt["tt"]

        P.wait("vector", "vv", v_loop2)
        v_ones = P.step("vector", "vv", lambda e: e.memset(
            v65[:].rearrange("p a (g u) -> p a g u", g=16)[:, :, :, 64:65], 1.0))
        P.wait("tensor", "gd", state["v_attw3"])
        vv_ev = [v_ones, v_ones]
        for t in range(NT):
            vb = t % 2
            for nb in range(2):
                P.wait("tensor", "vv", max(v_ones, vv_ev[vb]))
                for k4 in range(4):
                    P.step("tensor", "tt", lambda e, t=t, nb=nb, k4=k4, vb=vb:
                           e.matmul(psum[vb][:],
                                    lat_fm[:, k4, t * 128:(t + 1) * 128],
                                    wvu_b[:, k4, nb * 512:(nb + 1) * 512],
                                    start=(k4 == 0), stop=(k4 == 3)))
                v_vm = P.cnt["tt"]
                P.wait("vector", "tt", v_vm)
                vv_ev[vb] = P.step("vector", "vv", lambda e, t=t, nb=nb, vb=vb:
                                   e.tensor_copy(
                    out=v65[:, t, :].rearrange("p (g u) -> p g u", g=16)
                        [:, nb * 8:(nb + 1) * 8, 0:64],
                    in_=psum[vb][:].rearrange("p (g u) -> p g u", g=8)))
        v_loop3 = P.cnt["vv"]
        P.wait("gpsimd", "vv", v_loop3)
        state["v_wo"] = P.step("gpsimd", "gd", lambda e: e.dma_start(
            out=wo_b[:], in_=wo_d.rearrange("(k p) m -> p k m", p=128)), 16)
        if att_stop == 3:
            return v_loop3, P.cnt["tt"]

        P.wait("tensor", "gd", state["v_wo"])
        s_bank_exp = [0, 0]
        pt_bank_ev = [0, 0]
        pv_done = [0, 0]
        pt_cons = 0
        p_cons = 0
        oadd = [0, 0]
        for qt in range(NT):
            kr = (qt // 2 + 1) * 256
            nk128 = kr // 128
            for h2 in range(HA):
                h = h2
                hp = (h % 2) * 64
                ch = h // 2
                pvb = 6 + (h % 2)
                csizes = []
                off = 0
                while off < kr:
                    w = min(512, kr - off)
                    csizes.append((off, w))
                    off += w
                exp_v = []
                for ci, (off, w) in enumerate(csizes):
                    sbk = ci % 2
                    P.wait("tensor", "vv", v_loop3)
                    P.wait("tensor", "ss", s_bank_exp[sbk])
                    P.step("tensor", "tt", lambda e, qt=qt, hp=hp, ch=ch,
                           off=off, w=w, sbk=sbk: e.matmul(
                        psum[sbk][:, 0:w],
                        q_rot[hp:hp + 64, ch, qt * 128:(qt + 1) * 128],
                        k_rot[hp:hp + 64, ch, off:off + w],
                        start=True, stop=True))
                    v_smm = P.cnt["tt"]
                    P.wait("scalar", "tt", max(v_smm, p_cons))
                    s_bank_exp[sbk] = P.step("scalar", "ss",
                                             lambda e, off=off, w=w, sbk=sbk:
                                             e.activation(
                        out=p_sb[:, off:off + w], in_=psum[sbk][:, 0:w],
                        func=AF.Exp, bias=C_M32, scale=0.125))
                    exp_v.append(s_bank_exp[sbk])
                for g in range(0, nk128, 8):
                    gi = (g // 8) % 2
                    gbk = 4 + gi
                    pt8 = psum[gbk][:].bitcast(BF16).rearrange(
                        "p (c q) -> p c q", c=8)
                    n_in_g = min(8, nk128 - g)
                    P.wait("tensor", "vv", pt_bank_ev[gi])
                    P.wait("tensor", "ld", v_id)
                    for c2 in range(n_in_g):
                        ck = g + c2
                        P.wait("tensor", "ss", exp_v[(ck * 128) // 512])
                        P.step("tensor", "tt", lambda e, ck=ck, c2=c2, pt8=pt8:
                               e.transpose(pt8[:, c2, :],
                                           p_sb[:, ck * 128:(ck + 1) * 128],
                                           sb_id[:]))
                    v_tr = P.cnt["tt"]
                    p_cons = v_tr
                    P.wait("vector", "tt", max(v_tr, pt_cons))
                    pt_bank_ev[gi] = P.step(
                        "vector", "vv",
                        lambda e, g=g, n_in_g=n_in_g, pt8=pt8: e.tensor_copy(
                            out=pt_sb[:, g * 128:(g + n_in_g) * 128],
                            in_=pt8.rearrange("p c q -> p (c q)")
                            [:, 0:n_in_g * 128]))
                v_ptall = P.cnt["vv"]
                P.wait("tensor", "vv", max(v_ptall, pv_done[h % 2], v_ones))
                for ck in range(nk128):
                    P.step("tensor", "tt", lambda e, qt=qt, h=h, ck=ck, pvb=pvb:
                           e.matmul(psum[pvb][:, 0:65],
                                    pt_sb[:, ck * 128:(ck + 1) * 128],
                                    v65[:, ck, h * 65:(h + 1) * 65],
                                    start=(ck == 0), stop=(ck == nk128 - 1)))
                v_pv = P.cnt["tt"]
                pt_cons = v_pv
                P.wait("vector", "tt", v_pv)
                P.step("vector", "vv", lambda e, pvb=pvb: e.reciprocal(
                    dwork[:, 0:1], psum[pvb][:, 64:65]))
                P.drain("vector")
                pv_done[h % 2] = P.step("vector", "vv", lambda e, h=h, pvb=pvb:
                                        e.tensor_scalar(
                    out=atok[:, h * 64:(h + 1) * 64], in0=psum[pvb][:, 0:64],
                    scalar1=dwork[:, 0:1], scalar2=None, op0=ALU.mult))
            v_atok = P.cnt["vv"]
            v_afm = emit_transpose_tile(
                atok[:], afm[:], 4 + (qt % 2),
                waits_vv=(v_atok, pt_bank_ev[qt % 2]))
            pt_bank_ev[qt % 2] = v_afm
            for nb2 in range(2):
                P.wait("tensor", "vv", max(v_afm, oadd[qt % 2]))
                for f in range(8):
                    P.step("tensor", "tt", lambda e, nb2=nb2, f=f: e.matmul(
                        psum[2 + nb2][:], afm[:, f, :],
                        wo_b[:, f, nb2 * 512:(nb2 + 1) * 512],
                        start=(f == 0), stop=(f == 7)))
            v_om = P.cnt["tt"]
            P.wait("vector", "tt", v_om)
            P.step("vector", "vv", lambda e, qt=qt: e.scalar_tensor_tensor(
                out=sb_x[:, qt, 0:512], in0=psum[2][:], scalar=1.0,
                in1=sb_x[:, qt, 0:512], op0=ALU.mult, op1=ALU.add))
            oadd[qt % 2] = P.step("vector", "vv", lambda e, qt=qt:
                                  e.scalar_tensor_tensor(
                out=sb_x[:, qt, 512:1024], in0=psum[3][:], scalar=1.0,
                in1=sb_x[:, qt, 512:1024], op0=ALU.mult, op1=ALU.add))
        return P.cnt["vv"], P.cnt["tt"]

    state["v_wqf"] = P.step("gpsimd", "gd", lambda e: e.dma_start(
        out=wq_fsb[:], in_=wqf_d.rearrange("(k p) m -> p k m", p=128)), 16)
    state["v_keysT"] = P.step("gpsimd", "gd", lambda e: e.dma_start(
        out=keysT_b[:], in_=keysT_d.rearrange("c p k -> p c k")), 16)
    cc_pre = [0, 0]
    if not skip_ffn1:
        exp_ccw = P.cnt["sd2"] + 16
        P.wait("gpsimd", "sd2", exp_ccw)
        cc_pre[0] = P.step("gpsimd", "cc", lambda e: e.collective_compute(
            "AllReduce", ALU.add, replica_groups=[list(range(8))],
            ins=[ccin], outs=[ccout]))
        cc_pre[1] = exp_ccw
    emit_conv()
    if skip_ffn1:
        v_ffn1_vv, v_ffn1_tt = 1, 0
    else:
        v_ffn1_vv, v_ffn1_tt = emit_ffn(0, 1)

    if stage >= 2:
        P.wait("gpsimd", "vv", v_ffn1_vv)
        P.step("gpsimd", "gd", lambda e: e.dma_start(
            out=wq_asb[:], in_=wq_a.rearrange("(k p) m -> p k m", p=128)), 16)
        state["v_attw1"] = P.step("gpsimd", "gd", lambda e: e.dma_start(
            out=wdn_b[:], in_=wdn_d.rearrange("(k p) m -> p k m", p=128)), 16)
        state["v_attw2"] = P.step("gpsimd", "gd", lambda e: e.dma_start(
            out=wku_b[:], in_=wku_d.rearrange("(k p) m -> p k m", p=128)), 16)
        P.wait("sync", "vv", v_ffn1_vv)
        P.step("sync", "ld", lambda e: e.dma_start(out=ctab[:], in_=ctab_d), 16)
        state["v_rope_ld"] = P.step("sync", "ld", lambda e: e.dma_start(
            out=stab[:], in_=stab_d), 16)
        v_att_vv, v_att_tt = emit_attention(v_ffn1_vv, att_stop, att_tiles)
    else:
        v_att_vv, v_att_tt = v_ffn1_vv, v_ffn1_tt

    if stage >= 3:
        P.wait("gpsimd", "vv", v_att_vv)
        P.wait("gpsimd", "tt", v_att_tt)
        state["v_wqf"] = P.step("gpsimd", "gd", lambda e: e.dma_start(
            out=wq_fsb[:], in_=wqf_d.rearrange("(k p) m -> p k m", p=128)), 16)
        state["v_keysT"] = P.step("gpsimd", "gd", lambda e: e.dma_start(
            out=keysT_b[:], in_=keysT_d.rearrange("c p k -> p c k")), 16)
        v_f2_vv, _ = emit_ffn(1, v_att_vv)
    else:
        v_f2_vv = v_att_vv

    P.wait("scalar", "vv", v_f2_vv)
    v_st = P.step("scalar", "sd2", lambda e: e.dma_start(
        out=y_out.rearrange("(a p) d -> p a d", p=128), in_=sb_x[:]), 16)
    if dbg:
        dbg_specs = [
            ("d_xntok", xn_tok[:], [128, NT, 1024], BF16),
            ("d_c8", c8_all[:], [128, NT, 16, 8], F32),
            ("d_idx", idx_all[:], [128, NT, 64], U32),
            ("d_smax", smax_all[:], [128, NT, 64], F32),
            ("d_bnsc", BN_SC[:], [128, 16], F32),
            ("d_bnbi", BN_BI[:], [128, 16], F32),
            ("d_hdot", hdot[:], [128, 64], F32),
            ("d_xnfm", xn_fm[:], [128, 8, 2048], BF16),
            ("d_gbuf", gbuf[:], [128, NBUF, 2048], BF16),
            ("d_lnsc", LN_SC[:], [128, 16], F32),
            ("d_lnb2", LN_B2[:], [128, 16], F32),
        ]
        if stage >= 2:
            dbg_specs += [
                ("d_qrot", q_rot[:], [128, 8, 2048], BF16),
                ("d_krot", k_rot[:], [128, 8, 2048], BF16),
                ("d_v65", v65[:], [128, NT, 1040], BF16),
                ("d_atok", atok[:], [128, 1024], BF16),
                ("d_ptsb", pt_sb[:], [128, 2048], BF16),
                ("d_b1", sb_b1[:], [128, 16640], BF16),
                ("d_b0p", sb_b0[:, 0:2048], [128, 2048], BF16),
                ("d_aux0", lat_fm[:], [128, 4, 2048], BF16),
            ]
        for nm, ap, shp, dt2 in dbg_specs:
            od = dtn(nm, shp, dt2, kind="ExternalOutput").ap()
            v_st = P.step("scalar", "sd2",
                          lambda e, od=od, ap=ap: e.dma_start(out=od, in_=ap), 16)
    P.wait("scalar", "sd2", v_st)

    with nc.Block() as block:
        @block.sync
        def _(eng):
            for fn in P.fns["sync"]:
                fn(eng)

        @block.scalar
        def _(eng):
            for fn in P.fns["scalar"]:
                fn(eng)

        @block.vector
        def _(eng):
            for fn in P.fns["vector"]:
                fn(eng)

        @block.tensor
        def _(eng):
            for fn in P.fns["tensor"]:
                fn(eng)

        @block.gpsimd
        def _(eng):
            for fn in P.fns["gpsimd"]:
                fn(eng)

    stack.close()
    return nc


def make_consts():
    id128 = np.eye(128, dtype=ml_dtypes.bfloat16)
    swap = np.zeros((128, 128), dtype=ml_dtypes.bfloat16)
    for i in range(0, 128, 2):
        swap[i, i + 1] = 1
        swap[i + 1, i] = 1
    iota = np.tile(np.arange(256, dtype=np.uint32), (128, 1))
    inv = 1.0 / 10000.0 ** (np.arange(0, DA, 2, dtype=np.float32) / DA)
    ang = np.outer(np.arange(T, dtype=np.float32), inv)
    cos = np.cos(ang).astype(np.float32)
    sin = np.sin(ang).astype(np.float32)
    ctab = np.zeros((128, T), np.float32)
    stab = np.zeros((128, T), np.float32)
    for p in range(128):
        r = (p % 64) // 2
        ctab[p] = cos[:, r]
        stab[p] = sin[:, r] * (1.0 if p % 2 == 1 else -1.0)
    return id128, swap, iota, ctab, stab


def make_inputs(L0_attn_x, L0_attn_wq, L0_attn_w_down, L0_attn_wk_up,
                L0_attn_wv_up, L0_attn_wo, L0_ffn_w_q, L0_ffn_bn_gamma,
                L0_ffn_bn_beta, L0_ffn_keys, L0_ffn_down_embed, L0_ffn_up_embed):
    id128, swap, iota, ctab, stab = make_consts()
    keysT = np.ascontiguousarray(
        np.transpose(np.asarray(L0_ffn_keys), (2, 0, 3, 1)).reshape(16, KD, NK))
    fused = np.concatenate(
        [np.asarray(L0_ffn_down_embed), np.asarray(L0_ffn_up_embed)], axis=1)
    common = dict(
        wq_a=np.asarray(L0_attn_wq), wdn=np.asarray(L0_attn_w_down),
        wku=np.asarray(L0_attn_wk_up), wvu=np.asarray(L0_attn_wv_up),
        wo_a=np.asarray(L0_attn_wo), wqf=np.asarray(L0_ffn_w_q),
        bng=np.ascontiguousarray(np.asarray(L0_ffn_bn_gamma).reshape(16, 128).T),
        bnb=np.ascontiguousarray(np.asarray(L0_ffn_bn_beta).reshape(16, 128).T),
        keysT=keysT, fused=np.ascontiguousarray(fused),
        id128=id128, swap128=swap, iota256=iota, ctab=ctab, stab=stab,
    )
    x = np.asarray(L0_attn_x)
    return [dict(common, x=np.ascontiguousarray(x[c])) for c in range(8)]


def kernel(**inputs):
    nc = build(stage=3)
    in_maps = make_inputs(**inputs)
    res = run_bass_kernel_spmd(nc, in_maps, core_ids=list(range(8)))
    return np.stack([res.results[c]["y"] for c in range(8)], axis=0)


# version 05
# speedup vs baseline: 1.0180x; 1.0180x over previous
import numpy as np
import ml_dtypes
from contextlib import ExitStack

import concourse.bass as bass
import concourse.mybir as mybir
from concourse.bass_utils import run_bass_kernel_spmd
from concourse.bass import IndirectOffsetOnAxis

F32 = mybir.dt.float32
BF16 = mybir.dt.bfloat16
U32 = mybir.dt.uint32
I32 = mybir.dt.int32
U8 = mybir.dt.uint8
AF = mybir.ActivationFunctionType
ALU = mybir.AluOpType
AX = mybir.AxisListType

T = 2048
NT = 16
E = 1024
HA, DA = 16, 64
H, KD, NK, TK = 8, 128, 256, 8
NEXP = NK * NK
FQ = 2048
NBUF = 8
ENGS = ("sync", "scalar", "vector", "tensor", "gpsimd")


class Plan:
    def __init__(self, nc, stack):
        self.nc = nc
        self.fns = {e: [] for e in ENGS}
        self.sems = {}
        self.cnt = {}
        for s in ("ld", "sd2", "gd", "vv", "ss", "tt", "cc", "aa", "cw"):
            self.sems[s] = stack.enter_context(nc.semaphore(s))
            self.cnt[s] = 0

    def emit(self, eng, fn):
        self.fns[eng].append(fn)

    def wait(self, eng, sem, val):
        if val is None or val <= 0:
            return
        h = self.sems[sem]
        self.emit(eng, lambda e, h=h, v=val: e.wait_ge(h, v))

    def step(self, eng, sem, fn, n=1):
        self.cnt[sem] += n
        h = self.sems[sem]
        self.emit(eng, lambda e, h=h, n=n, fn=fn: fn(e).then_inc(h, n))
        return self.cnt[sem]

    def run(self, eng, fn):
        self.emit(eng, fn)

    def drain(self, eng):
        self.emit(eng, lambda e: e.drain())


def build(stage=3, dbg=False, att_stop=None, skip_ffn1=False, att_tiles=NT):
    nc = bass.Bass("TRN2", target_bir_lowering=False)
    stack = ExitStack()

    dtn = nc.dram_tensor
    x_in = dtn("x", [T, E], F32, kind="ExternalInput").ap()
    y_out = dtn("y", [T, E], F32, kind="ExternalOutput").ap()
    wq_a = dtn("wq_a", [E, E], F32, kind="ExternalInput").ap()
    wdn_d = dtn("wdn", [E, 512], F32, kind="ExternalInput").ap()
    wku_d = dtn("wku", [512, E], F32, kind="ExternalInput").ap()
    wvu_d = dtn("wvu", [512, E], F32, kind="ExternalInput").ap()
    wo_d = dtn("wo_a", [E, E], F32, kind="ExternalInput").ap()
    wqf_d = dtn("wqf", [E, FQ], F32, kind="ExternalInput").ap()
    bng_d = dtn("bng", [128, 16], F32, kind="ExternalInput").ap()
    bnb_d = dtn("bnb", [128, 16], F32, kind="ExternalInput").ap()
    keysT_d = dtn("keysT", [16, KD, NK], F32, kind="ExternalInput").ap()
    fused_d = dtn("fused", [NEXP, 2048], F32, kind="ExternalInput").ap()
    id_d = dtn("id128", [128, 128], BF16, kind="ExternalInput").ap()
    iota_d = dtn("iota256", [128, 256], U32, kind="ExternalInput").ap()
    ctab_d = dtn("ctab", [128, T], F32, kind="ExternalInput").ap()
    stab_d = dtn("stab", [128, T], F32, kind="ExternalInput").ap()
    swap_d = dtn("swap128", [128, 128], BF16, kind="ExternalInput").ap()

    fusedb = dtn("fusedb", [NEXP, 2048], BF16, kind="Internal").ap()
    ccin = dtn("ccin", [128, 32], F32, kind="Internal").ap()
    ccout = dtn("ccout", [128, 32], F32, kind="Internal").ap()

    def sbuf(name, shape, dt):
        return stack.enter_context(nc.sbuf_tensor(name, shape, dt))

    sb_x = sbuf("sb_x", [128, NT, E], F32)
    sb_b0 = sbuf("sb_b0", [128, 16384], BF16)
    sb_b1 = sbuf("sb_b1", [128, 16640], BF16)
    sb_b2 = sbuf("sb_b2", [128, 16384], BF16)
    sb_aux = sbuf("sb_aux", [128, 36864], U8)
    sb_id = sbuf("sb_id", [128, 128], BF16)
    sb_swap = sbuf("sb_swap", [128, 128], BF16)
    sb_iota = sbuf("sb_iota", [128, 256], U32)
    sb_small = sbuf("sb_small", [128, 96], F32)
    sb_conv = sbuf("sb_conv", [128, 2, 2048], BF16)

    psum = [stack.enter_context(nc.psum_tensor(f"pb{i}", [128, 512], F32))
            for i in range(8)]

    smallu = sb_small[:].bitcast(U32)
    C_M32 = sb_small[:, 0:1]
    C_MHI8 = smallu[:, 1:2]
    C_MHI16 = smallu[:, 2:3]
    C_MLO16 = smallu[:, 3:4]
    LN_SUM = sb_small[:, 4:20]
    LN_SQ = sb_small[:, 20:36]
    LN_SC = sb_small[:, 36:52]
    LN_B2 = sb_small[:, 52:68]
    LN_T4 = sb_small[:, 68:72]
    LN_T4B = sb_small[:, 72:76]
    ST_SUM = sb_small[:, 4:20]
    ST_SQ = sb_small[:, 20:36]
    ST_ALL = sb_small[:, 4:36]

    def aview(off, shape, dt):
        n = 1
        for s in shape[1:]:
            n *= s
        nbytes = n * mybir.dt.size(dt)
        assert off + nbytes <= 36864, (off, shape, dt)
        ap = sb_aux[:, off:off + nbytes].bitcast(dt)
        if len(shape) == 2:
            return ap
        kw = {f"d{i}": shape[i] for i in range(1, len(shape) - 1)}
        spec = " ".join(f"d{i}" for i in range(1, len(shape)))
        return ap.rearrange(f"p ({spec}) -> p {spec}", **kw)

    keysT_b = aview(0, [128, 16, NK], BF16)
    zwork = aview(0, [128, 8, 8, 8], F32)
    zidx = aview(2048, [128, 8, 8, 8], U32)
    sxw = aview(4096, [128, 8, 8], F32)
    syw = aview(4352, [128, 8, 8], F32)
    ixw = aview(4608, [128, 8, 8], U32)
    iyw = aview(4864, [128, 8, 8], U32)
    f8w = aview(5120, [128, 64], F32)
    s8w = aview(5376, [128, 8, 8], F32)
    e8w = aview(5632, [128, 8, 8], F32)
    r8w = aview(5888, [128, 8], F32)
    hdot = aview(6144, [128, 64], F32)
    gelu_c = aview(6400, [128, 64], F32)
    c8_all = aview(8192, [128, NT, 16, 8], F32)
    idx_all = aview(16384, [128, NT, 64], U32)
    smax_all = aview(20480, [128, NT, 64], F32)
    qn2 = aview(24576, [128, 2, FQ], BF16)
    simpay = aview(32768, [128, 2, NK], F32)
    sq_scr = aview(32768, [128, 1024], BF16)

    lat_fm = aview(0, [128, 4, T], BF16)
    ctab = aview(16384, [128, T], F32)
    stab = aview(24576, [128, T], F32)
    ln2t = aview(32768, [128, 1024], BF16)
    qraw = aview(34816, [128, 2, 128], BF16)
    lnfm = aview(24576, [128, 2, 8, 128], BF16)
    wvu_b = aview(16384, [128, 4, E], BF16)
    wo_b = aview(0, [128, 8, E], BF16)
    p_sb = aview(16384, [128, 2048], BF16)
    pt_sb = aview(20480, [128, 2048], BF16)
    atok = aview(24576, [128, 1024], BF16)
    afm = aview(26624, [128, 8, 128], BF16)
    dwork = aview(28672, [128, 16], F32)
    rope_tmp = aview(35328, [128, 128], F32)

    lnfm = aview(16384, [128, 2, 8, 128], BF16)
    lnfm = None

    lnfm_b0 = sb_b0[:, 0:2048].rearrange("p (r c q) -> p r c q", r=2, c=8)

    def lnfm_col(t, k):
        return lnfm_b0[:, t % 2, k, :]

    def lnfm_dst(t):
        return lnfm_b0[:, t % 2, :, :]

    xn_fm = sb_b0[:].rearrange("p (c t) -> p c t", c=8)
    k_rot = sb_b0[:].rearrange("p (c t) -> p c t", c=8)
    xn_tok = sb_b1[:, 0:16384].rearrange("p (a d) -> p a d", a=NT)
    v65 = sb_b1[:].rearrange("p (a d) -> p a d", a=NT)
    wq_asb = sb_b1[:, 0:8192].rearrange("p (k m) -> p k m", k=8)
    wdn_b = sb_b1[:, 8192:12288].rearrange("p (k m) -> p k m", k=8)
    wku_b = sb_b1[:, 12288:16384].rearrange("p (k m) -> p k m", k=4)
    wq_fsb = sb_b2[:].rearrange("p (k m) -> p k m", k=8)
    gbuf = sb_b2[:].rearrange("p (b d) -> p b d", b=NBUF)
    q_rot = sb_b2[:].rearrange("p (c t) -> p c t", c=8)

    P = Plan(nc, stack)
    state = {}

    v_x = P.step("sync", "ld", lambda e: e.dma_start(
        out=sb_x[:], in_=x_in.rearrange("(a p) d -> p a d", p=128)), 16)
    v_id = P.step("sync", "ld", lambda e: e.dma_start(out=sb_id[:], in_=id_d), 16)
    v_swap = P.step("sync", "ld", lambda e: e.dma_start(out=sb_swap[:], in_=swap_d), 16)
    v_iota = P.step("sync", "ld", lambda e: e.dma_start(out=sb_iota[:], in_=iota_d), 16)

    BN_G = aview(36352, [128, 16], F32)
    BN_B = aview(36416, [128, 16], F32)
    BN_SC = aview(36480, [128, 16], F32)
    BN_BI = aview(36544, [128, 16], F32)
    BN_T = aview(36608, [128, 16], F32)
    BN_T2 = aview(36672, [128, 16], F32)
    P.step("sync", "ld", lambda e: e.dma_start(out=BN_G[:], in_=bng_d), 16)
    v_bn = P.step("sync", "ld", lambda e: e.dma_start(out=BN_B[:], in_=bnb_d), 16)

    P.run("vector", lambda e: e.memset(C_M32, -32.0))
    P.run("vector", lambda e: e.memset(C_MHI8, 0xFFFFFF00))
    P.run("vector", lambda e: e.memset(C_MHI16, 0xFFFF0000))
    v_msk = P.step("vector", "vv", lambda e: e.memset(C_MLO16, 0x0000FFFF))

    CONV = 512
    RP = NEXP // CONV
    conv_w = [0] * CONV

    def emit_conv():
        base_w = P.cnt["cw"]
        base_r = P.cnt["gd"]
        conv_r = [0] * CONV
        for i in range(CONV):
            conv_w[i] = base_w + 16 * (i + 1)
            conv_r[i] = base_r + 16 * (i + 1)

        def reads(lo, hi):
            for i in range(lo, hi):
                if i >= 2:
                    P.wait("gpsimd", "cw", conv_w[i - 2])
                vr = P.step("gpsimd", "gd", lambda e, i=i: e.dma_start(
                    out=sb_conv[:, i % 2, :],
                    in_=fused_d[i * RP:(i + 1) * RP, :]), 16)
                assert vr == conv_r[i], (vr, conv_r[i])
        for i in range(CONV):
            P.wait("sync", "gd", conv_r[i])
            vw = P.step("sync", "cw", lambda e, i=i: e.dma_start(
                out=fusedb[i * RP:(i + 1) * RP, :],
                in_=sb_conv[:, i % 2, :]), 16)
            assert vw == conv_w[i], (vw, conv_w[i])
        return reads

    def emit_ln_stats(x_ready_vv):
        for t in range(NT):
            P.wait("vector", "ld", v_x)
            P.wait("vector", "vv", x_ready_vv)
            P.step("vector", "vv", lambda e, t=t: e.reduce_sum(
                out=LN_SUM[:, t:t + 1], in_=sb_x[:, t, :], axis=AX.X))
            P.wait("scalar", "ld", v_x)
            P.wait("scalar", "vv", x_ready_vv)
            P.step("scalar", "ss", lambda e, t=t: e.activation(
                out=sq_scr[:], in_=sb_x[:, t, :], func=AF.Square,
                accum_out=LN_SQ[:, t:t + 1]))
        v_sq = P.cnt["ss"]
        P.wait("vector", "ss", v_sq)
        P.drain("vector")
        P.step("vector", "vv", lambda e: e.tensor_scalar(
            out=LN_SUM[:], in0=LN_SUM[:], scalar1=1.0 / E, scalar2=None,
            op0=ALU.mult))
        P.step("vector", "vv", lambda e: e.tensor_scalar(
            out=LN_SQ[:], in0=LN_SQ[:], scalar1=1.0 / E, scalar2=None,
            op0=ALU.mult))
        P.drain("vector")
        P.step("vector", "vv", lambda e: e.tensor_tensor(
            out=LN_SC[:], in0=LN_SUM[:], in1=LN_SUM[:], op=ALU.mult))
        P.drain("vector")
        P.step("vector", "vv", lambda e: e.tensor_tensor(
            out=LN_SC[:], in0=LN_SQ[:], in1=LN_SC[:], op=ALU.subtract))
        P.drain("vector")
        v_var = P.step("vector", "vv", lambda e: e.tensor_scalar(
            out=LN_SC[:], in0=LN_SC[:], scalar1=1e-5, scalar2=None, op0=ALU.add))
        P.wait("scalar", "vv", v_var)
        v_sqr = P.step("scalar", "ss", lambda e: e.sqrt(LN_B2[:], LN_SC[:]))
        P.wait("vector", "ss", v_sqr)
        P.step("vector", "vv", lambda e: e.reciprocal(LN_SC[:], LN_B2[:]))
        P.drain("vector")
        v_ret = P.step("vector", "vv", lambda e: e.tensor_tensor(
            out=LN_B2[:], in0=LN_SUM[:], in1=LN_SC[:], op=ALU.mult))
        P.drain("vector")
        return v_ret

    def emit_ln_apply(t, out_ap, v_stats, extra_vv=0, strong=False):
        P.wait("vector", "vv", max(v_stats, extra_vv))
        v = P.step("vector", "vv", lambda e, t=t, out_ap=out_ap: e.tensor_scalar(
            out=out_ap, in0=sb_x[:, t, :],
            scalar1=LN_SC[:, t:t + 1], scalar2=LN_B2[:, t:t + 1],
            op0=ALU.mult, op1=ALU.subtract))
        if strong:
            P.drain("vector")
            P.step("aa" if False else "vector", "vv", lambda e: e.engine_nop())
            v = P.step("vector", "aa", lambda e: e.engine_nop())
        return v

    def emit_transpose_tile(src_ap, dst_ap, bk, waits_vv=()):
        pt8 = psum[bk][:].bitcast(BF16).rearrange("p (c q) -> p c q", c=8)
        for v in waits_vv:
            P.wait("tensor", "vv", v)
        P.wait("tensor", "ld", v_id)
        for c in range(8):
            P.step("tensor", "tt", lambda e, c=c, src_ap=src_ap, pt8=pt8:
                   e.transpose(pt8[:, c, :], src_ap[:, c * 128:(c + 1) * 128],
                               sb_id[:]))
        v_tr = P.cnt["tt"]
        P.wait("vector", "tt", v_tr)
        return P.step("vector", "vv", lambda e, pt8=pt8, dst_ap=dst_ap:
                      e.tensor_copy(out=dst_ap, in_=pt8))

    def emit_ffn(app, x_ready_vv):
        v_lnst = emit_ln_stats(x_ready_vv)
        tr_ev = [0, 0]
        for t in range(NT):
            v_a = emit_ln_apply(t, xn_tok[:, t, :], v_lnst)
            tr_ev[t % 2] = emit_transpose_tile(
                xn_tok[:, t, :], xn_fm[:, :, t * 128:(t + 1) * 128],
                4 + (t % 2), waits_vv=(v_a, tr_ev[t % 2]))
        v_xn = P.cnt["vv"]

        P.wait("tensor", "gd", state["v_wqf"])
        bank_free = [0, 0]
        v_comb = 0
        for m in range(16):
            for tb in range(4):
                bk = tb % 2
                P.wait("tensor", "vv", max(v_xn, bank_free[bk]))
                for k in range(8):
                    P.step("tensor", "tt", lambda e, m=m, tb=tb, k=k, bk=bk:
                           e.matmul(psum[bk][:],
                                    wq_fsb[:, k, m * 128:(m + 1) * 128],
                                    xn_fm[:, k, tb * 512:(tb + 1) * 512],
                                    start=(k == 0), stop=(k == 7)))
                v_mm = P.cnt["tt"]
                P.wait("scalar", "tt", v_mm)
                P.wait("scalar", "vv", v_comb)
                v_s = P.step("scalar", "ss", lambda e, tb=tb, bk=bk: e.activation(
                    out=sq_scr[:, 0:512], in_=psum[bk][:], func=AF.Square,
                    accum_out=LN_T4B[:, tb:tb + 1]))
                P.wait("vector", "tt", v_mm)
                P.wait("vector", "ss", v_s)
                bank_free[bk] = P.step("vector", "vv", lambda e, tb=tb, bk=bk:
                                       e.reduce_sum(out=LN_T4[:, tb:tb + 1],
                                                    in_=psum[bk][:], axis=AX.X))
            P.drain("vector")
            P.step("vector", "vv", lambda e, m=m: e.reduce_sum(
                out=ST_SUM[:, m:m + 1], in_=LN_T4[:], axis=AX.X))
            v_comb = P.step("vector", "vv", lambda e, m=m: e.reduce_sum(
                out=ST_SQ[:, m:m + 1], in_=LN_T4B[:], axis=AX.X))
            bank_free[1] = v_comb
        v_stats = P.cnt["vv"]

        P.wait("scalar", "vv", v_stats)
        v_ccw = P.step("scalar", "sd2", lambda e: e.dma_start(out=ccin, in_=ST_ALL), 16)
        if cc_pre[0]:
            v_cc = cc_pre[0]
            assert v_ccw == cc_pre[1], (v_ccw, cc_pre)
            cc_pre[0] = 0
        else:
            P.wait("gpsimd", "sd2", v_ccw)
            v_cc = P.step("gpsimd", "cc", lambda e: e.collective_compute(
                "AllReduce", ALU.add, replica_groups=[list(range(8))],
                ins=[ccin], outs=[ccout]))
        P.wait("scalar", "cc", v_cc)
        v_ccr = P.step("scalar", "sd2", lambda e: e.dma_start(
            out=ST_ALL, in_=ccout), 16)

        P.wait("vector", "ld", v_bn)
        P.wait("vector", "sd2", v_ccr)
        P.drain("vector")
        P.step("vector", "vv", lambda e: e.tensor_scalar(
            out=ST_SUM[:], in0=ST_SUM[:], scalar1=1.0 / (8 * T), scalar2=None,
            op0=ALU.mult))
        P.step("vector", "vv", lambda e: e.tensor_scalar(
            out=ST_SQ[:], in0=ST_SQ[:], scalar1=1.0 / (8 * T), scalar2=None,
            op0=ALU.mult))
        P.drain("vector")
        P.step("vector", "vv", lambda e: e.tensor_tensor(
            out=BN_T[:], in0=ST_SUM[:], in1=ST_SUM[:], op=ALU.mult))
        P.drain("vector")
        P.step("vector", "vv", lambda e: e.tensor_tensor(
            out=BN_T[:], in0=ST_SQ[:], in1=BN_T[:], op=ALU.subtract))
        P.drain("vector")
        v_v2 = P.step("vector", "vv", lambda e: e.tensor_scalar(
            out=BN_T[:], in0=BN_T[:], scalar1=1e-5, scalar2=None, op0=ALU.add))
        P.wait("scalar", "vv", v_v2)
        v_s2 = P.step("scalar", "ss", lambda e: e.sqrt(BN_T2[:], BN_T[:]))
        P.wait("vector", "ss", v_s2)
        P.step("vector", "vv", lambda e: e.reciprocal(BN_T[:], BN_T2[:]))
        P.drain("vector")
        P.step("vector", "vv", lambda e: e.tensor_tensor(
            out=BN_SC[:], in0=BN_G[:], in1=BN_T[:], op=ALU.mult))
        P.drain("vector")
        P.step("vector", "vv", lambda e: e.tensor_tensor(
            out=BN_T[:], in0=ST_SUM[:], in1=BN_SC[:], op=ALU.mult))
        P.drain("vector")
        v_bnc = P.step("vector", "vv", lambda e: e.tensor_tensor(
            out=BN_BI[:], in0=BN_B[:], in1=BN_T[:], op=ALU.subtract))
        P.drain("vector")

        P.wait("tensor", "gd", state["v_keysT"])
        sim_mm_last = [0, 0]
        sim_ev = [0, 0]
        for c in range(16):
            cb = c % 2
            qn_write_v = []
            for tb in range(4):
                bk = tb % 2
                P.wait("tensor", "vv", max(v_bnc, bank_free[bk]))
                for k in range(8):
                    P.step("tensor", "tt", lambda e, c=c, tb=tb, k=k, bk=bk:
                           e.matmul(psum[bk][:],
                                    wq_fsb[:, k, c * 128:(c + 1) * 128],
                                    xn_fm[:, k, tb * 512:(tb + 1) * 512],
                                    start=(k == 0), stop=(k == 7)))
                v_mm = P.cnt["tt"]
                P.wait("vector", "tt", max(v_mm, sim_mm_last[cb]))
                bank_free[bk] = P.step("vector", "vv",
                                       lambda e, c=c, tb=tb, cb=cb, bk=bk:
                                       e.tensor_scalar(
                    out=qn2[:, cb, tb * 512:(tb + 1) * 512], in0=psum[bk][:],
                    scalar1=BN_SC[:, c:c + 1], scalar2=BN_BI[:, c:c + 1],
                    op0=ALU.mult, op1=ALU.add))
            v_qn = P.cnt["vv"]
            for t in range(NT):
                sb2 = 2 + (t % 2)
                P.wait("tensor", "vv", max(v_qn, sim_ev[t % 2]))
                v_mm = P.step("tensor", "tt", lambda e, c=c, t=t, cb=cb, sb2=sb2:
                              e.matmul(psum[sb2][:, 0:256],
                                       qn2[:, cb, t * 128:(t + 1) * 128],
                                       keysT_b[:, c, :], start=True, stop=True))
                P.wait("vector", "tt", v_mm)
                P.wait("vector", "ld", v_iota)
                P.step("vector", "vv", lambda e, t=t, sb2=sb2:
                       e.scalar_tensor_tensor(
                    out=simpay[:, t % 2, :].bitcast(U32),
                    in0=psum[sb2][:, 0:256].bitcast(U32),
                    scalar=C_MHI8, in1=sb_iota[:],
                    op0=ALU.bitwise_and, op1=ALU.bitwise_or))
                P.drain("vector")
                sim_ev[t % 2] = P.step("vector", "vv", lambda e, c=c, t=t: e.max(
                    out=c8_all[:, t, c, :], in_=simpay[:, t % 2, :]))
            sim_mm_last[cb] = P.cnt["tt"]

        for t in range(NT):
            c8u = c8_all[:, t, :, :].bitcast(U32)
            P.step("vector", "vv", lambda e, c8u=c8u: e.tensor_scalar(
                out=sxw[:].rearrange("p h j -> p (h j)").bitcast(U32),
                in0=c8u[:, 0:8, :].rearrange("p h j -> p (h j)"),
                scalar1=C_MHI8, scalar2=None, op0=ALU.bitwise_and))
            P.step("vector", "vv", lambda e, c8u=c8u: e.tensor_scalar(
                out=syw[:].rearrange("p h j -> p (h j)").bitcast(U32),
                in0=c8u[:, 8:16, :].rearrange("p h j -> p (h j)"),
                scalar1=C_MHI8, scalar2=None, op0=ALU.bitwise_and))
            P.step("vector", "vv", lambda e, c8u=c8u: e.tensor_scalar(
                out=ixw[:].rearrange("p h j -> p (h j)"),
                in0=c8u[:, 0:8, :].rearrange("p h j -> p (h j)"),
                scalar1=24, scalar2=16,
                op0=ALU.logical_shift_left, op1=ALU.logical_shift_right))
            P.step("vector", "vv", lambda e, c8u=c8u: e.tensor_scalar(
                out=iyw[:].rearrange("p h j -> p (h j)"),
                in0=c8u[:, 8:16, :].rearrange("p h j -> p (h j)"),
                scalar1=24, scalar2=24,
                op0=ALU.logical_shift_left, op1=ALU.logical_shift_right))
            P.drain("vector")
            P.step("vector", "vv", lambda e: e.tensor_tensor(
                out=zwork[:],
                in0=sxw[:].unsqueeze(3).to_broadcast([128, 8, 8, 8]),
                in1=syw[:].unsqueeze(2).to_broadcast([128, 8, 8, 8]),
                op=ALU.add))
            P.drain("vector")
            P.step("vector", "vv", lambda e: e.scalar_tensor_tensor(
                out=zidx[:],
                in0=zwork[:].rearrange("p h a b -> p (h a b)").bitcast(U32)
                    .rearrange("p (h a b) -> p h a b", h=8, a=8),
                scalar=C_MHI16,
                in1=ixw[:].unsqueeze(3).to_broadcast([128, 8, 8, 8]),
                op0=ALU.bitwise_and, op1=ALU.bitwise_or))
            P.drain("vector")
            P.step("vector", "vv", lambda e: e.tensor_tensor(
                out=zidx[:], in0=zidx[:],
                in1=iyw[:].unsqueeze(2).to_broadcast([128, 8, 8, 8]),
                op=ALU.bitwise_or))
            P.drain("vector")
            for h in range(8):
                P.step("vector", "vv", lambda e, h=h: e.max(
                    out=f8w[:, h * 8:(h + 1) * 8],
                    in_=zidx[:, h, :, :].rearrange("p a b -> p (a b)")
                    .bitcast(F32)))
                P.step("vector", "vv", lambda e, h=h: e.max(
                    out=s8w[:, h, :],
                    in_=zwork[:, h, :, :].rearrange("p a b -> p (a b)")))
            P.wait("vector", "vv", v_msk)
            P.drain("vector")
            P.step("vector", "vv", lambda e, t=t: e.tensor_scalar(
                out=idx_all[:, t, :], in0=f8w[:].bitcast(U32),
                scalar1=C_MLO16, scalar2=None, op0=ALU.bitwise_and))
            P.step("vector", "vv", lambda e: e.tensor_tensor(
                out=e8w[:], in0=s8w[:],
                in1=s8w[:, :, 0:1].to_broadcast([128, 8, 8]), op=ALU.subtract))
            v_e = P.cnt["vv"]
            P.wait("scalar", "vv", v_e)
            v_x8 = P.step("scalar", "ss", lambda e: e.activation(
                out=e8w[:], in_=e8w[:], func=AF.Exp))
            P.wait("vector", "ss", v_x8)
            P.step("vector", "vv", lambda e: e.tensor_reduce(
                out=r8w[:], in_=e8w[:], axis=AX.X, op=ALU.add))
            P.drain("vector")
            P.step("vector", "vv", lambda e: e.reciprocal(r8w[:], r8w[:]))
            P.drain("vector")
            P.step("vector", "vv", lambda e, t=t: e.tensor_tensor(
                out=smax_all[:, t, :].rearrange("p (h j) -> p h j", h=8),
                in0=e8w[:], in1=r8w[:].unsqueeze(2).to_broadcast([128, 8, 8]),
                op=ALU.mult))
        v_topk = P.cnt["vv"]

        P.wait("gpsimd", "cw", conv_w[CONV - 1])
        P.wait("gpsimd", "vv", v_topk)
        up_mm = {}
        res_add = [0, 0]
        i = 0
        for t in range(NT):
            u0, u1 = (t % 2) * 2, (t % 2) * 2 + 1
            for hj in range(64):
                b = i % NBUF
                if i >= NBUF:
                    P.wait("gpsimd", "tt", up_mm[i - NBUF])
                v_g = P.step("gpsimd", "gd", lambda e, t=t, hj=hj, b=b:
                             e.indirect_dma_start(
                                 gbuf[:, b, :], None, fusedb,
                                 IndirectOffsetOnAxis(
                                     idx_all[:, t, hj:hj + 1].bitcast(I32), 0)), 16)
                P.wait("vector", "gd", v_g)
                v_d = P.step("vector", "vv", lambda e, t=t, hj=hj, b=b:
                             e.scalar_tensor_tensor(
                                 out=sq_scr[:, 0:1024], in0=gbuf[:, b, 0:1024],
                                 scalar=1.0, in1=xn_tok[:, t, :],
                                 op0=ALU.mult, op1=ALU.mult,
                                 accum_out=hdot[:, hj:hj + 1]))
                P.wait("scalar", "vv", v_d)
                v_gl = P.step("scalar", "ss", lambda e, hj=hj: e.activation(
                    out=gelu_c[:, hj:hj + 1], in_=hdot[:, hj:hj + 1],
                    func=AF.Gelu_apprx_tanh))
                P.wait("vector", "ss", v_gl)
                v_sc = P.step("vector", "vv", lambda e, t=t, hj=hj, b=b:
                              e.tensor_scalar(
                                  out=gbuf[:, b, 1024:2048],
                                  in0=gbuf[:, b, 1024:2048],
                                  scalar1=gelu_c[:, hj:hj + 1],
                                  scalar2=smax_all[:, t, hj:hj + 1],
                                  op0=ALU.mult, op1=ALU.mult))
                P.wait("tensor", "vv", v_sc)
                if hj == 0:
                    P.wait("tensor", "vv", res_add[t % 2])
                    P.wait("tensor", "ld", v_id)
                P.step("tensor", "tt", lambda e, b=b, hj=hj, u0=u0: e.matmul(
                    psum[u0][:], sb_id[:], gbuf[:, b, 1024:1536],
                    start=(hj == 0), stop=(hj == 63)))
                up_mm[i] = P.step("tensor", "tt", lambda e, b=b, hj=hj, u1=u1:
                                  e.matmul(psum[u1][:], sb_id[:],
                                           gbuf[:, b, 1536:2048],
                                           start=(hj == 0), stop=(hj == 63)))
                i += 1
            P.wait("vector", "tt", up_mm[i - 1])
            P.step("vector", "vv", lambda e, t=t, u0=u0: e.scalar_tensor_tensor(
                out=sb_x[:, t, 0:512], in0=psum[u0][:], scalar=0.5,
                in1=sb_x[:, t, 0:512], op0=ALU.mult, op1=ALU.add))
            res_add[t % 2] = P.step("vector", "vv", lambda e, t=t, u1=u1:
                                    e.scalar_tensor_tensor(
                out=sb_x[:, t, 512:1024], in0=psum[u1][:], scalar=0.5,
                in1=sb_x[:, t, 512:1024], op0=ALU.mult, op1=ALU.add))
        return P.cnt["vv"], P.cnt["tt"]

    def emit_rope_group(t, grp, pb, dst, v_qm, swb, last_rope):
        for mi in range(4):
            c = grp * 4 + mi
            P.wait("vector", "tt", v_qm)
            if last_rope["sw"]:
                P.wait("vector", "tt", last_rope["sw"])
            v_ev = P.step("vector", "vv", lambda e, mi=mi, pb=pb, c=c:
                          e.tensor_copy(out=qraw[:, c % 2, :],
                                        in_=psum[pb][:, mi * 128:(mi + 1) * 128]))
            P.wait("tensor", "vv", max(v_ev, last_rope["tt2"]))
            P.wait("tensor", "ld", v_swap)
            v_sw = P.step("tensor", "tt", lambda e, swb=swb, c=c: e.matmul(
                psum[swb][:, 0:128], sb_swap[:], qraw[:, c % 2, :],
                start=True, stop=True))
            last_rope["sw"] = v_sw
            P.wait("vector", "tt", v_sw)
            P.wait("vector", "ld", state.get("v_rope_ld", 0))
            P.drain("vector")
            P.step("vector", "vv", lambda e, t=t, swb=swb: e.tensor_tensor(
                out=rope_tmp[:], in0=psum[swb][:, 0:128],
                in1=stab[:, t * 128:(t + 1) * 128], op=ALU.mult))
            P.step("vector", "vv", lambda e, t=t, c=c, dst=dst: e.tensor_tensor(
                out=dst[:, c, t * 128:(t + 1) * 128], in0=qraw[:, c % 2, :],
                in1=ctab[:, t * 128:(t + 1) * 128], op=ALU.mult))
            P.drain("vector")
            last_rope["tt2"] = P.step("vector", "vv",
                                      lambda e, t=t, c=c, dst=dst: e.tensor_tensor(
                out=dst[:, c, t * 128:(t + 1) * 128],
                in0=dst[:, c, t * 128:(t + 1) * 128],
                in1=rope_tmp[:], op=ALU.add))
        return last_rope["tt2"]

    def emit_attention(x_ready_vv, att_stop=None, att_tiles=NT):
        v_lnst = emit_ln_stats(x_ready_vv)
        P.wait("tensor", "gd", state["v_attw1"])
        lat_ev = [0, 0]
        fm_ev = [0, 0]
        rq = {"sw": 0, "tt2": 0}
        for t in range(att_tiles):
            v_a = emit_ln_apply(t, ln2t[:], v_lnst,
                                extra_vv=max(fm_ev[0], fm_ev[1]), strong=True)
            P.wait("tensor", "aa", v_a)
            v_fm = emit_transpose_tile(ln2t[:], lnfm_dst(t), 4 + (t % 2),
                                       waits_vv=(fm_ev[t % 2],))
            fm_ev[t % 2] = v_fm
            lb = t % 2
            P.wait("tensor", "vv", max(v_fm, lat_ev[lb]))
            for ml in range(4):
                for k in range(8):
                    P.step("tensor", "tt", lambda e, t=t, ml=ml, k=k, lb=lb:
                           e.matmul(psum[lb][:, ml * 128:(ml + 1) * 128],
                                    wdn_b[:, k, ml * 128:(ml + 1) * 128],
                                    lnfm_col(t, k),
                                    start=(k == 0), stop=(k == 7)))
            v_lm = P.cnt["tt"]
            P.wait("vector", "tt", v_lm)
            lat_ev[lb] = P.step("vector", "vv", lambda e, t=t, lb=lb:
                                e.tensor_copy(
                out=lat_fm[:, :, t * 128:(t + 1) * 128],
                in_=psum[lb][:].rearrange("p (c q) -> p c q", c=4)))
            qb = 2 + (t % 2)
            for grp in range(2):
                P.wait("tensor", "vv", max(v_fm, rq["tt2"]))
                for m in range(grp * 4, grp * 4 + 4):
                    for k in range(8):
                        P.step("tensor", "tt",
                               lambda e, t=t, m=m, k=k, qb=qb: e.matmul(
                            psum[qb][:, (m % 4) * 128:(m % 4 + 1) * 128],
                            wq_asb[:, k, m * 128:(m + 1) * 128],
                            lnfm_col(t, k), start=(k == 0), stop=(k == 7)))
                emit_rope_group(t, grp, qb, q_rot, P.cnt["tt"], 6 + (t % 2), rq)
        v_loop1 = P.cnt["vv"]
        if att_stop == 1:
            return v_loop1, P.cnt["tt"]

        P.wait("tensor", "gd", state["v_attw2"])
        rk = {"sw": 0, "tt2": 0}
        for t in range(NT):
            kb = t % 2
            for grp in range(2):
                P.wait("tensor", "vv",
                       max(v_loop1 if t == 0 else 0, lat_ev[t % 2], rk["tt2"]))
                for m in range(grp * 4, grp * 4 + 4):
                    for k4 in range(4):
                        P.step("tensor", "tt",
                               lambda e, t=t, m=m, k4=k4, kb=kb: e.matmul(
                            psum[kb][:, (m % 4) * 128:(m % 4 + 1) * 128],
                            wku_b[:, k4, m * 128:(m + 1) * 128],
                            lat_fm[:, k4, t * 128:(t + 1) * 128],
                            start=(k4 == 0), stop=(k4 == 3)))
                emit_rope_group(t, grp, kb, k_rot, P.cnt["tt"], 2 + (t % 2), rk)
        v_loop2 = P.cnt["vv"]
        P.wait("gpsimd", "vv", v_loop2)
        state["v_attw3"] = P.step("gpsimd", "gd", lambda e: e.dma_start(
            out=wvu_b[:], in_=wvu_d.rearrange("(k p) m -> p k m", p=128)), 16)
        if att_stop == 2:
            return v_loop2, P.cnt["tt"]

        P.wait("vector", "vv", v_loop2)
        v_ones = P.step("vector", "vv", lambda e: e.memset(
            v65[:].rearrange("p a (g u) -> p a g u", g=16)[:, :, :, 64:65], 1.0))
        P.wait("tensor", "gd", state["v_attw3"])
        vv_ev = [v_ones, v_ones]
        for t in range(NT):
            vb = t % 2
            for nb in range(2):
                P.wait("tensor", "vv", max(v_ones, vv_ev[vb]))
                for k4 in range(4):
                    P.step("tensor", "tt", lambda e, t=t, nb=nb, k4=k4, vb=vb:
                           e.matmul(psum[vb][:],
                                    lat_fm[:, k4, t * 128:(t + 1) * 128],
                                    wvu_b[:, k4, nb * 512:(nb + 1) * 512],
                                    start=(k4 == 0), stop=(k4 == 3)))
                v_vm = P.cnt["tt"]
                P.wait("vector", "tt", v_vm)
                vv_ev[vb] = P.step("vector", "vv", lambda e, t=t, nb=nb, vb=vb:
                                   e.tensor_copy(
                    out=v65[:, t, :].rearrange("p (g u) -> p g u", g=16)
                        [:, nb * 8:(nb + 1) * 8, 0:64],
                    in_=psum[vb][:].rearrange("p (g u) -> p g u", g=8)))
        v_loop3 = P.cnt["vv"]
        P.wait("gpsimd", "vv", v_loop3)
        state["v_wo"] = P.step("gpsimd", "gd", lambda e: e.dma_start(
            out=wo_b[:], in_=wo_d.rearrange("(k p) m -> p k m", p=128)), 16)
        if att_stop == 3:
            return v_loop3, P.cnt["tt"]

        P.wait("tensor", "gd", state["v_wo"])
        s_bank_exp = [0, 0]
        pt_bank_ev = [0, 0]
        pv_done = [0, 0]
        pt_cons = 0
        p_cons = 0
        oadd = [0, 0]
        for qt in range(NT):
            kr = (qt // 2 + 1) * 256
            nk128 = kr // 128
            for h2 in range(HA):
                h = h2
                hp = (h % 2) * 64
                ch = h // 2
                pvb = 6 + (h % 2)
                csizes = []
                off = 0
                while off < kr:
                    w = min(512, kr - off)
                    csizes.append((off, w))
                    off += w
                exp_v = []
                for ci, (off, w) in enumerate(csizes):
                    sbk = ci % 2
                    P.wait("tensor", "vv", v_loop3)
                    P.wait("tensor", "ss", s_bank_exp[sbk])
                    P.step("tensor", "tt", lambda e, qt=qt, hp=hp, ch=ch,
                           off=off, w=w, sbk=sbk: e.matmul(
                        psum[sbk][:, 0:w],
                        q_rot[hp:hp + 64, ch, qt * 128:(qt + 1) * 128],
                        k_rot[hp:hp + 64, ch, off:off + w],
                        start=True, stop=True))
                    v_smm = P.cnt["tt"]
                    P.wait("scalar", "tt", max(v_smm, p_cons))
                    s_bank_exp[sbk] = P.step("scalar", "ss",
                                             lambda e, off=off, w=w, sbk=sbk:
                                             e.activation(
                        out=p_sb[:, off:off + w], in_=psum[sbk][:, 0:w],
                        func=AF.Exp, bias=C_M32, scale=0.125))
                    exp_v.append(s_bank_exp[sbk])
                for g in range(0, nk128, 8):
                    gi = (g // 8) % 2
                    gbk = 4 + gi
                    pt8 = psum[gbk][:].bitcast(BF16).rearrange(
                        "p (c q) -> p c q", c=8)
                    n_in_g = min(8, nk128 - g)
                    P.wait("tensor", "vv", pt_bank_ev[gi])
                    P.wait("tensor", "ld", v_id)
                    for c2 in range(n_in_g):
                        ck = g + c2
                        P.wait("tensor", "ss", exp_v[(ck * 128) // 512])
                        P.step("tensor", "tt", lambda e, ck=ck, c2=c2, pt8=pt8:
                               e.transpose(pt8[:, c2, :],
                                           p_sb[:, ck * 128:(ck + 1) * 128],
                                           sb_id[:]))
                    v_tr = P.cnt["tt"]
                    p_cons = v_tr
                    P.wait("vector", "tt", max(v_tr, pt_cons))
                    pt_bank_ev[gi] = P.step(
                        "vector", "vv",
                        lambda e, g=g, n_in_g=n_in_g, pt8=pt8: e.tensor_copy(
                            out=pt_sb[:, g * 128:(g + n_in_g) * 128],
                            in_=pt8.rearrange("p c q -> p (c q)")
                            [:, 0:n_in_g * 128]))
                v_ptall = P.cnt["vv"]
                P.wait("tensor", "vv", max(v_ptall, pv_done[h % 2], v_ones))
                for ck in range(nk128):
                    P.step("tensor", "tt", lambda e, qt=qt, h=h, ck=ck, pvb=pvb:
                           e.matmul(psum[pvb][:, 0:65],
                                    pt_sb[:, ck * 128:(ck + 1) * 128],
                                    v65[:, ck, h * 65:(h + 1) * 65],
                                    start=(ck == 0), stop=(ck == nk128 - 1)))
                v_pv = P.cnt["tt"]
                pt_cons = v_pv
                P.wait("vector", "tt", v_pv)
                P.step("vector", "vv", lambda e, pvb=pvb: e.reciprocal(
                    dwork[:, 0:1], psum[pvb][:, 64:65]))
                P.drain("vector")
                pv_done[h % 2] = P.step("vector", "vv", lambda e, h=h, pvb=pvb:
                                        e.tensor_scalar(
                    out=atok[:, h * 64:(h + 1) * 64], in0=psum[pvb][:, 0:64],
                    scalar1=dwork[:, 0:1], scalar2=None, op0=ALU.mult))
            v_atok = P.cnt["vv"]
            v_afm = emit_transpose_tile(
                atok[:], afm[:], 4 + (qt % 2),
                waits_vv=(v_atok, pt_bank_ev[qt % 2]))
            pt_bank_ev[qt % 2] = v_afm
            for nb2 in range(2):
                P.wait("tensor", "vv", max(v_afm, oadd[qt % 2]))
                for f in range(8):
                    P.step("tensor", "tt", lambda e, nb2=nb2, f=f: e.matmul(
                        psum[2 + nb2][:], afm[:, f, :],
                        wo_b[:, f, nb2 * 512:(nb2 + 1) * 512],
                        start=(f == 0), stop=(f == 7)))
            v_om = P.cnt["tt"]
            P.wait("vector", "tt", v_om)
            P.step("vector", "vv", lambda e, qt=qt: e.scalar_tensor_tensor(
                out=sb_x[:, qt, 0:512], in0=psum[2][:], scalar=1.0,
                in1=sb_x[:, qt, 0:512], op0=ALU.mult, op1=ALU.add))
            oadd[qt % 2] = P.step("vector", "vv", lambda e, qt=qt:
                                  e.scalar_tensor_tensor(
                out=sb_x[:, qt, 512:1024], in0=psum[3][:], scalar=1.0,
                in1=sb_x[:, qt, 512:1024], op0=ALU.mult, op1=ALU.add))
        return P.cnt["vv"], P.cnt["tt"]

    state["v_wqf"] = P.step("gpsimd", "gd", lambda e: e.dma_start(
        out=wq_fsb[:], in_=wqf_d.rearrange("(k p) m -> p k m", p=128)), 16)
    state["v_keysT"] = P.step("gpsimd", "gd", lambda e: e.dma_start(
        out=keysT_b[:], in_=keysT_d.rearrange("c p k -> p c k")), 16)
    cc_pre = [0, 0]
    conv_reads = emit_conv()
    conv_reads(0, 170)
    if not skip_ffn1:
        exp_ccw = P.cnt["sd2"] + 16
        P.wait("gpsimd", "sd2", exp_ccw)
        cc_pre[0] = P.step("gpsimd", "cc", lambda e: e.collective_compute(
            "AllReduce", ALU.add, replica_groups=[list(range(8))],
            ins=[ccin], outs=[ccout]))
        cc_pre[1] = exp_ccw
    conv_reads(170, CONV)
    if skip_ffn1:
        v_ffn1_vv, v_ffn1_tt = 1, 0
    else:
        v_ffn1_vv, v_ffn1_tt = emit_ffn(0, 1)

    if stage >= 2:
        P.wait("gpsimd", "vv", v_ffn1_vv)
        P.step("gpsimd", "gd", lambda e: e.dma_start(
            out=wq_asb[:], in_=wq_a.rearrange("(k p) m -> p k m", p=128)), 16)
        state["v_attw1"] = P.step("gpsimd", "gd", lambda e: e.dma_start(
            out=wdn_b[:], in_=wdn_d.rearrange("(k p) m -> p k m", p=128)), 16)
        state["v_attw2"] = P.step("gpsimd", "gd", lambda e: e.dma_start(
            out=wku_b[:], in_=wku_d.rearrange("(k p) m -> p k m", p=128)), 16)
        P.wait("sync", "vv", v_ffn1_vv)
        P.step("sync", "ld", lambda e: e.dma_start(out=ctab[:], in_=ctab_d), 16)
        state["v_rope_ld"] = P.step("sync", "ld", lambda e: e.dma_start(
            out=stab[:], in_=stab_d), 16)
        v_att_vv, v_att_tt = emit_attention(v_ffn1_vv, att_stop, att_tiles)
    else:
        v_att_vv, v_att_tt = v_ffn1_vv, v_ffn1_tt

    if stage >= 3:
        P.wait("gpsimd", "vv", v_att_vv)
        P.wait("gpsimd", "tt", v_att_tt)
        state["v_wqf"] = P.step("gpsimd", "gd", lambda e: e.dma_start(
            out=wq_fsb[:], in_=wqf_d.rearrange("(k p) m -> p k m", p=128)), 16)
        state["v_keysT"] = P.step("gpsimd", "gd", lambda e: e.dma_start(
            out=keysT_b[:], in_=keysT_d.rearrange("c p k -> p c k")), 16)
        v_f2_vv, _ = emit_ffn(1, v_att_vv)
    else:
        v_f2_vv = v_att_vv

    P.wait("scalar", "vv", v_f2_vv)
    v_st = P.step("scalar", "sd2", lambda e: e.dma_start(
        out=y_out.rearrange("(a p) d -> p a d", p=128), in_=sb_x[:]), 16)
    if dbg:
        dbg_specs = [
            ("d_xntok", xn_tok[:], [128, NT, 1024], BF16),
            ("d_c8", c8_all[:], [128, NT, 16, 8], F32),
            ("d_idx", idx_all[:], [128, NT, 64], U32),
            ("d_smax", smax_all[:], [128, NT, 64], F32),
            ("d_bnsc", BN_SC[:], [128, 16], F32),
            ("d_bnbi", BN_BI[:], [128, 16], F32),
            ("d_hdot", hdot[:], [128, 64], F32),
            ("d_xnfm", xn_fm[:], [128, 8, 2048], BF16),
            ("d_gbuf", gbuf[:], [128, NBUF, 2048], BF16),
            ("d_lnsc", LN_SC[:], [128, 16], F32),
            ("d_lnb2", LN_B2[:], [128, 16], F32),
        ]
        if stage >= 2:
            dbg_specs += [
                ("d_qrot", q_rot[:], [128, 8, 2048], BF16),
                ("d_krot", k_rot[:], [128, 8, 2048], BF16),
                ("d_v65", v65[:], [128, NT, 1040], BF16),
                ("d_atok", atok[:], [128, 1024], BF16),
                ("d_ptsb", pt_sb[:], [128, 2048], BF16),
                ("d_b1", sb_b1[:], [128, 16640], BF16),
                ("d_b0p", sb_b0[:, 0:2048], [128, 2048], BF16),
                ("d_aux0", lat_fm[:], [128, 4, 2048], BF16),
            ]
        for nm, ap, shp, dt2 in dbg_specs:
            od = dtn(nm, shp, dt2, kind="ExternalOutput").ap()
            v_st = P.step("scalar", "sd2",
                          lambda e, od=od, ap=ap: e.dma_start(out=od, in_=ap), 16)
    P.wait("scalar", "sd2", v_st)

    with nc.Block() as block:
        @block.sync
        def _(eng):
            for fn in P.fns["sync"]:
                fn(eng)

        @block.scalar
        def _(eng):
            for fn in P.fns["scalar"]:
                fn(eng)

        @block.vector
        def _(eng):
            for fn in P.fns["vector"]:
                fn(eng)

        @block.tensor
        def _(eng):
            for fn in P.fns["tensor"]:
                fn(eng)

        @block.gpsimd
        def _(eng):
            for fn in P.fns["gpsimd"]:
                fn(eng)

    stack.close()
    return nc


def make_consts():
    id128 = np.eye(128, dtype=ml_dtypes.bfloat16)
    swap = np.zeros((128, 128), dtype=ml_dtypes.bfloat16)
    for i in range(0, 128, 2):
        swap[i, i + 1] = 1
        swap[i + 1, i] = 1
    iota = np.tile(np.arange(256, dtype=np.uint32), (128, 1))
    inv = 1.0 / 10000.0 ** (np.arange(0, DA, 2, dtype=np.float32) / DA)
    ang = np.outer(np.arange(T, dtype=np.float32), inv)
    cos = np.cos(ang).astype(np.float32)
    sin = np.sin(ang).astype(np.float32)
    ctab = np.zeros((128, T), np.float32)
    stab = np.zeros((128, T), np.float32)
    for p in range(128):
        r = (p % 64) // 2
        ctab[p] = cos[:, r]
        stab[p] = sin[:, r] * (1.0 if p % 2 == 1 else -1.0)
    return id128, swap, iota, ctab, stab


def make_inputs(L0_attn_x, L0_attn_wq, L0_attn_w_down, L0_attn_wk_up,
                L0_attn_wv_up, L0_attn_wo, L0_ffn_w_q, L0_ffn_bn_gamma,
                L0_ffn_bn_beta, L0_ffn_keys, L0_ffn_down_embed, L0_ffn_up_embed):
    id128, swap, iota, ctab, stab = make_consts()
    keysT = np.ascontiguousarray(
        np.transpose(np.asarray(L0_ffn_keys), (2, 0, 3, 1)).reshape(16, KD, NK))
    fused = np.concatenate(
        [np.asarray(L0_ffn_down_embed), np.asarray(L0_ffn_up_embed)], axis=1)
    common = dict(
        wq_a=np.asarray(L0_attn_wq), wdn=np.asarray(L0_attn_w_down),
        wku=np.asarray(L0_attn_wk_up), wvu=np.asarray(L0_attn_wv_up),
        wo_a=np.asarray(L0_attn_wo), wqf=np.asarray(L0_ffn_w_q),
        bng=np.ascontiguousarray(np.asarray(L0_ffn_bn_gamma).reshape(16, 128).T),
        bnb=np.ascontiguousarray(np.asarray(L0_ffn_bn_beta).reshape(16, 128).T),
        keysT=keysT, fused=np.ascontiguousarray(fused),
        id128=id128, swap128=swap, iota256=iota, ctab=ctab, stab=stab,
    )
    x = np.asarray(L0_attn_x)
    return [dict(common, x=np.ascontiguousarray(x[c])) for c in range(8)]


def kernel(**inputs):
    nc = build(stage=3)
    in_maps = make_inputs(**inputs)
    res = run_bass_kernel_spmd(nc, in_maps, core_ids=list(range(8)))
    return np.stack([res.results[c]["y"] for c in range(8)], axis=0)


# version 06 (final)
# speedup vs baseline: 1.0315x; 1.0132x over previous
import numpy as np
import ml_dtypes
from contextlib import ExitStack

import concourse.bass as bass
import concourse.mybir as mybir
from concourse.bass_utils import run_bass_kernel_spmd
from concourse.bass import IndirectOffsetOnAxis

F32 = mybir.dt.float32
BF16 = mybir.dt.bfloat16
U32 = mybir.dt.uint32
I32 = mybir.dt.int32
U8 = mybir.dt.uint8
AF = mybir.ActivationFunctionType
ALU = mybir.AluOpType
AX = mybir.AxisListType

T = 2048
NT = 16
E = 1024
HA, DA = 16, 64
H, KD, NK, TK = 8, 128, 256, 8
NEXP = NK * NK
FQ = 2048
NBUF = 8
ENGS = ("sync", "scalar", "vector", "tensor", "gpsimd")


class Plan:
    def __init__(self, nc, stack):
        self.nc = nc
        self.fns = {e: [] for e in ENGS}
        self.sems = {}
        self.cnt = {}
        for s in ("ld", "sd2", "gd", "vv", "ss", "tt", "cc", "aa", "cw"):
            self.sems[s] = stack.enter_context(nc.semaphore(s))
            self.cnt[s] = 0

    def emit(self, eng, fn):
        self.fns[eng].append(fn)

    def wait(self, eng, sem, val):
        if val is None or val <= 0:
            return
        h = self.sems[sem]
        self.emit(eng, lambda e, h=h, v=val: e.wait_ge(h, v))

    def step(self, eng, sem, fn, n=1):
        self.cnt[sem] += n
        h = self.sems[sem]
        self.emit(eng, lambda e, h=h, n=n, fn=fn: fn(e).then_inc(h, n))
        return self.cnt[sem]

    def run(self, eng, fn):
        self.emit(eng, fn)

    def drain(self, eng):
        self.emit(eng, lambda e: e.drain())


def build(stage=3, dbg=False, att_stop=None, skip_ffn1=False, att_tiles=NT):
    nc = bass.Bass("TRN2", target_bir_lowering=False)
    stack = ExitStack()

    dtn = nc.dram_tensor
    x_in = dtn("x", [T, E], F32, kind="ExternalInput").ap()
    y_out = dtn("y", [T, E], F32, kind="ExternalOutput").ap()
    wq_a = dtn("wq_a", [E, E], F32, kind="ExternalInput").ap()
    wdn_d = dtn("wdn", [E, 512], F32, kind="ExternalInput").ap()
    wku_d = dtn("wku", [512, E], F32, kind="ExternalInput").ap()
    wvu_d = dtn("wvu", [512, E], F32, kind="ExternalInput").ap()
    wo_d = dtn("wo_a", [E, E], F32, kind="ExternalInput").ap()
    wqf_d = dtn("wqf", [E, FQ], F32, kind="ExternalInput").ap()
    bng_d = dtn("bng", [128, 16], F32, kind="ExternalInput").ap()
    bnb_d = dtn("bnb", [128, 16], F32, kind="ExternalInput").ap()
    keysT_d = dtn("keysT", [16, KD, NK], F32, kind="ExternalInput").ap()
    fused_d = dtn("fused", [NEXP, 2048], F32, kind="ExternalInput").ap()
    id_d = dtn("id128", [128, 128], BF16, kind="ExternalInput").ap()
    iota_d = dtn("iota256", [128, 256], U32, kind="ExternalInput").ap()
    ctab_d = dtn("ctab", [128, T], F32, kind="ExternalInput").ap()
    stab_d = dtn("stab", [128, T], F32, kind="ExternalInput").ap()
    swap_d = dtn("swap128", [128, 128], BF16, kind="ExternalInput").ap()

    fusedb = dtn("fusedb", [NEXP, 2048], BF16, kind="Internal").ap()
    ccin = dtn("ccin", [128, 32], F32, kind="Internal").ap()
    ccout = dtn("ccout", [128, 32], F32, kind="Internal").ap()

    def sbuf(name, shape, dt):
        return stack.enter_context(nc.sbuf_tensor(name, shape, dt))

    sb_x = sbuf("sb_x", [128, NT, E], F32)
    sb_b0 = sbuf("sb_b0", [128, 16384], BF16)
    sb_b1 = sbuf("sb_b1", [128, 16640], BF16)
    sb_b2 = sbuf("sb_b2", [128, 16384], BF16)
    sb_aux = sbuf("sb_aux", [128, 36864], U8)
    sb_id = sbuf("sb_id", [128, 128], BF16)
    sb_swap = sbuf("sb_swap", [128, 128], BF16)
    sb_iota = sbuf("sb_iota", [128, 256], U32)
    sb_small = sbuf("sb_small", [128, 96], F32)
    sb_conv = sbuf("sb_conv", [128, 2, 2048], BF16)

    psum = [stack.enter_context(nc.psum_tensor(f"pb{i}", [128, 512], F32))
            for i in range(8)]

    smallu = sb_small[:].bitcast(U32)
    C_M32 = sb_small[:, 0:1]
    C_MHI8 = smallu[:, 1:2]
    C_MHI16 = smallu[:, 2:3]
    C_MLO16 = smallu[:, 3:4]
    LN_SUM = sb_small[:, 4:20]
    LN_SQ = sb_small[:, 20:36]
    LN_SC = sb_small[:, 36:52]
    LN_B2 = sb_small[:, 52:68]
    LN_T4 = sb_small[:, 68:72]
    LN_T4B = sb_small[:, 72:76]
    ST_SUM = sb_small[:, 4:20]
    ST_SQ = sb_small[:, 20:36]
    ST_ALL = sb_small[:, 4:36]

    def aview(off, shape, dt):
        n = 1
        for s in shape[1:]:
            n *= s
        nbytes = n * mybir.dt.size(dt)
        assert off + nbytes <= 36864, (off, shape, dt)
        ap = sb_aux[:, off:off + nbytes].bitcast(dt)
        if len(shape) == 2:
            return ap
        kw = {f"d{i}": shape[i] for i in range(1, len(shape) - 1)}
        spec = " ".join(f"d{i}" for i in range(1, len(shape)))
        return ap.rearrange(f"p ({spec}) -> p {spec}", **kw)

    keysT_b = aview(0, [128, 16, NK], BF16)
    zwork = aview(0, [128, 8, 8, 8], F32)
    zidx = aview(2048, [128, 8, 8, 8], U32)
    sxw = aview(4096, [128, 8, 8], F32)
    syw = aview(4352, [128, 8, 8], F32)
    ixw = aview(4608, [128, 8, 8], U32)
    iyw = aview(4864, [128, 8, 8], U32)
    f8w = aview(5120, [128, 64], F32)
    s8w = aview(5376, [128, 8, 8], F32)
    e8w = aview(5632, [128, 8, 8], F32)
    r8w = aview(5888, [128, 8], F32)
    hdot = aview(6144, [128, 64], F32)
    gelu_c = aview(6400, [128, 64], F32)
    c8_all = aview(8192, [128, NT, 16, 8], F32)
    idx_all = aview(16384, [128, NT, 64], U32)
    smax_all = aview(20480, [128, NT, 64], F32)
    qn2 = aview(24576, [128, 2, FQ], BF16)
    simpay = aview(32768, [128, 2, NK], F32)
    sq_scr = aview(32768, [128, 1024], BF16)

    lat_fm = aview(0, [128, 4, T], BF16)
    ctab = aview(16384, [128, T], F32)
    stab = aview(24576, [128, T], F32)
    ln2t = aview(32768, [128, 1024], BF16)
    qraw = aview(34816, [128, 2, 128], BF16)
    lnfm = aview(24576, [128, 2, 8, 128], BF16)
    wvu_b = aview(16384, [128, 4, E], BF16)
    wo_b = aview(0, [128, 8, E], BF16)
    p_sb = aview(16384, [128, 2048], BF16)
    pt_sb = aview(20480, [128, 2048], BF16)
    atok = aview(24576, [128, 1024], BF16)
    afm = aview(26624, [128, 8, 128], BF16)
    dwork = aview(28672, [128, 16], F32)
    rope_tmp = aview(35328, [128, 128], F32)

    lnfm = aview(16384, [128, 2, 8, 128], BF16)
    lnfm = None

    lnfm_b0 = sb_b0[:, 0:2048].rearrange("p (r c q) -> p r c q", r=2, c=8)

    def lnfm_col(t, k):
        return lnfm_b0[:, t % 2, k, :]

    def lnfm_dst(t):
        return lnfm_b0[:, t % 2, :, :]

    xn_fm = sb_b0[:].rearrange("p (c t) -> p c t", c=8)
    k_rot = sb_b0[:].rearrange("p (c t) -> p c t", c=8)
    xn_tok = sb_b1[:, 0:16384].rearrange("p (a d) -> p a d", a=NT)
    v65 = sb_b1[:].rearrange("p (a d) -> p a d", a=NT)
    wq_asb = sb_b1[:, 0:8192].rearrange("p (k m) -> p k m", k=8)
    wdn_b = sb_b1[:, 8192:12288].rearrange("p (k m) -> p k m", k=8)
    wku_b = sb_b1[:, 12288:16384].rearrange("p (k m) -> p k m", k=4)
    wq_fsb = sb_b2[:].rearrange("p (k m) -> p k m", k=8)
    gbuf = sb_b2[:].rearrange("p (b d) -> p b d", b=NBUF)
    q_rot = sb_b2[:].rearrange("p (c t) -> p c t", c=8)

    P = Plan(nc, stack)
    state = {}

    v_x = P.step("sync", "ld", lambda e: e.dma_start(
        out=sb_x[:], in_=x_in.rearrange("(a p) d -> p a d", p=128)), 16)
    v_id = P.step("sync", "ld", lambda e: e.dma_start(out=sb_id[:], in_=id_d), 16)
    v_swap = P.step("sync", "ld", lambda e: e.dma_start(out=sb_swap[:], in_=swap_d), 16)
    v_iota = P.step("sync", "ld", lambda e: e.dma_start(out=sb_iota[:], in_=iota_d), 16)

    BN_G = aview(36352, [128, 16], F32)
    BN_B = aview(36416, [128, 16], F32)
    BN_SC = aview(36480, [128, 16], F32)
    BN_BI = aview(36544, [128, 16], F32)
    BN_T = aview(36608, [128, 16], F32)
    BN_T2 = aview(36672, [128, 16], F32)
    P.step("sync", "ld", lambda e: e.dma_start(out=BN_G[:], in_=bng_d), 16)
    v_bn = P.step("sync", "ld", lambda e: e.dma_start(out=BN_B[:], in_=bnb_d), 16)

    P.run("vector", lambda e: e.memset(C_M32, -32.0))
    P.run("vector", lambda e: e.memset(C_MHI8, 0xFFFFFF00))
    P.run("vector", lambda e: e.memset(C_MHI16, 0xFFFF0000))
    v_msk = P.step("vector", "vv", lambda e: e.memset(C_MLO16, 0x0000FFFF))

    CONV = 512
    RP = NEXP // CONV
    conv_w = [0] * CONV

    def emit_conv():
        NC2 = 16
        RR = NEXP // NC2
        for i in range(NC2):
            conv_w[CONV - 1] = P.step("gpsimd", "cw", lambda e, i=i: e.dma_start(
                out=fusedb[i * RR:(i + 1) * RR, :],
                in_=fused_d[i * RR:(i + 1) * RR, :]), 16)

    def emit_ln_stats(x_ready_vv):
        for t in range(NT):
            P.wait("vector", "ld", v_x)
            P.wait("vector", "vv", x_ready_vv)
            P.step("vector", "vv", lambda e, t=t: e.reduce_sum(
                out=LN_SUM[:, t:t + 1], in_=sb_x[:, t, :], axis=AX.X))
            P.wait("scalar", "ld", v_x)
            P.wait("scalar", "vv", x_ready_vv)
            P.step("scalar", "ss", lambda e, t=t: e.activation(
                out=sq_scr[:], in_=sb_x[:, t, :], func=AF.Square,
                accum_out=LN_SQ[:, t:t + 1]))
        v_sq = P.cnt["ss"]
        P.wait("vector", "ss", v_sq)
        P.drain("vector")
        P.step("vector", "vv", lambda e: e.tensor_scalar(
            out=LN_SUM[:], in0=LN_SUM[:], scalar1=1.0 / E, scalar2=None,
            op0=ALU.mult))
        P.step("vector", "vv", lambda e: e.tensor_scalar(
            out=LN_SQ[:], in0=LN_SQ[:], scalar1=1.0 / E, scalar2=None,
            op0=ALU.mult))
        P.drain("vector")
        P.step("vector", "vv", lambda e: e.tensor_tensor(
            out=LN_SC[:], in0=LN_SUM[:], in1=LN_SUM[:], op=ALU.mult))
        P.drain("vector")
        P.step("vector", "vv", lambda e: e.tensor_tensor(
            out=LN_SC[:], in0=LN_SQ[:], in1=LN_SC[:], op=ALU.subtract))
        P.drain("vector")
        v_var = P.step("vector", "vv", lambda e: e.tensor_scalar(
            out=LN_SC[:], in0=LN_SC[:], scalar1=1e-5, scalar2=None, op0=ALU.add))
        P.wait("scalar", "vv", v_var)
        v_sqr = P.step("scalar", "ss", lambda e: e.sqrt(LN_B2[:], LN_SC[:]))
        P.wait("vector", "ss", v_sqr)
        P.step("vector", "vv", lambda e: e.reciprocal(LN_SC[:], LN_B2[:]))
        P.drain("vector")
        v_ret = P.step("vector", "vv", lambda e: e.tensor_tensor(
            out=LN_B2[:], in0=LN_SUM[:], in1=LN_SC[:], op=ALU.mult))
        P.drain("vector")
        return v_ret

    def emit_ln_apply(t, out_ap, v_stats, extra_vv=0, strong=False):
        P.wait("vector", "vv", max(v_stats, extra_vv))
        v = P.step("vector", "vv", lambda e, t=t, out_ap=out_ap: e.tensor_scalar(
            out=out_ap, in0=sb_x[:, t, :],
            scalar1=LN_SC[:, t:t + 1], scalar2=LN_B2[:, t:t + 1],
            op0=ALU.mult, op1=ALU.subtract))
        if strong:
            P.drain("vector")
            P.step("aa" if False else "vector", "vv", lambda e: e.engine_nop())
            v = P.step("vector", "aa", lambda e: e.engine_nop())
        return v

    def emit_transpose_tile(src_ap, dst_ap, bk, waits_vv=()):
        pt8 = psum[bk][:].bitcast(BF16).rearrange("p (c q) -> p c q", c=8)
        for v in waits_vv:
            P.wait("tensor", "vv", v)
        P.wait("tensor", "ld", v_id)
        for c in range(8):
            P.step("tensor", "tt", lambda e, c=c, src_ap=src_ap, pt8=pt8:
                   e.transpose(pt8[:, c, :], src_ap[:, c * 128:(c + 1) * 128],
                               sb_id[:]))
        v_tr = P.cnt["tt"]
        P.wait("vector", "tt", v_tr)
        return P.step("vector", "vv", lambda e, pt8=pt8, dst_ap=dst_ap:
                      e.tensor_copy(out=dst_ap, in_=pt8))

    def emit_ffn(app, x_ready_vv):
        v_lnst = emit_ln_stats(x_ready_vv)
        tr_ev = [0, 0]
        for t in range(NT):
            v_a = emit_ln_apply(t, xn_tok[:, t, :], v_lnst)
            tr_ev[t % 2] = emit_transpose_tile(
                xn_tok[:, t, :], xn_fm[:, :, t * 128:(t + 1) * 128],
                4 + (t % 2), waits_vv=(v_a, tr_ev[t % 2]))
        v_xn = P.cnt["vv"]

        P.wait("tensor", "gd", state["v_wqf"])
        bank_free = [0, 0]
        v_comb = 0
        for m in range(16):
            for tb in range(4):
                bk = tb % 2
                P.wait("tensor", "vv", max(v_xn, bank_free[bk]))
                for k in range(8):
                    P.step("tensor", "tt", lambda e, m=m, tb=tb, k=k, bk=bk:
                           e.matmul(psum[bk][:],
                                    wq_fsb[:, k, m * 128:(m + 1) * 128],
                                    xn_fm[:, k, tb * 512:(tb + 1) * 512],
                                    start=(k == 0), stop=(k == 7)))
                v_mm = P.cnt["tt"]
                P.wait("scalar", "tt", v_mm)
                P.wait("scalar", "vv", v_comb)
                v_s = P.step("scalar", "ss", lambda e, tb=tb, bk=bk: e.activation(
                    out=sq_scr[:, 0:512], in_=psum[bk][:], func=AF.Square,
                    accum_out=LN_T4B[:, tb:tb + 1]))
                P.wait("vector", "tt", v_mm)
                P.wait("vector", "ss", v_s)
                bank_free[bk] = P.step("vector", "vv", lambda e, tb=tb, bk=bk:
                                       e.reduce_sum(out=LN_T4[:, tb:tb + 1],
                                                    in_=psum[bk][:], axis=AX.X))
            P.drain("vector")
            P.step("vector", "vv", lambda e, m=m: e.reduce_sum(
                out=ST_SUM[:, m:m + 1], in_=LN_T4[:], axis=AX.X))
            v_comb = P.step("vector", "vv", lambda e, m=m: e.reduce_sum(
                out=ST_SQ[:, m:m + 1], in_=LN_T4B[:], axis=AX.X))
            bank_free[1] = v_comb
        v_stats = P.cnt["vv"]

        P.wait("scalar", "vv", v_stats)
        v_ccw = P.step("scalar", "sd2", lambda e: e.dma_start(out=ccin, in_=ST_ALL), 16)
        if cc_pre[0]:
            v_cc = cc_pre[0]
            assert v_ccw == cc_pre[1], (v_ccw, cc_pre)
            cc_pre[0] = 0
        else:
            P.wait("gpsimd", "sd2", v_ccw)
            v_cc = P.step("gpsimd", "cc", lambda e: e.collective_compute(
                "AllReduce", ALU.add, replica_groups=[list(range(8))],
                ins=[ccin], outs=[ccout]))
        P.wait("scalar", "cc", v_cc)
        v_ccr = P.step("scalar", "sd2", lambda e: e.dma_start(
            out=ST_ALL, in_=ccout), 16)

        P.wait("vector", "ld", v_bn)
        P.wait("vector", "sd2", v_ccr)
        P.drain("vector")
        P.step("vector", "vv", lambda e: e.tensor_scalar(
            out=ST_SUM[:], in0=ST_SUM[:], scalar1=1.0 / (8 * T), scalar2=None,
            op0=ALU.mult))
        P.step("vector", "vv", lambda e: e.tensor_scalar(
            out=ST_SQ[:], in0=ST_SQ[:], scalar1=1.0 / (8 * T), scalar2=None,
            op0=ALU.mult))
        P.drain("vector")
        P.step("vector", "vv", lambda e: e.tensor_tensor(
            out=BN_T[:], in0=ST_SUM[:], in1=ST_SUM[:], op=ALU.mult))
        P.drain("vector")
        P.step("vector", "vv", lambda e: e.tensor_tensor(
            out=BN_T[:], in0=ST_SQ[:], in1=BN_T[:], op=ALU.subtract))
        P.drain("vector")
        v_v2 = P.step("vector", "vv", lambda e: e.tensor_scalar(
            out=BN_T[:], in0=BN_T[:], scalar1=1e-5, scalar2=None, op0=ALU.add))
        P.wait("scalar", "vv", v_v2)
        v_s2 = P.step("scalar", "ss", lambda e: e.sqrt(BN_T2[:], BN_T[:]))
        P.wait("vector", "ss", v_s2)
        P.step("vector", "vv", lambda e: e.reciprocal(BN_T[:], BN_T2[:]))
        P.drain("vector")
        P.step("vector", "vv", lambda e: e.tensor_tensor(
            out=BN_SC[:], in0=BN_G[:], in1=BN_T[:], op=ALU.mult))
        P.drain("vector")
        P.step("vector", "vv", lambda e: e.tensor_tensor(
            out=BN_T[:], in0=ST_SUM[:], in1=BN_SC[:], op=ALU.mult))
        P.drain("vector")
        v_bnc = P.step("vector", "vv", lambda e: e.tensor_tensor(
            out=BN_BI[:], in0=BN_B[:], in1=BN_T[:], op=ALU.subtract))
        P.drain("vector")

        P.wait("tensor", "gd", state["v_keysT"])
        sim_mm_last = [0, 0]
        sim_ev = [0, 0]
        for c in range(16):
            cb = c % 2
            qn_write_v = []
            for tb in range(4):
                bk = tb % 2
                P.wait("tensor", "vv", max(v_bnc, bank_free[bk]))
                for k in range(8):
                    P.step("tensor", "tt", lambda e, c=c, tb=tb, k=k, bk=bk:
                           e.matmul(psum[bk][:],
                                    wq_fsb[:, k, c * 128:(c + 1) * 128],
                                    xn_fm[:, k, tb * 512:(tb + 1) * 512],
                                    start=(k == 0), stop=(k == 7)))
                v_mm = P.cnt["tt"]
                P.wait("vector", "tt", max(v_mm, sim_mm_last[cb]))
                bank_free[bk] = P.step("vector", "vv",
                                       lambda e, c=c, tb=tb, cb=cb, bk=bk:
                                       e.tensor_scalar(
                    out=qn2[:, cb, tb * 512:(tb + 1) * 512], in0=psum[bk][:],
                    scalar1=BN_SC[:, c:c + 1], scalar2=BN_BI[:, c:c + 1],
                    op0=ALU.mult, op1=ALU.add))
            v_qn = P.cnt["vv"]
            for t in range(NT):
                sb2 = 2 + (t % 2)
                P.wait("tensor", "vv", max(v_qn, sim_ev[t % 2]))
                v_mm = P.step("tensor", "tt", lambda e, c=c, t=t, cb=cb, sb2=sb2:
                              e.matmul(psum[sb2][:, 0:256],
                                       qn2[:, cb, t * 128:(t + 1) * 128],
                                       keysT_b[:, c, :], start=True, stop=True))
                P.wait("vector", "tt", v_mm)
                P.wait("vector", "ld", v_iota)
                P.step("vector", "vv", lambda e, t=t, sb2=sb2:
                       e.scalar_tensor_tensor(
                    out=simpay[:, t % 2, :].bitcast(U32),
                    in0=psum[sb2][:, 0:256].bitcast(U32),
                    scalar=C_MHI8, in1=sb_iota[:],
                    op0=ALU.bitwise_and, op1=ALU.bitwise_or))
                P.drain("vector")
                sim_ev[t % 2] = P.step("vector", "vv", lambda e, c=c, t=t: e.max(
                    out=c8_all[:, t, c, :], in_=simpay[:, t % 2, :]))
            sim_mm_last[cb] = P.cnt["tt"]

        for t in range(NT):
            c8u = c8_all[:, t, :, :].bitcast(U32)
            P.step("vector", "vv", lambda e, c8u=c8u: e.tensor_scalar(
                out=sxw[:].rearrange("p h j -> p (h j)").bitcast(U32),
                in0=c8u[:, 0:8, :].rearrange("p h j -> p (h j)"),
                scalar1=C_MHI8, scalar2=None, op0=ALU.bitwise_and))
            P.step("vector", "vv", lambda e, c8u=c8u: e.tensor_scalar(
                out=syw[:].rearrange("p h j -> p (h j)").bitcast(U32),
                in0=c8u[:, 8:16, :].rearrange("p h j -> p (h j)"),
                scalar1=C_MHI8, scalar2=None, op0=ALU.bitwise_and))
            P.step("vector", "vv", lambda e, c8u=c8u: e.tensor_scalar(
                out=ixw[:].rearrange("p h j -> p (h j)"),
                in0=c8u[:, 0:8, :].rearrange("p h j -> p (h j)"),
                scalar1=24, scalar2=16,
                op0=ALU.logical_shift_left, op1=ALU.logical_shift_right))
            P.step("vector", "vv", lambda e, c8u=c8u: e.tensor_scalar(
                out=iyw[:].rearrange("p h j -> p (h j)"),
                in0=c8u[:, 8:16, :].rearrange("p h j -> p (h j)"),
                scalar1=24, scalar2=24,
                op0=ALU.logical_shift_left, op1=ALU.logical_shift_right))
            P.drain("vector")
            P.step("vector", "vv", lambda e: e.tensor_tensor(
                out=zwork[:],
                in0=sxw[:].unsqueeze(3).to_broadcast([128, 8, 8, 8]),
                in1=syw[:].unsqueeze(2).to_broadcast([128, 8, 8, 8]),
                op=ALU.add))
            P.drain("vector")
            P.step("vector", "vv", lambda e: e.scalar_tensor_tensor(
                out=zidx[:],
                in0=zwork[:].rearrange("p h a b -> p (h a b)").bitcast(U32)
                    .rearrange("p (h a b) -> p h a b", h=8, a=8),
                scalar=C_MHI16,
                in1=ixw[:].unsqueeze(3).to_broadcast([128, 8, 8, 8]),
                op0=ALU.bitwise_and, op1=ALU.bitwise_or))
            P.drain("vector")
            P.step("vector", "vv", lambda e: e.tensor_tensor(
                out=zidx[:], in0=zidx[:],
                in1=iyw[:].unsqueeze(2).to_broadcast([128, 8, 8, 8]),
                op=ALU.bitwise_or))
            P.drain("vector")
            for h in range(8):
                P.step("vector", "vv", lambda e, h=h: e.max(
                    out=f8w[:, h * 8:(h + 1) * 8],
                    in_=zidx[:, h, :, :].rearrange("p a b -> p (a b)")
                    .bitcast(F32)))
                P.step("vector", "vv", lambda e, h=h: e.max(
                    out=s8w[:, h, :],
                    in_=zwork[:, h, :, :].rearrange("p a b -> p (a b)")))
            P.wait("vector", "vv", v_msk)
            P.drain("vector")
            P.step("vector", "vv", lambda e, t=t: e.tensor_scalar(
                out=idx_all[:, t, :], in0=f8w[:].bitcast(U32),
                scalar1=C_MLO16, scalar2=None, op0=ALU.bitwise_and))
            P.step("vector", "vv", lambda e: e.tensor_tensor(
                out=e8w[:], in0=s8w[:],
                in1=s8w[:, :, 0:1].to_broadcast([128, 8, 8]), op=ALU.subtract))
            v_e = P.cnt["vv"]
            P.wait("scalar", "vv", v_e)
            v_x8 = P.step("scalar", "ss", lambda e: e.activation(
                out=e8w[:], in_=e8w[:], func=AF.Exp))
            P.wait("vector", "ss", v_x8)
            P.step("vector", "vv", lambda e: e.tensor_reduce(
                out=r8w[:], in_=e8w[:], axis=AX.X, op=ALU.add))
            P.drain("vector")
            P.step("vector", "vv", lambda e: e.reciprocal(r8w[:], r8w[:]))
            P.drain("vector")
            P.step("vector", "vv", lambda e, t=t: e.tensor_tensor(
                out=smax_all[:, t, :].rearrange("p (h j) -> p h j", h=8),
                in0=e8w[:], in1=r8w[:].unsqueeze(2).to_broadcast([128, 8, 8]),
                op=ALU.mult))
        v_topk = P.cnt["vv"]

        P.wait("gpsimd", "cw", conv_w[CONV - 1])
        P.wait("gpsimd", "vv", v_topk)
        up_mm = {}
        res_add = [0, 0]
        i = 0
        for t in range(NT):
            u0, u1 = (t % 2) * 2, (t % 2) * 2 + 1
            for hj in range(64):
                b = i % NBUF
                if i >= NBUF:
                    P.wait("gpsimd", "tt", up_mm[i - NBUF])
                v_g = P.step("gpsimd", "gd", lambda e, t=t, hj=hj, b=b:
                             e.indirect_dma_start(
                                 gbuf[:, b, :], None, fusedb,
                                 IndirectOffsetOnAxis(
                                     idx_all[:, t, hj:hj + 1].bitcast(I32), 0)), 16)
                P.wait("vector", "gd", v_g)
                v_d = P.step("vector", "vv", lambda e, t=t, hj=hj, b=b:
                             e.scalar_tensor_tensor(
                                 out=sq_scr[:, 0:1024], in0=gbuf[:, b, 0:1024],
                                 scalar=1.0, in1=xn_tok[:, t, :],
                                 op0=ALU.mult, op1=ALU.mult,
                                 accum_out=hdot[:, hj:hj + 1]))
                P.wait("scalar", "vv", v_d)
                v_gl = P.step("scalar", "ss", lambda e, hj=hj: e.activation(
                    out=gelu_c[:, hj:hj + 1], in_=hdot[:, hj:hj + 1],
                    func=AF.Gelu_apprx_tanh))
                P.wait("vector", "ss", v_gl)
                v_sc = P.step("vector", "vv", lambda e, t=t, hj=hj, b=b:
                              e.tensor_scalar(
                                  out=gbuf[:, b, 1024:2048],
                                  in0=gbuf[:, b, 1024:2048],
                                  scalar1=gelu_c[:, hj:hj + 1],
                                  scalar2=smax_all[:, t, hj:hj + 1],
                                  op0=ALU.mult, op1=ALU.mult))
                P.wait("tensor", "vv", v_sc)
                if hj == 0:
                    P.wait("tensor", "vv", res_add[t % 2])
                    P.wait("tensor", "ld", v_id)
                P.step("tensor", "tt", lambda e, b=b, hj=hj, u0=u0: e.matmul(
                    psum[u0][:], sb_id[:], gbuf[:, b, 1024:1536],
                    start=(hj == 0), stop=(hj == 63)))
                up_mm[i] = P.step("tensor", "tt", lambda e, b=b, hj=hj, u1=u1:
                                  e.matmul(psum[u1][:], sb_id[:],
                                           gbuf[:, b, 1536:2048],
                                           start=(hj == 0), stop=(hj == 63)))
                i += 1
            P.wait("vector", "tt", up_mm[i - 1])
            P.step("vector", "vv", lambda e, t=t, u0=u0: e.scalar_tensor_tensor(
                out=sb_x[:, t, 0:512], in0=psum[u0][:], scalar=0.5,
                in1=sb_x[:, t, 0:512], op0=ALU.mult, op1=ALU.add))
            res_add[t % 2] = P.step("vector", "vv", lambda e, t=t, u1=u1:
                                    e.scalar_tensor_tensor(
                out=sb_x[:, t, 512:1024], in0=psum[u1][:], scalar=0.5,
                in1=sb_x[:, t, 512:1024], op0=ALU.mult, op1=ALU.add))
        return P.cnt["vv"], P.cnt["tt"]

    def emit_rope_group(t, grp, pb, dst, v_qm, swb, last_rope):
        for mi in range(4):
            c = grp * 4 + mi
            P.wait("vector", "tt", v_qm)
            if last_rope["sw"]:
                P.wait("vector", "tt", last_rope["sw"])
            v_ev = P.step("vector", "vv", lambda e, mi=mi, pb=pb, c=c:
                          e.tensor_copy(out=qraw[:, c % 2, :],
                                        in_=psum[pb][:, mi * 128:(mi + 1) * 128]))
            P.wait("tensor", "vv", max(v_ev, last_rope["tt2"]))
            P.wait("tensor", "ld", v_swap)
            v_sw = P.step("tensor", "tt", lambda e, swb=swb, c=c: e.matmul(
                psum[swb][:, 0:128], sb_swap[:], qraw[:, c % 2, :],
                start=True, stop=True))
            last_rope["sw"] = v_sw
            P.wait("vector", "tt", v_sw)
            P.wait("vector", "ld", state.get("v_rope_ld", 0))
            P.drain("vector")
            P.step("vector", "vv", lambda e, t=t, swb=swb: e.tensor_tensor(
                out=rope_tmp[:], in0=psum[swb][:, 0:128],
                in1=stab[:, t * 128:(t + 1) * 128], op=ALU.mult))
            P.step("vector", "vv", lambda e, t=t, c=c, dst=dst: e.tensor_tensor(
                out=dst[:, c, t * 128:(t + 1) * 128], in0=qraw[:, c % 2, :],
                in1=ctab[:, t * 128:(t + 1) * 128], op=ALU.mult))
            P.drain("vector")
            last_rope["tt2"] = P.step("vector", "vv",
                                      lambda e, t=t, c=c, dst=dst: e.tensor_tensor(
                out=dst[:, c, t * 128:(t + 1) * 128],
                in0=dst[:, c, t * 128:(t + 1) * 128],
                in1=rope_tmp[:], op=ALU.add))
        return last_rope["tt2"]

    def emit_attention(x_ready_vv, att_stop=None, att_tiles=NT):
        v_lnst = emit_ln_stats(x_ready_vv)
        P.wait("tensor", "gd", state["v_attw1"])
        lat_ev = [0, 0]
        fm_ev = [0, 0]
        rq = {"sw": 0, "tt2": 0}
        for t in range(att_tiles):
            v_a = emit_ln_apply(t, ln2t[:], v_lnst,
                                extra_vv=max(fm_ev[0], fm_ev[1]), strong=True)
            P.wait("tensor", "aa", v_a)
            v_fm = emit_transpose_tile(ln2t[:], lnfm_dst(t), 4 + (t % 2),
                                       waits_vv=(fm_ev[t % 2],))
            fm_ev[t % 2] = v_fm
            lb = t % 2
            P.wait("tensor", "vv", max(v_fm, lat_ev[lb]))
            for ml in range(4):
                for k in range(8):
                    P.step("tensor", "tt", lambda e, t=t, ml=ml, k=k, lb=lb:
                           e.matmul(psum[lb][:, ml * 128:(ml + 1) * 128],
                                    wdn_b[:, k, ml * 128:(ml + 1) * 128],
                                    lnfm_col(t, k),
                                    start=(k == 0), stop=(k == 7)))
            v_lm = P.cnt["tt"]
            P.wait("vector", "tt", v_lm)
            lat_ev[lb] = P.step("vector", "vv", lambda e, t=t, lb=lb:
                                e.tensor_copy(
                out=lat_fm[:, :, t * 128:(t + 1) * 128],
                in_=psum[lb][:].rearrange("p (c q) -> p c q", c=4)))
            qb = 2 + (t % 2)
            for grp in range(2):
                P.wait("tensor", "vv", max(v_fm, rq["tt2"]))
                for m in range(grp * 4, grp * 4 + 4):
                    for k in range(8):
                        P.step("tensor", "tt",
                               lambda e, t=t, m=m, k=k, qb=qb: e.matmul(
                            psum[qb][:, (m % 4) * 128:(m % 4 + 1) * 128],
                            wq_asb[:, k, m * 128:(m + 1) * 128],
                            lnfm_col(t, k), start=(k == 0), stop=(k == 7)))
                emit_rope_group(t, grp, qb, q_rot, P.cnt["tt"], 6 + (t % 2), rq)
        v_loop1 = P.cnt["vv"]
        if att_stop == 1:
            return v_loop1, P.cnt["tt"]

        P.wait("tensor", "gd", state["v_attw2"])
        rk = {"sw": 0, "tt2": 0}
        for t in range(NT):
            kb = t % 2
            for grp in range(2):
                P.wait("tensor", "vv",
                       max(v_loop1 if t == 0 else 0, lat_ev[t % 2], rk["tt2"]))
                for m in range(grp * 4, grp * 4 + 4):
                    for k4 in range(4):
                        P.step("tensor", "tt",
                               lambda e, t=t, m=m, k4=k4, kb=kb: e.matmul(
                            psum[kb][:, (m % 4) * 128:(m % 4 + 1) * 128],
                            wku_b[:, k4, m * 128:(m + 1) * 128],
                            lat_fm[:, k4, t * 128:(t + 1) * 128],
                            start=(k4 == 0), stop=(k4 == 3)))
                emit_rope_group(t, grp, kb, k_rot, P.cnt["tt"], 2 + (t % 2), rk)
        v_loop2 = P.cnt["vv"]
        P.wait("gpsimd", "vv", v_loop2)
        state["v_attw3"] = P.step("gpsimd", "gd", lambda e: e.dma_start(
            out=wvu_b[:], in_=wvu_d.rearrange("(k p) m -> p k m", p=128)), 16)
        if att_stop == 2:
            return v_loop2, P.cnt["tt"]

        P.wait("vector", "vv", v_loop2)
        v_ones = P.step("vector", "vv", lambda e: e.memset(
            v65[:].rearrange("p a (g u) -> p a g u", g=16)[:, :, :, 64:65], 1.0))
        P.wait("tensor", "gd", state["v_attw3"])
        vv_ev = [v_ones, v_ones]
        for t in range(NT):
            vb = t % 2
            for nb in range(2):
                P.wait("tensor", "vv", max(v_ones, vv_ev[vb]))
                for k4 in range(4):
                    P.step("tensor", "tt", lambda e, t=t, nb=nb, k4=k4, vb=vb:
                           e.matmul(psum[vb][:],
                                    lat_fm[:, k4, t * 128:(t + 1) * 128],
                                    wvu_b[:, k4, nb * 512:(nb + 1) * 512],
                                    start=(k4 == 0), stop=(k4 == 3)))
                v_vm = P.cnt["tt"]
                P.wait("vector", "tt", v_vm)
                vv_ev[vb] = P.step("vector", "vv", lambda e, t=t, nb=nb, vb=vb:
                                   e.tensor_copy(
                    out=v65[:, t, :].rearrange("p (g u) -> p g u", g=16)
                        [:, nb * 8:(nb + 1) * 8, 0:64],
                    in_=psum[vb][:].rearrange("p (g u) -> p g u", g=8)))
        v_loop3 = P.cnt["vv"]
        P.wait("gpsimd", "vv", v_loop3)
        state["v_wo"] = P.step("gpsimd", "gd", lambda e: e.dma_start(
            out=wo_b[:], in_=wo_d.rearrange("(k p) m -> p k m", p=128)), 16)
        if att_stop == 3:
            return v_loop3, P.cnt["tt"]

        P.wait("tensor", "gd", state["v_wo"])
        s_bank_exp = [0, 0]
        pt_bank_ev = [0, 0]
        pv_done = [0, 0]
        pt_cons = 0
        p_cons = 0
        oadd = [0, 0]
        for qt in range(NT):
            kr = (qt // 2 + 1) * 256
            nk128 = kr // 128
            for h2 in range(HA):
                h = h2
                hp = (h % 2) * 64
                ch = h // 2
                pvb = 6 + (h % 2)
                csizes = []
                off = 0
                while off < kr:
                    w = min(512, kr - off)
                    csizes.append((off, w))
                    off += w
                exp_v = []
                for ci, (off, w) in enumerate(csizes):
                    sbk = ci % 2
                    P.wait("tensor", "vv", v_loop3)
                    P.wait("tensor", "ss", s_bank_exp[sbk])
                    P.step("tensor", "tt", lambda e, qt=qt, hp=hp, ch=ch,
                           off=off, w=w, sbk=sbk: e.matmul(
                        psum[sbk][:, 0:w],
                        q_rot[hp:hp + 64, ch, qt * 128:(qt + 1) * 128],
                        k_rot[hp:hp + 64, ch, off:off + w],
                        start=True, stop=True))
                    v_smm = P.cnt["tt"]
                    P.wait("scalar", "tt", max(v_smm, p_cons))
                    s_bank_exp[sbk] = P.step("scalar", "ss",
                                             lambda e, off=off, w=w, sbk=sbk:
                                             e.activation(
                        out=p_sb[:, off:off + w], in_=psum[sbk][:, 0:w],
                        func=AF.Exp, bias=C_M32, scale=0.125))
                    exp_v.append(s_bank_exp[sbk])
                for g in range(0, nk128, 8):
                    gi = (g // 8) % 2
                    gbk = 4 + gi
                    pt8 = psum[gbk][:].bitcast(BF16).rearrange(
                        "p (c q) -> p c q", c=8)
                    n_in_g = min(8, nk128 - g)
                    P.wait("tensor", "vv", pt_bank_ev[gi])
                    P.wait("tensor", "ld", v_id)
                    for c2 in range(n_in_g):
                        ck = g + c2
                        P.wait("tensor", "ss", exp_v[(ck * 128) // 512])
                        P.step("tensor", "tt", lambda e, ck=ck, c2=c2, pt8=pt8:
                               e.transpose(pt8[:, c2, :],
                                           p_sb[:, ck * 128:(ck + 1) * 128],
                                           sb_id[:]))
                    v_tr = P.cnt["tt"]
                    p_cons = v_tr
                    P.wait("vector", "tt", max(v_tr, pt_cons))
                    pt_bank_ev[gi] = P.step(
                        "vector", "vv",
                        lambda e, g=g, n_in_g=n_in_g, pt8=pt8: e.tensor_copy(
                            out=pt_sb[:, g * 128:(g + n_in_g) * 128],
                            in_=pt8.rearrange("p c q -> p (c q)")
                            [:, 0:n_in_g * 128]))
                v_ptall = P.cnt["vv"]
                P.wait("tensor", "vv", max(v_ptall, pv_done[h % 2], v_ones))
                for ck in range(nk128):
                    P.step("tensor", "tt", lambda e, qt=qt, h=h, ck=ck, pvb=pvb:
                           e.matmul(psum[pvb][:, 0:65],
                                    pt_sb[:, ck * 128:(ck + 1) * 128],
                                    v65[:, ck, h * 65:(h + 1) * 65],
                                    start=(ck == 0), stop=(ck == nk128 - 1)))
                v_pv = P.cnt["tt"]
                pt_cons = v_pv
                P.wait("vector", "tt", v_pv)
                P.step("vector", "vv", lambda e, pvb=pvb: e.reciprocal(
                    dwork[:, 0:1], psum[pvb][:, 64:65]))
                P.drain("vector")
                pv_done[h % 2] = P.step("vector", "vv", lambda e, h=h, pvb=pvb:
                                        e.tensor_scalar(
                    out=atok[:, h * 64:(h + 1) * 64], in0=psum[pvb][:, 0:64],
                    scalar1=dwork[:, 0:1], scalar2=None, op0=ALU.mult))
            v_atok = P.cnt["vv"]
            v_afm = emit_transpose_tile(
                atok[:], afm[:], 4 + (qt % 2),
                waits_vv=(v_atok, pt_bank_ev[qt % 2]))
            pt_bank_ev[qt % 2] = v_afm
            for nb2 in range(2):
                P.wait("tensor", "vv", max(v_afm, oadd[qt % 2]))
                for f in range(8):
                    P.step("tensor", "tt", lambda e, nb2=nb2, f=f: e.matmul(
                        psum[2 + nb2][:], afm[:, f, :],
                        wo_b[:, f, nb2 * 512:(nb2 + 1) * 512],
                        start=(f == 0), stop=(f == 7)))
            v_om = P.cnt["tt"]
            P.wait("vector", "tt", v_om)
            P.step("vector", "vv", lambda e, qt=qt: e.scalar_tensor_tensor(
                out=sb_x[:, qt, 0:512], in0=psum[2][:], scalar=1.0,
                in1=sb_x[:, qt, 0:512], op0=ALU.mult, op1=ALU.add))
            oadd[qt % 2] = P.step("vector", "vv", lambda e, qt=qt:
                                  e.scalar_tensor_tensor(
                out=sb_x[:, qt, 512:1024], in0=psum[3][:], scalar=1.0,
                in1=sb_x[:, qt, 512:1024], op0=ALU.mult, op1=ALU.add))
        return P.cnt["vv"], P.cnt["tt"]

    state["v_wqf"] = P.step("gpsimd", "gd", lambda e: e.dma_start(
        out=wq_fsb[:], in_=wqf_d.rearrange("(k p) m -> p k m", p=128)), 16)
    state["v_keysT"] = P.step("gpsimd", "gd", lambda e: e.dma_start(
        out=keysT_b[:], in_=keysT_d.rearrange("c p k -> p c k")), 16)
    cc_pre = [0, 0]
    if not skip_ffn1:
        exp_ccw = P.cnt["sd2"] + 16
        P.wait("gpsimd", "sd2", exp_ccw)
        cc_pre[0] = P.step("gpsimd", "cc", lambda e: e.collective_compute(
            "AllReduce", ALU.add, replica_groups=[list(range(8))],
            ins=[ccin], outs=[ccout]))
        cc_pre[1] = exp_ccw
    emit_conv()
    if skip_ffn1:
        v_ffn1_vv, v_ffn1_tt = 1, 0
    else:
        v_ffn1_vv, v_ffn1_tt = emit_ffn(0, 1)

    if stage >= 2:
        P.wait("gpsimd", "vv", v_ffn1_vv)
        P.step("gpsimd", "gd", lambda e: e.dma_start(
            out=wq_asb[:], in_=wq_a.rearrange("(k p) m -> p k m", p=128)), 16)
        state["v_attw1"] = P.step("gpsimd", "gd", lambda e: e.dma_start(
            out=wdn_b[:], in_=wdn_d.rearrange("(k p) m -> p k m", p=128)), 16)
        state["v_attw2"] = P.step("gpsimd", "gd", lambda e: e.dma_start(
            out=wku_b[:], in_=wku_d.rearrange("(k p) m -> p k m", p=128)), 16)
        P.wait("sync", "vv", v_ffn1_vv)
        P.step("sync", "ld", lambda e: e.dma_start(out=ctab[:], in_=ctab_d), 16)
        state["v_rope_ld"] = P.step("sync", "ld", lambda e: e.dma_start(
            out=stab[:], in_=stab_d), 16)
        v_att_vv, v_att_tt = emit_attention(v_ffn1_vv, att_stop, att_tiles)
    else:
        v_att_vv, v_att_tt = v_ffn1_vv, v_ffn1_tt

    if stage >= 3:
        P.wait("gpsimd", "vv", v_att_vv)
        P.wait("gpsimd", "tt", v_att_tt)
        state["v_wqf"] = P.step("gpsimd", "gd", lambda e: e.dma_start(
            out=wq_fsb[:], in_=wqf_d.rearrange("(k p) m -> p k m", p=128)), 16)
        state["v_keysT"] = P.step("gpsimd", "gd", lambda e: e.dma_start(
            out=keysT_b[:], in_=keysT_d.rearrange("c p k -> p c k")), 16)
        v_f2_vv, _ = emit_ffn(1, v_att_vv)
    else:
        v_f2_vv = v_att_vv

    P.wait("scalar", "vv", v_f2_vv)
    v_st = P.step("scalar", "sd2", lambda e: e.dma_start(
        out=y_out.rearrange("(a p) d -> p a d", p=128), in_=sb_x[:]), 16)
    if dbg:
        dbg_specs = [
            ("d_xntok", xn_tok[:], [128, NT, 1024], BF16),
            ("d_c8", c8_all[:], [128, NT, 16, 8], F32),
            ("d_idx", idx_all[:], [128, NT, 64], U32),
            ("d_smax", smax_all[:], [128, NT, 64], F32),
            ("d_bnsc", BN_SC[:], [128, 16], F32),
            ("d_bnbi", BN_BI[:], [128, 16], F32),
            ("d_hdot", hdot[:], [128, 64], F32),
            ("d_xnfm", xn_fm[:], [128, 8, 2048], BF16),
            ("d_gbuf", gbuf[:], [128, NBUF, 2048], BF16),
            ("d_lnsc", LN_SC[:], [128, 16], F32),
            ("d_lnb2", LN_B2[:], [128, 16], F32),
        ]
        if stage >= 2:
            dbg_specs += [
                ("d_qrot", q_rot[:], [128, 8, 2048], BF16),
                ("d_krot", k_rot[:], [128, 8, 2048], BF16),
                ("d_v65", v65[:], [128, NT, 1040], BF16),
                ("d_atok", atok[:], [128, 1024], BF16),
                ("d_ptsb", pt_sb[:], [128, 2048], BF16),
                ("d_b1", sb_b1[:], [128, 16640], BF16),
                ("d_b0p", sb_b0[:, 0:2048], [128, 2048], BF16),
                ("d_aux0", lat_fm[:], [128, 4, 2048], BF16),
            ]
        for nm, ap, shp, dt2 in dbg_specs:
            od = dtn(nm, shp, dt2, kind="ExternalOutput").ap()
            v_st = P.step("scalar", "sd2",
                          lambda e, od=od, ap=ap: e.dma_start(out=od, in_=ap), 16)
    P.wait("scalar", "sd2", v_st)

    with nc.Block() as block:
        @block.sync
        def _(eng):
            for fn in P.fns["sync"]:
                fn(eng)

        @block.scalar
        def _(eng):
            for fn in P.fns["scalar"]:
                fn(eng)

        @block.vector
        def _(eng):
            for fn in P.fns["vector"]:
                fn(eng)

        @block.tensor
        def _(eng):
            for fn in P.fns["tensor"]:
                fn(eng)

        @block.gpsimd
        def _(eng):
            for fn in P.fns["gpsimd"]:
                fn(eng)

    stack.close()
    return nc


def make_consts():
    id128 = np.eye(128, dtype=ml_dtypes.bfloat16)
    swap = np.zeros((128, 128), dtype=ml_dtypes.bfloat16)
    for i in range(0, 128, 2):
        swap[i, i + 1] = 1
        swap[i + 1, i] = 1
    iota = np.tile(np.arange(256, dtype=np.uint32), (128, 1))
    inv = 1.0 / 10000.0 ** (np.arange(0, DA, 2, dtype=np.float32) / DA)
    ang = np.outer(np.arange(T, dtype=np.float32), inv)
    cos = np.cos(ang).astype(np.float32)
    sin = np.sin(ang).astype(np.float32)
    ctab = np.zeros((128, T), np.float32)
    stab = np.zeros((128, T), np.float32)
    for p in range(128):
        r = (p % 64) // 2
        ctab[p] = cos[:, r]
        stab[p] = sin[:, r] * (1.0 if p % 2 == 1 else -1.0)
    return id128, swap, iota, ctab, stab


def make_inputs(L0_attn_x, L0_attn_wq, L0_attn_w_down, L0_attn_wk_up,
                L0_attn_wv_up, L0_attn_wo, L0_ffn_w_q, L0_ffn_bn_gamma,
                L0_ffn_bn_beta, L0_ffn_keys, L0_ffn_down_embed, L0_ffn_up_embed):
    id128, swap, iota, ctab, stab = make_consts()
    keysT = np.ascontiguousarray(
        np.transpose(np.asarray(L0_ffn_keys), (2, 0, 3, 1)).reshape(16, KD, NK))
    fused = np.concatenate(
        [np.asarray(L0_ffn_down_embed), np.asarray(L0_ffn_up_embed)], axis=1)
    common = dict(
        wq_a=np.asarray(L0_attn_wq), wdn=np.asarray(L0_attn_w_down),
        wku=np.asarray(L0_attn_wk_up), wvu=np.asarray(L0_attn_wv_up),
        wo_a=np.asarray(L0_attn_wo), wqf=np.asarray(L0_ffn_w_q),
        bng=np.ascontiguousarray(np.asarray(L0_ffn_bn_gamma).reshape(16, 128).T),
        bnb=np.ascontiguousarray(np.asarray(L0_ffn_bn_beta).reshape(16, 128).T),
        keysT=keysT, fused=np.ascontiguousarray(fused),
        id128=id128, swap128=swap, iota256=iota, ctab=ctab, stab=stab,
    )
    x = np.asarray(L0_attn_x)
    return [dict(common, x=np.ascontiguousarray(x[c])) for c in range(8)]


def kernel(**inputs):
    nc = build(stage=3)
    in_maps = make_inputs(**inputs)
    res = run_bass_kernel_spmd(nc, in_maps, core_ids=list(range(8)))
    return np.stack([res.results[c]["y"] for c in range(8)], axis=0)


# version 07
# speedup vs baseline: 1.1288x; 1.0944x over previous
import numpy as np
import ml_dtypes
from contextlib import ExitStack

import concourse.bass as bass
import concourse.mybir as mybir
from concourse.bass_utils import run_bass_kernel_spmd
from concourse.bass import IndirectOffsetOnAxis

F32 = mybir.dt.float32
BF16 = mybir.dt.bfloat16
U32 = mybir.dt.uint32
I32 = mybir.dt.int32
U8 = mybir.dt.uint8
AF = mybir.ActivationFunctionType
ALU = mybir.AluOpType
AX = mybir.AxisListType

T = 2048
NT = 16
E = 1024
HA, DA = 16, 64
H, KD, NK, TK = 8, 128, 256, 8
NEXP = NK * NK
FQ = 2048
NBUF = 8
ENGS = ("sync", "scalar", "vector", "tensor", "gpsimd")


class Plan:
    def __init__(self, nc, stack):
        self.nc = nc
        self.fns = {e: [] for e in ENGS}
        self.sems = {}
        self.cnt = {}
        for s in ("ld", "sd2", "gd", "vv", "ss", "tt", "cc", "aa", "cw"):
            self.sems[s] = stack.enter_context(nc.semaphore(s))
            self.cnt[s] = 0

    def emit(self, eng, fn):
        self.fns[eng].append(fn)

    def wait(self, eng, sem, val):
        if val is None or val <= 0:
            return
        h = self.sems[sem]
        self.emit(eng, lambda e, h=h, v=val: e.wait_ge(h, v))

    def step(self, eng, sem, fn, n=1):
        self.cnt[sem] += n
        h = self.sems[sem]
        self.emit(eng, lambda e, h=h, n=n, fn=fn: fn(e).then_inc(h, n))
        return self.cnt[sem]

    def run(self, eng, fn):
        self.emit(eng, fn)

    def drain(self, eng):
        self.emit(eng, lambda e: e.drain())


def build(stage=3, dbg=False, att_stop=None, skip_ffn1=False, att_tiles=NT):
    nc = bass.Bass("TRN2", target_bir_lowering=False)
    stack = ExitStack()

    dtn = nc.dram_tensor
    x_in = dtn("x", [T, E], F32, kind="ExternalInput").ap()
    y_out = dtn("y", [T, E], F32, kind="ExternalOutput").ap()
    wq_a = dtn("wq_a", [E, E], F32, kind="ExternalInput").ap()
    wdn_d = dtn("wdn", [E, 512], F32, kind="ExternalInput").ap()
    wku_d = dtn("wku", [512, E], F32, kind="ExternalInput").ap()
    wvu_d = dtn("wvu", [512, E], F32, kind="ExternalInput").ap()
    wo_d = dtn("wo_a", [E, E], F32, kind="ExternalInput").ap()
    wqf_d = dtn("wqf", [E, FQ], F32, kind="ExternalInput").ap()
    bng_d = dtn("bng", [128, 16], F32, kind="ExternalInput").ap()
    bnb_d = dtn("bnb", [128, 16], F32, kind="ExternalInput").ap()
    keysT_d = dtn("keysT", [16, KD, NK], F32, kind="ExternalInput").ap()
    fused_d = dtn("fused", [NEXP, 2048], F32, kind="ExternalInput").ap()
    id_d = dtn("id128", [128, 128], BF16, kind="ExternalInput").ap()
    iota_d = dtn("iota256", [128, 256], U32, kind="ExternalInput").ap()
    ctab_d = dtn("ctab", [128, T], F32, kind="ExternalInput").ap()
    stab_d = dtn("stab", [128, T], F32, kind="ExternalInput").ap()
    swap_d = dtn("swap128", [128, 128], BF16, kind="ExternalInput").ap()

    fusedb = dtn("fusedb", [NEXP, 2048], BF16, kind="Internal").ap()
    ccin = dtn("ccin", [128, 32], F32, kind="Internal").ap()
    ccout = dtn("ccout", [128, 32], F32, kind="Internal").ap()

    def sbuf(name, shape, dt):
        return stack.enter_context(nc.sbuf_tensor(name, shape, dt))

    sb_x = sbuf("sb_x", [128, NT, E], F32)
    sb_b0 = sbuf("sb_b0", [128, 16384], BF16)
    sb_b1 = sbuf("sb_b1", [128, 16640], BF16)
    sb_b2 = sbuf("sb_b2", [128, 16384], BF16)
    sb_aux = sbuf("sb_aux", [128, 36864], U8)
    sb_id = sbuf("sb_id", [128, 128], BF16)
    sb_swap = sbuf("sb_swap", [128, 128], BF16)
    sb_iota = sbuf("sb_iota", [128, 256], U32)
    sb_small = sbuf("sb_small", [128, 96], F32)
    sb_conv = sbuf("sb_conv", [128, 2, 2048], BF16)

    psum = [stack.enter_context(nc.psum_tensor(f"pb{i}", [128, 512], F32))
            for i in range(8)]

    smallu = sb_small[:].bitcast(U32)
    C_M32 = sb_small[:, 0:1]
    C_MHI8 = smallu[:, 1:2]
    C_MHI16 = smallu[:, 2:3]
    C_MLO16 = smallu[:, 3:4]
    LN_SUM = sb_small[:, 4:20]
    LN_SQ = sb_small[:, 20:36]
    LN_SC = sb_small[:, 36:52]
    LN_B2 = sb_small[:, 52:68]
    LN_T4 = sb_small[:, 68:72]
    LN_T4B = sb_small[:, 72:76]
    ST_SUM = sb_small[:, 4:20]
    ST_SQ = sb_small[:, 20:36]
    ST_ALL = sb_small[:, 4:36]

    def aview(off, shape, dt):
        n = 1
        for s in shape[1:]:
            n *= s
        nbytes = n * mybir.dt.size(dt)
        assert off + nbytes <= 36864, (off, shape, dt)
        ap = sb_aux[:, off:off + nbytes].bitcast(dt)
        if len(shape) == 2:
            return ap
        kw = {f"d{i}": shape[i] for i in range(1, len(shape) - 1)}
        spec = " ".join(f"d{i}" for i in range(1, len(shape)))
        return ap.rearrange(f"p ({spec}) -> p {spec}", **kw)

    keysT_b = aview(0, [128, 16, NK], BF16)
    zwork = aview(0, [128, 8, 8, 8], F32)
    zidx = aview(2048, [128, 8, 8, 8], U32)
    sxw = aview(4096, [128, 8, 8], F32)
    syw = aview(4352, [128, 8, 8], F32)
    ixw = aview(4608, [128, 8, 8], U32)
    iyw = aview(4864, [128, 8, 8], U32)
    f8w = aview(5120, [128, 64], F32)
    s8w = aview(5376, [128, 8, 8], F32)
    e8w = aview(5632, [128, 8, 8], F32)
    r8w = aview(5888, [128, 8], F32)
    hdot = aview(6144, [128, 64], F32)
    gelu_c = aview(6400, [128, 64], F32)
    c8_all = aview(8192, [128, NT, 16, 8], F32)
    idx_all = aview(16384, [128, NT, 64], U32)
    smax_all = aview(20480, [128, NT, 64], F32)
    qn2 = aview(24576, [128, 2, FQ], BF16)
    simpay = aview(32768, [128, 2, NK], F32)
    sq_scr = aview(32768, [128, 1024], BF16)

    lat_fm = aview(0, [128, 4, T], BF16)
    ctab = aview(16384, [128, T], F32)
    stab = aview(24576, [128, T], F32)
    ln2t = aview(32768, [128, 1024], BF16)
    qraw = aview(34816, [128, 2, 128], BF16)
    lnfm = aview(24576, [128, 2, 8, 128], BF16)
    wvu_b = aview(16384, [128, 4, E], BF16)
    wo_b = aview(0, [128, 8, E], BF16)
    p_sb = aview(16384, [128, 2048], BF16)
    pt_sb = aview(20480, [128, 2048], BF16)
    atok = aview(24576, [128, 1024], BF16)
    afm = aview(26624, [128, 8, 128], BF16)
    dwork = aview(28672, [128, 16], F32)
    rope_tmp = aview(35328, [128, 128], F32)

    lnfm = aview(16384, [128, 2, 8, 128], BF16)
    lnfm = None

    lnfm_b0 = sb_b0[:, 0:2048].rearrange("p (r c q) -> p r c q", r=2, c=8)

    def lnfm_col(t, k):
        return lnfm_b0[:, t % 2, k, :]

    def lnfm_dst(t):
        return lnfm_b0[:, t % 2, :, :]

    xn_fm = sb_b0[:].rearrange("p (c t) -> p c t", c=8)
    k_rot = sb_b0[:].rearrange("p (c t) -> p c t", c=8)
    xn_tok = sb_b1[:, 0:16384].rearrange("p (a d) -> p a d", a=NT)
    v65 = sb_b1[:].rearrange("p (a d) -> p a d", a=NT)
    wq_asb = sb_b1[:, 0:8192].rearrange("p (k m) -> p k m", k=8)
    wdn_b = sb_b1[:, 8192:12288].rearrange("p (k m) -> p k m", k=8)
    wku_b = sb_b1[:, 12288:16384].rearrange("p (k m) -> p k m", k=4)
    wq_fsb = sb_b2[:].rearrange("p (k m) -> p k m", k=8)
    gbuf = sb_b2[:].rearrange("p (b d) -> p b d", b=NBUF)
    q_rot = sb_b2[:].rearrange("p (c t) -> p c t", c=8)

    P = Plan(nc, stack)
    state = {}

    v_x = P.step("sync", "ld", lambda e: e.dma_start(
        out=sb_x[:], in_=x_in.rearrange("(a p) d -> p a d", p=128)), 16)
    v_id = P.step("sync", "ld", lambda e: e.dma_start(out=sb_id[:], in_=id_d), 16)
    v_swap = P.step("sync", "ld", lambda e: e.dma_start(out=sb_swap[:], in_=swap_d), 16)
    v_iota = P.step("sync", "ld", lambda e: e.dma_start(out=sb_iota[:], in_=iota_d), 16)

    BN_G = aview(36352, [128, 16], F32)
    BN_B = aview(36416, [128, 16], F32)
    BN_SC = aview(36480, [128, 16], F32)
    BN_BI = aview(36544, [128, 16], F32)
    BN_T = aview(36608, [128, 16], F32)
    BN_T2 = aview(36672, [128, 16], F32)
    P.step("sync", "ld", lambda e: e.dma_start(out=BN_G[:], in_=bng_d), 16)
    v_bn = P.step("sync", "ld", lambda e: e.dma_start(out=BN_B[:], in_=bnb_d), 16)

    P.run("vector", lambda e: e.memset(C_M32, -32.0))
    P.run("vector", lambda e: e.memset(C_MHI8, 0xFFFFFF00))
    P.run("vector", lambda e: e.memset(C_MHI16, 0xFFFF0000))
    v_msk = P.step("vector", "vv", lambda e: e.memset(C_MLO16, 0x0000FFFF))

    CONV = 512
    RP = NEXP // CONV
    conv_w = [0] * CONV

    def emit_conv():
        NC2 = 16
        RR = NEXP // NC2
        for i in range(NC2):
            conv_w[CONV - 1] = P.step("gpsimd", "cw", lambda e, i=i: e.dma_start(
                out=fusedb[i * RR:(i + 1) * RR, :],
                in_=fused_d[i * RR:(i + 1) * RR, :]), 16)

    def emit_ln_stats(x_ready_vv):
        for t in range(NT):
            P.wait("vector", "ld", v_x)
            P.wait("vector", "vv", x_ready_vv)
            P.step("vector", "vv", lambda e, t=t: e.reduce_sum(
                out=LN_SUM[:, t:t + 1], in_=sb_x[:, t, :], axis=AX.X))
            P.wait("scalar", "ld", v_x)
            P.wait("scalar", "vv", x_ready_vv)
            P.step("scalar", "ss", lambda e, t=t: e.activation(
                out=sq_scr[:], in_=sb_x[:, t, :], func=AF.Square,
                accum_out=LN_SQ[:, t:t + 1]))
        v_sq = P.cnt["ss"]
        P.wait("vector", "ss", v_sq)
        P.drain("vector")
        P.step("vector", "vv", lambda e: e.tensor_scalar(
            out=LN_SUM[:], in0=LN_SUM[:], scalar1=1.0 / E, scalar2=None,
            op0=ALU.mult))
        P.step("vector", "vv", lambda e: e.tensor_scalar(
            out=LN_SQ[:], in0=LN_SQ[:], scalar1=1.0 / E, scalar2=None,
            op0=ALU.mult))
        P.drain("vector")
        P.step("vector", "vv", lambda e: e.tensor_tensor(
            out=LN_SC[:], in0=LN_SUM[:], in1=LN_SUM[:], op=ALU.mult))
        P.drain("vector")
        P.step("vector", "vv", lambda e: e.tensor_tensor(
            out=LN_SC[:], in0=LN_SQ[:], in1=LN_SC[:], op=ALU.subtract))
        P.drain("vector")
        v_var = P.step("vector", "vv", lambda e: e.tensor_scalar(
            out=LN_SC[:], in0=LN_SC[:], scalar1=1e-5, scalar2=None, op0=ALU.add))
        P.wait("scalar", "vv", v_var)
        v_sqr = P.step("scalar", "ss", lambda e: e.sqrt(LN_B2[:], LN_SC[:]))
        P.wait("vector", "ss", v_sqr)
        P.step("vector", "vv", lambda e: e.reciprocal(LN_SC[:], LN_B2[:]))
        P.drain("vector")
        v_ret = P.step("vector", "vv", lambda e: e.tensor_tensor(
            out=LN_B2[:], in0=LN_SUM[:], in1=LN_SC[:], op=ALU.mult))
        P.drain("vector")
        return v_ret

    def emit_ln_apply(t, out_ap, v_stats, extra_vv=0, strong=False):
        P.wait("vector", "vv", max(v_stats, extra_vv))
        v = P.step("vector", "vv", lambda e, t=t, out_ap=out_ap: e.tensor_scalar(
            out=out_ap, in0=sb_x[:, t, :],
            scalar1=LN_SC[:, t:t + 1], scalar2=LN_B2[:, t:t + 1],
            op0=ALU.mult, op1=ALU.subtract))
        if strong:
            P.drain("vector")
            P.step("aa" if False else "vector", "vv", lambda e: e.engine_nop())
            v = P.step("vector", "aa", lambda e: e.engine_nop())
        return v

    def emit_transpose_tile(src_ap, dst_ap, bk, waits_vv=()):
        pt8 = psum[bk][:].bitcast(BF16).rearrange("p (c q) -> p c q", c=8)
        for v in waits_vv:
            P.wait("tensor", "vv", v)
        P.wait("tensor", "ld", v_id)
        for c in range(8):
            P.step("tensor", "tt", lambda e, c=c, src_ap=src_ap, pt8=pt8:
                   e.transpose(pt8[:, c, :], src_ap[:, c * 128:(c + 1) * 128],
                               sb_id[:]))
        v_tr = P.cnt["tt"]
        P.wait("vector", "tt", v_tr)
        return P.step("vector", "vv", lambda e, pt8=pt8, dst_ap=dst_ap:
                      e.tensor_copy(out=dst_ap, in_=pt8))

    def emit_ffn(app, x_ready_vv):
        v_lnst = emit_ln_stats(x_ready_vv)
        tr_ev = [0, 0]
        for t in range(NT):
            v_a = emit_ln_apply(t, xn_tok[:, t, :], v_lnst)
            tr_ev[t % 2] = emit_transpose_tile(
                xn_tok[:, t, :], xn_fm[:, :, t * 128:(t + 1) * 128],
                4 + (t % 2), waits_vv=(v_a, tr_ev[t % 2]))
        v_xn = P.cnt["vv"]

        P.wait("tensor", "gd", state["v_wqf"])
        bank_free = [0, 0]
        v_comb = 0
        for m in range(16):
            for tb in range(4):
                bk = tb % 2
                P.wait("tensor", "vv", max(v_xn, bank_free[bk]))
                for k in range(8):
                    P.step("tensor", "tt", lambda e, m=m, tb=tb, k=k, bk=bk:
                           e.matmul(psum[bk][:],
                                    wq_fsb[:, k, m * 128:(m + 1) * 128],
                                    xn_fm[:, k, tb * 512:(tb + 1) * 512],
                                    start=(k == 0), stop=(k == 7)))
                v_mm = P.cnt["tt"]
                P.wait("scalar", "tt", v_mm)
                P.wait("scalar", "vv", v_comb)
                v_s = P.step("scalar", "ss", lambda e, tb=tb, bk=bk: e.activation(
                    out=sq_scr[:, 0:512], in_=psum[bk][:], func=AF.Square,
                    accum_out=LN_T4B[:, tb:tb + 1]))
                P.wait("vector", "tt", v_mm)
                P.wait("vector", "ss", v_s)
                bank_free[bk] = P.step("vector", "vv", lambda e, tb=tb, bk=bk:
                                       e.reduce_sum(out=LN_T4[:, tb:tb + 1],
                                                    in_=psum[bk][:], axis=AX.X))
            P.drain("vector")
            P.step("vector", "vv", lambda e, m=m: e.reduce_sum(
                out=ST_SUM[:, m:m + 1], in_=LN_T4[:], axis=AX.X))
            v_comb = P.step("vector", "vv", lambda e, m=m: e.reduce_sum(
                out=ST_SQ[:, m:m + 1], in_=LN_T4B[:], axis=AX.X))
            bank_free[1] = v_comb
        v_stats = P.cnt["vv"]

        P.wait("scalar", "vv", v_stats)
        v_ccw = P.step("scalar", "sd2", lambda e: e.dma_start(out=ccin, in_=ST_ALL), 16)
        if cc_pre[0]:
            v_cc = cc_pre[0]
            assert v_ccw == cc_pre[1], (v_ccw, cc_pre)
            cc_pre[0] = 0
        else:
            P.wait("gpsimd", "sd2", v_ccw)
            v_cc = P.step("gpsimd", "cc", lambda e: e.collective_compute(
                "AllReduce", ALU.add, replica_groups=[list(range(8))],
                ins=[ccin], outs=[ccout]))
        P.wait("scalar", "cc", v_cc)
        v_ccr = P.step("scalar", "sd2", lambda e: e.dma_start(
            out=ST_ALL, in_=ccout), 16)

        P.wait("vector", "ld", v_bn)
        P.wait("vector", "sd2", v_ccr)
        P.drain("vector")
        P.step("vector", "vv", lambda e: e.tensor_scalar(
            out=ST_SUM[:], in0=ST_SUM[:], scalar1=1.0 / (8 * T), scalar2=None,
            op0=ALU.mult))
        P.step("vector", "vv", lambda e: e.tensor_scalar(
            out=ST_SQ[:], in0=ST_SQ[:], scalar1=1.0 / (8 * T), scalar2=None,
            op0=ALU.mult))
        P.drain("vector")
        P.step("vector", "vv", lambda e: e.tensor_tensor(
            out=BN_T[:], in0=ST_SUM[:], in1=ST_SUM[:], op=ALU.mult))
        P.drain("vector")
        P.step("vector", "vv", lambda e: e.tensor_tensor(
            out=BN_T[:], in0=ST_SQ[:], in1=BN_T[:], op=ALU.subtract))
        P.drain("vector")
        v_v2 = P.step("vector", "vv", lambda e: e.tensor_scalar(
            out=BN_T[:], in0=BN_T[:], scalar1=1e-5, scalar2=None, op0=ALU.add))
        P.wait("scalar", "vv", v_v2)
        v_s2 = P.step("scalar", "ss", lambda e: e.sqrt(BN_T2[:], BN_T[:]))
        P.wait("vector", "ss", v_s2)
        P.step("vector", "vv", lambda e: e.reciprocal(BN_T[:], BN_T2[:]))
        P.drain("vector")
        P.step("vector", "vv", lambda e: e.tensor_tensor(
            out=BN_SC[:], in0=BN_G[:], in1=BN_T[:], op=ALU.mult))
        P.drain("vector")
        P.step("vector", "vv", lambda e: e.tensor_tensor(
            out=BN_T[:], in0=ST_SUM[:], in1=BN_SC[:], op=ALU.mult))
        P.drain("vector")
        v_bnc = P.step("vector", "vv", lambda e: e.tensor_tensor(
            out=BN_BI[:], in0=BN_B[:], in1=BN_T[:], op=ALU.subtract))
        P.drain("vector")

        P.wait("tensor", "gd", state["v_keysT"])
        sim_mm_last = [0, 0]
        sim_ev = [0, 0]
        for c in range(16):
            cb = c % 2
            qn_write_v = []
            for tb in range(4):
                bk = tb % 2
                P.wait("tensor", "vv", max(v_bnc, bank_free[bk]))
                for k in range(8):
                    P.step("tensor", "tt", lambda e, c=c, tb=tb, k=k, bk=bk:
                           e.matmul(psum[bk][:],
                                    wq_fsb[:, k, c * 128:(c + 1) * 128],
                                    xn_fm[:, k, tb * 512:(tb + 1) * 512],
                                    start=(k == 0), stop=(k == 7)))
                v_mm = P.cnt["tt"]
                P.wait("vector", "tt", max(v_mm, sim_mm_last[cb]))
                bank_free[bk] = P.step("vector", "vv",
                                       lambda e, c=c, tb=tb, cb=cb, bk=bk:
                                       e.tensor_scalar(
                    out=qn2[:, cb, tb * 512:(tb + 1) * 512], in0=psum[bk][:],
                    scalar1=BN_SC[:, c:c + 1], scalar2=BN_BI[:, c:c + 1],
                    op0=ALU.mult, op1=ALU.add))
            v_qn = P.cnt["vv"]
            for t in range(NT):
                sb2 = 2 + (t % 2)
                P.wait("tensor", "vv", max(v_qn, sim_ev[t % 2]))
                v_mm = P.step("tensor", "tt", lambda e, c=c, t=t, cb=cb, sb2=sb2:
                              e.matmul(psum[sb2][:, 0:256],
                                       qn2[:, cb, t * 128:(t + 1) * 128],
                                       keysT_b[:, c, :], start=True, stop=True))
                P.wait("vector", "tt", v_mm)
                P.wait("vector", "ld", v_iota)
                P.step("vector", "vv", lambda e, t=t, sb2=sb2:
                       e.scalar_tensor_tensor(
                    out=simpay[:, t % 2, :].bitcast(U32),
                    in0=psum[sb2][:, 0:256].bitcast(U32),
                    scalar=C_MHI8, in1=sb_iota[:],
                    op0=ALU.bitwise_and, op1=ALU.bitwise_or))
                P.drain("vector")
                sim_ev[t % 2] = P.step("vector", "vv", lambda e, c=c, t=t: e.max(
                    out=c8_all[:, t, c, :], in_=simpay[:, t % 2, :]))
            sim_mm_last[cb] = P.cnt["tt"]

        for t in range(NT):
            c8u = c8_all[:, t, :, :].bitcast(U32)
            P.step("vector", "vv", lambda e, c8u=c8u: e.tensor_scalar(
                out=sxw[:].rearrange("p h j -> p (h j)").bitcast(U32),
                in0=c8u[:, 0:8, :].rearrange("p h j -> p (h j)"),
                scalar1=C_MHI8, scalar2=None, op0=ALU.bitwise_and))
            P.step("vector", "vv", lambda e, c8u=c8u: e.tensor_scalar(
                out=syw[:].rearrange("p h j -> p (h j)").bitcast(U32),
                in0=c8u[:, 8:16, :].rearrange("p h j -> p (h j)"),
                scalar1=C_MHI8, scalar2=None, op0=ALU.bitwise_and))
            P.step("vector", "vv", lambda e, c8u=c8u: e.tensor_scalar(
                out=ixw[:].rearrange("p h j -> p (h j)"),
                in0=c8u[:, 0:8, :].rearrange("p h j -> p (h j)"),
                scalar1=24, scalar2=16,
                op0=ALU.logical_shift_left, op1=ALU.logical_shift_right))
            P.step("vector", "vv", lambda e, c8u=c8u: e.tensor_scalar(
                out=iyw[:].rearrange("p h j -> p (h j)"),
                in0=c8u[:, 8:16, :].rearrange("p h j -> p (h j)"),
                scalar1=24, scalar2=24,
                op0=ALU.logical_shift_left, op1=ALU.logical_shift_right))
            P.drain("vector")
            P.step("vector", "vv", lambda e: e.tensor_tensor(
                out=zwork[:],
                in0=sxw[:].unsqueeze(3).to_broadcast([128, 8, 8, 8]),
                in1=syw[:].unsqueeze(2).to_broadcast([128, 8, 8, 8]),
                op=ALU.add))
            P.drain("vector")
            P.step("vector", "vv", lambda e: e.scalar_tensor_tensor(
                out=zidx[:],
                in0=zwork[:].rearrange("p h a b -> p (h a b)").bitcast(U32)
                    .rearrange("p (h a b) -> p h a b", h=8, a=8),
                scalar=C_MHI16,
                in1=ixw[:].unsqueeze(3).to_broadcast([128, 8, 8, 8]),
                op0=ALU.bitwise_and, op1=ALU.bitwise_or))
            P.drain("vector")
            P.step("vector", "vv", lambda e: e.tensor_tensor(
                out=zidx[:], in0=zidx[:],
                in1=iyw[:].unsqueeze(2).to_broadcast([128, 8, 8, 8]),
                op=ALU.bitwise_or))
            P.drain("vector")
            for h in range(8):
                P.step("vector", "vv", lambda e, h=h: e.max(
                    out=f8w[:, h * 8:(h + 1) * 8],
                    in_=zidx[:, h, :, :].rearrange("p a b -> p (a b)")
                    .bitcast(F32)))
                P.step("vector", "vv", lambda e, h=h: e.max(
                    out=s8w[:, h, :],
                    in_=zwork[:, h, :, :].rearrange("p a b -> p (a b)")))
            P.wait("vector", "vv", v_msk)
            P.drain("vector")
            P.step("vector", "vv", lambda e, t=t: e.tensor_scalar(
                out=idx_all[:, t, :], in0=f8w[:].bitcast(U32),
                scalar1=C_MLO16, scalar2=None, op0=ALU.bitwise_and))
            P.step("vector", "vv", lambda e: e.tensor_tensor(
                out=e8w[:], in0=s8w[:],
                in1=s8w[:, :, 0:1].to_broadcast([128, 8, 8]), op=ALU.subtract))
            v_e = P.cnt["vv"]
            P.wait("scalar", "vv", v_e)
            v_x8 = P.step("scalar", "ss", lambda e: e.activation(
                out=e8w[:], in_=e8w[:], func=AF.Exp))
            P.wait("vector", "ss", v_x8)
            P.step("vector", "vv", lambda e: e.tensor_reduce(
                out=r8w[:], in_=e8w[:], axis=AX.X, op=ALU.add))
            P.drain("vector")
            P.step("vector", "vv", lambda e: e.reciprocal(r8w[:], r8w[:]))
            P.drain("vector")
            P.step("vector", "vv", lambda e, t=t: e.tensor_tensor(
                out=smax_all[:, t, :].rearrange("p (h j) -> p h j", h=8),
                in0=e8w[:], in1=r8w[:].unsqueeze(2).to_broadcast([128, 8, 8]),
                op=ALU.mult))
        v_topk = P.cnt["vv"]

        P.wait("gpsimd", "cw", conv_w[CONV - 1])
        P.wait("gpsimd", "vv", v_topk)
        up_mm = {}
        res_add = [0, 0]
        LAG = 2
        NB_TOT = NT * 64
        gelu_v = {}
        for i in range(NB_TOT + LAG):
            if i < NB_TOT:
                t, hj = i // 64, i % 64
                b = i % NBUF
                if i >= NBUF:
                    P.wait("gpsimd", "tt", up_mm[i - NBUF])
                v_g = P.step("gpsimd", "gd", lambda e, t=t, hj=hj, b=b:
                             e.indirect_dma_start(
                                 gbuf[:, b, :], None, fusedb,
                                 IndirectOffsetOnAxis(
                                     idx_all[:, t, hj:hj + 1].bitcast(I32), 0)), 16)
                P.wait("vector", "gd", v_g)
                v_d = P.step("vector", "vv", lambda e, t=t, hj=hj, b=b:
                             e.scalar_tensor_tensor(
                                 out=sq_scr[:, 0:1024], in0=gbuf[:, b, 0:1024],
                                 scalar=1.0, in1=xn_tok[:, t, :],
                                 op0=ALU.mult, op1=ALU.mult,
                                 accum_out=hdot[:, hj:hj + 1]))
                P.wait("scalar", "vv", v_d)
                gelu_v[i] = P.step("scalar", "ss", lambda e, hj=hj: e.activation(
                    out=gelu_c[:, hj:hj + 1], in_=hdot[:, hj:hj + 1],
                    func=AF.Gelu_apprx_tanh))
            j = i - LAG
            if j >= 0:
                t2, hj2 = j // 64, j % 64
                b2 = j % NBUF
                u0, u1 = (t2 % 2) * 2, (t2 % 2) * 2 + 1
                P.wait("vector", "ss", gelu_v[j])
                v_sc = P.step("vector", "vv", lambda e, t2=t2, hj2=hj2, b2=b2:
                              e.tensor_scalar(
                                  out=gbuf[:, b2, 1024:2048],
                                  in0=gbuf[:, b2, 1024:2048],
                                  scalar1=gelu_c[:, hj2:hj2 + 1],
                                  scalar2=smax_all[:, t2, hj2:hj2 + 1],
                                  op0=ALU.mult, op1=ALU.mult))
                P.wait("tensor", "vv", v_sc)
                if hj2 == 0:
                    P.wait("tensor", "vv", res_add[t2 % 2])
                    P.wait("tensor", "ld", v_id)
                P.step("tensor", "tt", lambda e, b2=b2, hj2=hj2, u0=u0: e.matmul(
                    psum[u0][:], sb_id[:], gbuf[:, b2, 1024:1536],
                    start=(hj2 == 0), stop=(hj2 == 63)))
                up_mm[j] = P.step("tensor", "tt", lambda e, b2=b2, hj2=hj2, u1=u1:
                                  e.matmul(psum[u1][:], sb_id[:],
                                           gbuf[:, b2, 1536:2048],
                                           start=(hj2 == 0), stop=(hj2 == 63)))
                if hj2 == 63:
                    P.wait("vector", "tt", up_mm[j])
                    P.step("vector", "vv", lambda e, t2=t2, u0=u0:
                           e.scalar_tensor_tensor(
                        out=sb_x[:, t2, 0:512], in0=psum[u0][:], scalar=0.5,
                        in1=sb_x[:, t2, 0:512], op0=ALU.mult, op1=ALU.add))
                    res_add[t2 % 2] = P.step("vector", "vv",
                                             lambda e, t2=t2, u1=u1:
                                             e.scalar_tensor_tensor(
                        out=sb_x[:, t2, 512:1024], in0=psum[u1][:], scalar=0.5,
                        in1=sb_x[:, t2, 512:1024], op0=ALU.mult, op1=ALU.add))
        return P.cnt["vv"], P.cnt["tt"]

    def emit_rope_group(t, grp, pb, dst, v_qm, swb, last_rope):
        for mi in range(4):
            c = grp * 4 + mi
            P.wait("vector", "tt", v_qm)
            if last_rope["sw"]:
                P.wait("vector", "tt", last_rope["sw"])
            v_ev = P.step("vector", "vv", lambda e, mi=mi, pb=pb, c=c:
                          e.tensor_copy(out=qraw[:, c % 2, :],
                                        in_=psum[pb][:, mi * 128:(mi + 1) * 128]))
            P.wait("tensor", "vv", max(v_ev, last_rope["tt2"]))
            P.wait("tensor", "ld", v_swap)
            v_sw = P.step("tensor", "tt", lambda e, swb=swb, c=c: e.matmul(
                psum[swb][:, 0:128], sb_swap[:], qraw[:, c % 2, :],
                start=True, stop=True))
            last_rope["sw"] = v_sw
            P.wait("vector", "tt", v_sw)
            P.wait("vector", "ld", state.get("v_rope_ld", 0))
            P.drain("vector")
            P.step("vector", "vv", lambda e, t=t, swb=swb: e.tensor_tensor(
                out=rope_tmp[:], in0=psum[swb][:, 0:128],
                in1=stab[:, t * 128:(t + 1) * 128], op=ALU.mult))
            P.step("vector", "vv", lambda e, t=t, c=c, dst=dst: e.tensor_tensor(
                out=dst[:, c, t * 128:(t + 1) * 128], in0=qraw[:, c % 2, :],
                in1=ctab[:, t * 128:(t + 1) * 128], op=ALU.mult))
            P.drain("vector")
            last_rope["tt2"] = P.step("vector", "vv",
                                      lambda e, t=t, c=c, dst=dst: e.tensor_tensor(
                out=dst[:, c, t * 128:(t + 1) * 128],
                in0=dst[:, c, t * 128:(t + 1) * 128],
                in1=rope_tmp[:], op=ALU.add))
        return last_rope["tt2"]

    def emit_attention(x_ready_vv, att_stop=None, att_tiles=NT):
        v_lnst = emit_ln_stats(x_ready_vv)
        P.wait("tensor", "gd", state["v_attw1"])
        lat_ev = [0, 0]
        fm_ev = [0, 0]
        rq = {"sw": 0, "tt2": 0}
        for t in range(att_tiles):
            v_a = emit_ln_apply(t, ln2t[:], v_lnst,
                                extra_vv=max(fm_ev[0], fm_ev[1]), strong=True)
            P.wait("tensor", "aa", v_a)
            v_fm = emit_transpose_tile(ln2t[:], lnfm_dst(t), 4 + (t % 2),
                                       waits_vv=(fm_ev[t % 2],))
            fm_ev[t % 2] = v_fm
            lb = t % 2
            P.wait("tensor", "vv", max(v_fm, lat_ev[lb]))
            for ml in range(4):
                for k in range(8):
                    P.step("tensor", "tt", lambda e, t=t, ml=ml, k=k, lb=lb:
                           e.matmul(psum[lb][:, ml * 128:(ml + 1) * 128],
                                    wdn_b[:, k, ml * 128:(ml + 1) * 128],
                                    lnfm_col(t, k),
                                    start=(k == 0), stop=(k == 7)))
            v_lm = P.cnt["tt"]
            P.wait("vector", "tt", v_lm)
            lat_ev[lb] = P.step("vector", "vv", lambda e, t=t, lb=lb:
                                e.tensor_copy(
                out=lat_fm[:, :, t * 128:(t + 1) * 128],
                in_=psum[lb][:].rearrange("p (c q) -> p c q", c=4)))
            qb = 2 + (t % 2)
            for grp in range(2):
                P.wait("tensor", "vv", max(v_fm, rq["tt2"]))
                for m in range(grp * 4, grp * 4 + 4):
                    for k in range(8):
                        P.step("tensor", "tt",
                               lambda e, t=t, m=m, k=k, qb=qb: e.matmul(
                            psum[qb][:, (m % 4) * 128:(m % 4 + 1) * 128],
                            wq_asb[:, k, m * 128:(m + 1) * 128],
                            lnfm_col(t, k), start=(k == 0), stop=(k == 7)))
                emit_rope_group(t, grp, qb, q_rot, P.cnt["tt"], 6 + (t % 2), rq)
        v_loop1 = P.cnt["vv"]
        if att_stop == 1:
            return v_loop1, P.cnt["tt"]

        P.wait("tensor", "gd", state["v_attw2"])
        rk = {"sw": 0, "tt2": 0}
        for t in range(NT):
            kb = t % 2
            for grp in range(2):
                P.wait("tensor", "vv",
                       max(v_loop1 if t == 0 else 0, lat_ev[t % 2], rk["tt2"]))
                for m in range(grp * 4, grp * 4 + 4):
                    for k4 in range(4):
                        P.step("tensor", "tt",
                               lambda e, t=t, m=m, k4=k4, kb=kb: e.matmul(
                            psum[kb][:, (m % 4) * 128:(m % 4 + 1) * 128],
                            wku_b[:, k4, m * 128:(m + 1) * 128],
                            lat_fm[:, k4, t * 128:(t + 1) * 128],
                            start=(k4 == 0), stop=(k4 == 3)))
                emit_rope_group(t, grp, kb, k_rot, P.cnt["tt"], 2 + (t % 2), rk)
        v_loop2 = P.cnt["vv"]
        P.wait("gpsimd", "vv", v_loop2)
        state["v_attw3"] = P.step("gpsimd", "gd", lambda e: e.dma_start(
            out=wvu_b[:], in_=wvu_d.rearrange("(k p) m -> p k m", p=128)), 16)
        if att_stop == 2:
            return v_loop2, P.cnt["tt"]

        P.wait("vector", "vv", v_loop2)
        v_ones = P.step("vector", "vv", lambda e: e.memset(
            v65[:].rearrange("p a (g u) -> p a g u", g=16)[:, :, :, 64:65], 1.0))
        P.wait("tensor", "gd", state["v_attw3"])
        vv_ev = [v_ones, v_ones]
        for t in range(NT):
            vb = t % 2
            for nb in range(2):
                P.wait("tensor", "vv", max(v_ones, vv_ev[vb]))
                for k4 in range(4):
                    P.step("tensor", "tt", lambda e, t=t, nb=nb, k4=k4, vb=vb:
                           e.matmul(psum[vb][:],
                                    lat_fm[:, k4, t * 128:(t + 1) * 128],
                                    wvu_b[:, k4, nb * 512:(nb + 1) * 512],
                                    start=(k4 == 0), stop=(k4 == 3)))
                v_vm = P.cnt["tt"]
                P.wait("vector", "tt", v_vm)
                vv_ev[vb] = P.step("vector", "vv", lambda e, t=t, nb=nb, vb=vb:
                                   e.tensor_copy(
                    out=v65[:, t, :].rearrange("p (g u) -> p g u", g=16)
                        [:, nb * 8:(nb + 1) * 8, 0:64],
                    in_=psum[vb][:].rearrange("p (g u) -> p g u", g=8)))
        v_loop3 = P.cnt["vv"]
        P.wait("gpsimd", "vv", v_loop3)
        state["v_wo"] = P.step("gpsimd", "gd", lambda e: e.dma_start(
            out=wo_b[:], in_=wo_d.rearrange("(k p) m -> p k m", p=128)), 16)
        if att_stop == 3:
            return v_loop3, P.cnt["tt"]

        P.wait("tensor", "gd", state["v_wo"])
        s_bank_exp = [0, 0]
        pt_bank_ev = [0, 0]
        pv_done = [0, 0]
        pt_cons = 0
        p_cons = 0
        oadd = [0, 0]
        for qt in range(NT):
            kr = (qt // 2 + 1) * 256
            nk128 = kr // 128
            for h2 in range(HA):
                h = h2
                hp = (h % 2) * 64
                ch = h // 2
                pvb = 6 + (h % 2)
                csizes = []
                off = 0
                while off < kr:
                    w = min(512, kr - off)
                    csizes.append((off, w))
                    off += w
                exp_v = []
                for ci, (off, w) in enumerate(csizes):
                    sbk = ci % 2
                    P.wait("tensor", "vv", v_loop3)
                    P.wait("tensor", "ss", s_bank_exp[sbk])
                    P.step("tensor", "tt", lambda e, qt=qt, hp=hp, ch=ch,
                           off=off, w=w, sbk=sbk: e.matmul(
                        psum[sbk][:, 0:w],
                        q_rot[hp:hp + 64, ch, qt * 128:(qt + 1) * 128],
                        k_rot[hp:hp + 64, ch, off:off + w],
                        start=True, stop=True))
                    v_smm = P.cnt["tt"]
                    P.wait("scalar", "tt", max(v_smm, p_cons))
                    s_bank_exp[sbk] = P.step("scalar", "ss",
                                             lambda e, off=off, w=w, sbk=sbk:
                                             e.activation(
                        out=p_sb[:, off:off + w], in_=psum[sbk][:, 0:w],
                        func=AF.Exp, bias=C_M32, scale=0.125))
                    exp_v.append(s_bank_exp[sbk])
                for g in range(0, nk128, 8):
                    gi = (g // 8) % 2
                    gbk = 4 + gi
                    pt8 = psum[gbk][:].bitcast(BF16).rearrange(
                        "p (c q) -> p c q", c=8)
                    n_in_g = min(8, nk128 - g)
                    P.wait("tensor", "vv", pt_bank_ev[gi])
                    P.wait("tensor", "ld", v_id)
                    for c2 in range(n_in_g):
                        ck = g + c2
                        P.wait("tensor", "ss", exp_v[(ck * 128) // 512])
                        P.step("tensor", "tt", lambda e, ck=ck, c2=c2, pt8=pt8:
                               e.transpose(pt8[:, c2, :],
                                           p_sb[:, ck * 128:(ck + 1) * 128],
                                           sb_id[:]))
                    v_tr = P.cnt["tt"]
                    p_cons = v_tr
                    P.wait("vector", "tt", max(v_tr, pt_cons))
                    pt_bank_ev[gi] = P.step(
                        "vector", "vv",
                        lambda e, g=g, n_in_g=n_in_g, pt8=pt8: e.tensor_copy(
                            out=pt_sb[:, g * 128:(g + n_in_g) * 128],
                            in_=pt8.rearrange("p c q -> p (c q)")
                            [:, 0:n_in_g * 128]))
                v_ptall = P.cnt["vv"]
                P.wait("tensor", "vv", max(v_ptall, pv_done[h % 2], v_ones))
                for ck in range(nk128):
                    P.step("tensor", "tt", lambda e, qt=qt, h=h, ck=ck, pvb=pvb:
                           e.matmul(psum[pvb][:, 0:65],
                                    pt_sb[:, ck * 128:(ck + 1) * 128],
                                    v65[:, ck, h * 65:(h + 1) * 65],
                                    start=(ck == 0), stop=(ck == nk128 - 1)))
                v_pv = P.cnt["tt"]
                pt_cons = v_pv
                P.wait("vector", "tt", v_pv)
                P.step("vector", "vv", lambda e, pvb=pvb: e.reciprocal(
                    dwork[:, 0:1], psum[pvb][:, 64:65]))
                P.drain("vector")
                pv_done[h % 2] = P.step("vector", "vv", lambda e, h=h, pvb=pvb:
                                        e.tensor_scalar(
                    out=atok[:, h * 64:(h + 1) * 64], in0=psum[pvb][:, 0:64],
                    scalar1=dwork[:, 0:1], scalar2=None, op0=ALU.mult))
            v_atok = P.cnt["vv"]
            v_afm = emit_transpose_tile(
                atok[:], afm[:], 4 + (qt % 2),
                waits_vv=(v_atok, pt_bank_ev[qt % 2]))
            pt_bank_ev[qt % 2] = v_afm
            for nb2 in range(2):
                P.wait("tensor", "vv", max(v_afm, oadd[qt % 2]))
                for f in range(8):
                    P.step("tensor", "tt", lambda e, nb2=nb2, f=f: e.matmul(
                        psum[2 + nb2][:], afm[:, f, :],
                        wo_b[:, f, nb2 * 512:(nb2 + 1) * 512],
                        start=(f == 0), stop=(f == 7)))
            v_om = P.cnt["tt"]
            P.wait("vector", "tt", v_om)
            P.step("vector", "vv", lambda e, qt=qt: e.scalar_tensor_tensor(
                out=sb_x[:, qt, 0:512], in0=psum[2][:], scalar=1.0,
                in1=sb_x[:, qt, 0:512], op0=ALU.mult, op1=ALU.add))
            oadd[qt % 2] = P.step("vector", "vv", lambda e, qt=qt:
                                  e.scalar_tensor_tensor(
                out=sb_x[:, qt, 512:1024], in0=psum[3][:], scalar=1.0,
                in1=sb_x[:, qt, 512:1024], op0=ALU.mult, op1=ALU.add))
        return P.cnt["vv"], P.cnt["tt"]

    state["v_wqf"] = P.step("gpsimd", "gd", lambda e: e.dma_start(
        out=wq_fsb[:], in_=wqf_d.rearrange("(k p) m -> p k m", p=128)), 16)
    state["v_keysT"] = P.step("gpsimd", "gd", lambda e: e.dma_start(
        out=keysT_b[:], in_=keysT_d.rearrange("c p k -> p c k")), 16)
    cc_pre = [0, 0]
    if not skip_ffn1:
        exp_ccw = P.cnt["sd2"] + 16
        P.wait("gpsimd", "sd2", exp_ccw)
        cc_pre[0] = P.step("gpsimd", "cc", lambda e: e.collective_compute(
            "AllReduce", ALU.add, replica_groups=[list(range(8))],
            ins=[ccin], outs=[ccout]))
        cc_pre[1] = exp_ccw
    emit_conv()
    if skip_ffn1:
        v_ffn1_vv, v_ffn1_tt = 1, 0
    else:
        v_ffn1_vv, v_ffn1_tt = emit_ffn(0, 1)

    if stage >= 2:
        P.wait("gpsimd", "vv", v_ffn1_vv)
        P.step("gpsimd", "gd", lambda e: e.dma_start(
            out=wq_asb[:], in_=wq_a.rearrange("(k p) m -> p k m", p=128)), 16)
        state["v_attw1"] = P.step("gpsimd", "gd", lambda e: e.dma_start(
            out=wdn_b[:], in_=wdn_d.rearrange("(k p) m -> p k m", p=128)), 16)
        state["v_attw2"] = P.step("gpsimd", "gd", lambda e: e.dma_start(
            out=wku_b[:], in_=wku_d.rearrange("(k p) m -> p k m", p=128)), 16)
        P.wait("sync", "vv", v_ffn1_vv)
        P.step("sync", "ld", lambda e: e.dma_start(out=ctab[:], in_=ctab_d), 16)
        state["v_rope_ld"] = P.step("sync", "ld", lambda e: e.dma_start(
            out=stab[:], in_=stab_d), 16)
        v_att_vv, v_att_tt = emit_attention(v_ffn1_vv, att_stop, att_tiles)
    else:
        v_att_vv, v_att_tt = v_ffn1_vv, v_ffn1_tt

    if stage >= 3:
        P.wait("gpsimd", "vv", v_att_vv)
        P.wait("gpsimd", "tt", v_att_tt)
        state["v_wqf"] = P.step("gpsimd", "gd", lambda e: e.dma_start(
            out=wq_fsb[:], in_=wqf_d.rearrange("(k p) m -> p k m", p=128)), 16)
        state["v_keysT"] = P.step("gpsimd", "gd", lambda e: e.dma_start(
            out=keysT_b[:], in_=keysT_d.rearrange("c p k -> p c k")), 16)
        v_f2_vv, _ = emit_ffn(1, v_att_vv)
    else:
        v_f2_vv = v_att_vv

    P.wait("scalar", "vv", v_f2_vv)
    v_st = P.step("scalar", "sd2", lambda e: e.dma_start(
        out=y_out.rearrange("(a p) d -> p a d", p=128), in_=sb_x[:]), 16)
    if dbg:
        dbg_specs = [
            ("d_xntok", xn_tok[:], [128, NT, 1024], BF16),
            ("d_c8", c8_all[:], [128, NT, 16, 8], F32),
            ("d_idx", idx_all[:], [128, NT, 64], U32),
            ("d_smax", smax_all[:], [128, NT, 64], F32),
            ("d_bnsc", BN_SC[:], [128, 16], F32),
            ("d_bnbi", BN_BI[:], [128, 16], F32),
            ("d_hdot", hdot[:], [128, 64], F32),
            ("d_xnfm", xn_fm[:], [128, 8, 2048], BF16),
            ("d_gbuf", gbuf[:], [128, NBUF, 2048], BF16),
            ("d_lnsc", LN_SC[:], [128, 16], F32),
            ("d_lnb2", LN_B2[:], [128, 16], F32),
        ]
        if stage >= 2:
            dbg_specs += [
                ("d_qrot", q_rot[:], [128, 8, 2048], BF16),
                ("d_krot", k_rot[:], [128, 8, 2048], BF16),
                ("d_v65", v65[:], [128, NT, 1040], BF16),
                ("d_atok", atok[:], [128, 1024], BF16),
                ("d_ptsb", pt_sb[:], [128, 2048], BF16),
                ("d_b1", sb_b1[:], [128, 16640], BF16),
                ("d_b0p", sb_b0[:, 0:2048], [128, 2048], BF16),
                ("d_aux0", lat_fm[:], [128, 4, 2048], BF16),
            ]
        for nm, ap, shp, dt2 in dbg_specs:
            od = dtn(nm, shp, dt2, kind="ExternalOutput").ap()
            v_st = P.step("scalar", "sd2",
                          lambda e, od=od, ap=ap: e.dma_start(out=od, in_=ap), 16)
    P.wait("scalar", "sd2", v_st)

    with nc.Block() as block:
        @block.sync
        def _(eng):
            for fn in P.fns["sync"]:
                fn(eng)

        @block.scalar
        def _(eng):
            for fn in P.fns["scalar"]:
                fn(eng)

        @block.vector
        def _(eng):
            for fn in P.fns["vector"]:
                fn(eng)

        @block.tensor
        def _(eng):
            for fn in P.fns["tensor"]:
                fn(eng)

        @block.gpsimd
        def _(eng):
            for fn in P.fns["gpsimd"]:
                fn(eng)

    stack.close()
    return nc


def make_consts():
    id128 = np.eye(128, dtype=ml_dtypes.bfloat16)
    swap = np.zeros((128, 128), dtype=ml_dtypes.bfloat16)
    for i in range(0, 128, 2):
        swap[i, i + 1] = 1
        swap[i + 1, i] = 1
    iota = np.tile(np.arange(256, dtype=np.uint32), (128, 1))
    inv = 1.0 / 10000.0 ** (np.arange(0, DA, 2, dtype=np.float32) / DA)
    ang = np.outer(np.arange(T, dtype=np.float32), inv)
    cos = np.cos(ang).astype(np.float32)
    sin = np.sin(ang).astype(np.float32)
    ctab = np.zeros((128, T), np.float32)
    stab = np.zeros((128, T), np.float32)
    for p in range(128):
        r = (p % 64) // 2
        ctab[p] = cos[:, r]
        stab[p] = sin[:, r] * (1.0 if p % 2 == 1 else -1.0)
    return id128, swap, iota, ctab, stab


def make_inputs(L0_attn_x, L0_attn_wq, L0_attn_w_down, L0_attn_wk_up,
                L0_attn_wv_up, L0_attn_wo, L0_ffn_w_q, L0_ffn_bn_gamma,
                L0_ffn_bn_beta, L0_ffn_keys, L0_ffn_down_embed, L0_ffn_up_embed):
    id128, swap, iota, ctab, stab = make_consts()
    keysT = np.ascontiguousarray(
        np.transpose(np.asarray(L0_ffn_keys), (2, 0, 3, 1)).reshape(16, KD, NK))
    fused = np.concatenate(
        [np.asarray(L0_ffn_down_embed), np.asarray(L0_ffn_up_embed)], axis=1)
    common = dict(
        wq_a=np.asarray(L0_attn_wq), wdn=np.asarray(L0_attn_w_down),
        wku=np.asarray(L0_attn_wk_up), wvu=np.asarray(L0_attn_wv_up),
        wo_a=np.asarray(L0_attn_wo), wqf=np.asarray(L0_ffn_w_q),
        bng=np.ascontiguousarray(np.asarray(L0_ffn_bn_gamma).reshape(16, 128).T),
        bnb=np.ascontiguousarray(np.asarray(L0_ffn_bn_beta).reshape(16, 128).T),
        keysT=keysT, fused=np.ascontiguousarray(fused),
        id128=id128, swap128=swap, iota256=iota, ctab=ctab, stab=stab,
    )
    x = np.asarray(L0_attn_x)
    return [dict(common, x=np.ascontiguousarray(x[c])) for c in range(8)]


def kernel(**inputs):
    nc = build(stage=3)
    in_maps = make_inputs(**inputs)
    res = run_bass_kernel_spmd(nc, in_maps, core_ids=list(range(8)))
    return np.stack([res.results[c]["y"] for c in range(8)], axis=0)
